# version 19
# baseline (speedup 1.0000x reference)
import math
import numpy as np
import ml_dtypes
import concourse.bass as bass
import concourse.mybir as mybir
from concourse.bass_utils import run_bass_kernel_spmd

F32 = mybir.dt.float32
BF16 = mybir.dt.bfloat16
AF = mybir.ActivationFunctionType
ALU = mybir.AluOpType
AX = mybir.AxisListType

B, S, D = 16, 2048, 1024
NCORES = 8
BPC = B // NCORES
NH, HD = 16, 64
NTC = 16
GAMMAS = (0.5, 1.0, 2.0)
CSC = float(np.float32(math.sqrt(2.0 / 256.0)))
LN16 = float(np.float32(math.log(16.0)))
MAGIC = 12582912.0
TWO_PI = float(np.float32(2 * math.pi))
SINSHRINK = float(np.float32(1.0 - 4e-6))
AWID = 52000
_DBG_NE = None
_DBG_NTT = None
_DBG_PRE = 2
_DBG_NP = 3
_DBG_NBLK = 4
_DBG_SERIAL = 0


class SemC:
    __slots__ = ("name", "h", "n", "per")

    def __init__(self, name):
        self.name, self.h, self.n = name, None, 0


class Em:
    STREAMS = ("pe", "act", "dve", "pool", "sp")

    def __init__(self, nc):
        self.nc = nc
        self.ops = {s: [] for s in self.STREAMS}
        self.sems = []
        self.dirty = {s: False for s in self.STREAMS}
        self.pending = {s: [] for s in self.STREAMS}

    def sem(self, name):
        s = SemC(name)
        self.sems.append(s)
        return s

    def op(self, stream, body, waits=(), incs=(), dma=False):
        w = self.pending[stream] + [x for x in waits if x is not None]
        self.pending[stream] = []
        for s, n in incs:
            s.n += n
        self.ops[stream].append([w, body, list(incs), dma])
        self.dirty[stream] = True

    def barrier(self, bar):
        if not hasattr(bar, "per"):
            bar.per = {s: self.sem(f"{bar.name}_{s}") for s in self.STREAMS}
        edges = []
        for s in self.STREAMS:
            if self.ops[s]:
                bs = bar.per[s]
                self.ops[s].append([[], lambda e: e.nop(), [(bs, 1)], False])
                bs.n += 1
            edges.append((bar.per[s], bar.per[s].n))
        for s in self.STREAMS:
            for e_ in edges:
                if e_[0] is not bar.per[s]:
                    self.pending[s].append(e_)

    def n_ops(self):
        return sum(len(v) for v in self.ops.values())

    def finalize(self, es):
        for s in self.sems:
            s.h = es.enter_context(self.nc.semaphore(s.name))
        with self.nc.Block() as blk:
            def runner(stream):
                def run(eng):
                    for waits, body, incs, _dma in self.ops[stream]:
                        for sc, v in waits:
                            if v > 0:
                                eng.wait_ge(sc.h, v)
                        inst = body(eng)
                        for sc, n in incs:
                            inst.then_inc(sc.h, n)
                return run
            blk.tensor(runner("pe"))
            blk.scalar(runner("act"))
            blk.vector(runner("dve"))
            blk.gpsimd(runner("pool"))
            blk.sync(runner("sp"))


class Guard:
    def __init__(self):
        self.m = {}

    def w(self, key):
        return self.m.get(key)

    def s(self, key, sem):
        self.m[key] = (sem, sem.n)


class Tile:
    __slots__ = ("h", "wid", "c0", "cols")

    def __init__(self, h, wid, c0, cols):
        self.h, self.wid, self.c0, self.cols = h, wid, c0, cols

    def ap(self, p0=0, np_=128, fdims=None, off=0):
        base = p0 * self.wid + self.c0 + off
        if fdims is None:
            fdims = [[1, self.cols]]
        return bass.AP(self.h, base, [[self.wid, np_]] + [list(d) for d in fdims])

    def __getitem__(self, idx):
        ps, cs = idx
        p0 = ps.start or 0
        p1 = ps.stop if ps.stop is not None else 128
        c0 = cs.start or 0
        c1 = cs.stop if cs.stop is not None else self.cols
        return self.ap(p0, p1 - p0, [[1, c1 - c0]], off=c0)


def build_program(mode="full"):
    nc = bass.Bass("TRN2", target_bir_lowering=False)

    X = nc.dram_tensor("x", [BPC, S, D], F32, kind="ExternalInput")
    CK = nc.dram_tensor("conv_k", [3, 3, 256, 1024], F32, kind="ExternalInput")
    CB = nc.dram_tensor("conv_b", [128, 24], F32, kind="ExternalInput")
    RW = nc.dram_tensor("rff_w", [3, 3, 1024, 256], F32, kind="ExternalInput")
    RB = nc.dram_tensor("rff_b", [128, 18], F32, kind="ExternalInput")
    PW = nc.dram_tensor("proj_w", [3, 768, 1024], F32, kind="ExternalInput")
    PB = nc.dram_tensor("proj_b", [1, 3072], F32, kind="ExternalInput")
    OM = nc.dram_tensor("omega", [64, 256], F32, kind="ExternalInput")
    LMT = nc.dram_tensor("lmT", [64, 64], F32, kind="ExternalInput")
    OW = nc.dram_tensor("out_w", [1024, 1024], F32, kind="ExternalInput")
    OB = nc.dram_tensor("out_b", [1, 1024], F32, kind="ExternalInput")
    RT = nc.dram_tensor("rwT", [128, 64], F32, kind="ExternalInput")
    EBT = nc.dram_tensor("expb", [8, 1], F32, kind="ExternalInput")
    W1 = nc.dram_tensor("w1", [8, 1024, 2048], F32, kind="ExternalInput")
    W3 = nc.dram_tensor("w3", [8, 1024, 2048], F32, kind="ExternalInput")
    W2 = nc.dram_tensor("w2", [8, 2048, 1024], F32, kind="ExternalInput")
    SW1 = nc.dram_tensor("sw1", [1, 1024, 2048], F32, kind="ExternalInput")
    SW3 = nc.dram_tensor("sw3", [1, 1024, 2048], F32, kind="ExternalInput")
    SW2 = nc.dram_tensor("sw2", [1, 2048, 1024], F32, kind="ExternalInput")
    SST = nc.dram_tensor("sstab", [128, 1024], BF16, kind="ExternalInput")
    CCT = nc.dram_tensor("cctab", [128, 1024], BF16, kind="ExternalInput")
    OUT = nc.dram_tensor("out", [BPC, S, D], F32, kind="ExternalOutput")
    dbg = {}
    if mode == "attn1":
        dbg["x2_out"] = nc.dram_tensor("x2_out", [128, 16384], F32, kind="ExternalOutput")
        dbg["arena_out"] = nc.dram_tensor("arena_out", [128, AWID], F32, kind="ExternalOutput")
    if mode == "ffn1":
        dbg["xn2_in"] = nc.dram_tensor("xn2_in", [128, 16384], BF16, kind="ExternalInput")
        dbg["cw_in"] = nc.dram_tensor("cw_in", [8, 2048], F32, kind="ExternalInput")
        dbg["x2_in"] = nc.dram_tensor("x2_in", [128, 16384], F32, kind="ExternalInput")

    from contextlib import ExitStack
    es = ExitStack()
    a32h = es.enter_context(nc.sbuf_tensor("arena", [128, AWID], F32))
    abfh = a32h.bitcast(BF16)
    PS = [es.enter_context(nc.psum_tensor(f"ps{i}", [128, 512], F32)) for i in range(8)]
    PSB = [p.bitcast(BF16) for p in PS]

    def T32(c0, cols):
        return Tile(a32h, AWID, c0, cols)

    def TBF(c0, cols):
        return Tile(abfh, 2 * AWID, 2 * c0, cols)

    off_ = [0]
    def alloc(n):
        c = off_[0]
        off_[0] += n
        assert off_[0] <= AWID, off_[0]
        return c
    sst = TBF(alloc(512), 1024)
    cct = TBF(alloc(512), 1024)
    ident = TBF(alloc(64), 128)
    aug_om = TBF(alloc(128), 256)
    om_dup = TBF(alloc(128), 256)
    mhalf = TBF(alloc(64), 128)
    nylhs = TBF(alloc(32), 64)
    biasny = T32(alloc(1), 1)
    bias1 = T32(alloc(18), 18)
    bias2 = T32(alloc(18), 18)
    convb = T32(alloc(24), 24)
    rwt_b = TBF(alloc(32), 64)
    expb = T32(alloc(1), 1)
    pbrow = TBF(alloc(1536), 3072)
    outbr = TBF(alloc(512), 1024)
    ones1 = TBF(alloc(64), 128)
    onecol = TBF(alloc(1), 1)
    bln16 = T32(alloc(1), 1)
    epsln = T32(alloc(1), 1)
    eps6 = T32(alloc(1), 1)
    m64t = T32(alloc(1), 1)
    small = T32(alloc(16), 16)
    rsc = T32(alloc(64), 64)
    eps65 = Tile(rsc.h, rsc.wid, rsc.c0 + 48, 1)
    c_xn1 = alloc(8200)
    c_ck = alloc(3072)
    c_rw = alloc(3072)
    c_pw = alloc(3072)
    c_q = alloc(8192)
    c_v = alloc(8192)
    c_kv = alloc(2080)
    c_ow = alloc(4096)
    c_tz = alloc(6656)
    c_fz = alloc(1536)
    assert off_[0] <= AWID, off_[0]

    xn1 = TBF(c_xn1, 16400)
    ckt = TBF(c_ck, 6144)
    rwt = TBF(c_rw, 6144)
    pwt = TBF(c_pw, 6144)
    q_fm = TBF(c_q, 16384)
    v_sb = TBF(c_v, 16384)
    attnpre = TBF(c_v, 16384)
    kvacc = T32(c_kv, 2048)
    ksum = T32(c_kv + 2048, 32)
    outw_t = TBF(c_ow, 8192)
    kvks = TBF(c_tz + 5120, 2080)
    x_til = [T32(c_tz + 1024 * i, 1024) for i in range(2)]
    xsq_t = T32(c_tz + 2048, 1024)
    xn_tm = TBF(c_tz + 3072, 1024)
    h_fm = TBF(c_tz + 3584, 4096)
    qr_tm = TBF(c_tz, 2048)
    qr_tm2 = TBF(c_tz + 1024, 2048)
    kk_tm2 = TBF(c_ow, 2048)
    xnt_b = TBF(c_ow + 1024, 1024)
    pbst = T32(c_tz, 3072)
    lmt_f = T32(c_tz + 3072, 64)
    lmsq = T32(c_tz + 3136, 64)
    feats = TBF(c_fz, 3072)
    t1_t = [T32(c_tz + 5632 + 512 * i, 512) for i in range(2)]
    kk_tm = TBF(c_tz + 0, 2048)
    kkfm = TBF(c_tz + 1024, 2048)
    kny = TBF(c_tz + 2048, 2048)
    kp_all = TBF(c_tz + 3072, 4096)
    qp_t = [TBF(c_tz + 1024 * i, 2048) for i in range(2)]
    qsq_t = [TBF(c_tz + 2048 + 256 * i, 512) for i in range(2)]
    nd_t = [TBF(c_tz + 2560 + 256 * i, 512) for i in range(3)]
    xb2 = [T32(c_tz + 4096, 1024), T32(c_tz + 5632, 1024)]
    xsq2 = T32(c_fz, 1024)
    xn_tm2 = TBF(c_fz + 1024, 1024)
    x2lo = T32(c_xn1, 12288)
    x2hi = T32(c_tz, 4096)

    def x2ap(tc, a, b):
        if tc < 12:
            return x2lo[0:128, tc * 1024 + a: tc * 1024 + b]
        return x2hi[0:128, (tc - 12) * 1024 + a: (tc - 12) * 1024 + b]
    w1b = TBF(c_xn1, 16384)
    w3b = TBF(c_q, 16384)
    eo_all = TBF(c_xn1 + 8192, 8192)
    gs_t = [TBF(c_xn1 + 12288 + 256 * i, 512) for i in range(2)]
    us_t = [TBF(c_xn1 + 12800 + 256 * i, 512) for i in range(2)]
    px_t = [T32(c_xn1 + 13312 + 1024 * i, 1024) for i in range(2)]
    cwrow = TBF(c_xn1 + 15360, 4096)
    cwb = TBF(c_kv, 4096)
    w2b = TBF(c_ow, 16384)
    xn2 = TBF(c_v, 16384)
    logits = TBF(c_fz, 2048)
    rB1 = T32(c_fz + 1024, 128)
    rB2 = T32(c_fz + 1152, 128)
    rB3 = T32(c_fz + 1280, 128)
    cw_tm = TBF(c_tz + 4096, 256)
    cw_fm = TBF(c_tz + 5632, 2048)

    import kernel as _K
    _K.LAYOUT = dict(c_xn1=c_xn1, c_ck=c_ck, c_rw=c_rw, c_pw=c_pw, c_q=c_q, c_v=c_v,
                     c_kv=c_kv, c_ow=c_ow, c_tz=c_tz, c_fz=c_fz)
    em = Em(nc)
    G = Guard()

    def E(stream, body, waits=(), incs=(), dma=False):
        em.op(stream, body, waits, incs, dma)

    s_ld = em.sem("ld"); s_ldp = em.sem("ldp"); s_bar = em.sem("bar")
    s_setP = em.sem("setP"); s_setA = em.sem("setA"); s_setD = em.sem("setD")
    s_setPE = em.sem("setPE")
    s_x = [em.sem(f"x{i}") for i in range(2)]
    s_awc = em.sem("awc"); s_awr = em.sem("awr"); s_awp = em.sem("awp"); s_ow = em.sem("ow")
    s_lnA = em.sem("lnA"); s_lnD = em.sem("lnD")
    s_peC = em.sem("peC"); s_acC = em.sem("acC")
    s_peR = em.sem("peR"); s_acR = em.sem("acR"); s_dvR = em.sem("dvR"); s_acS = em.sem("acS")
    s_peP = em.sem("peP"); s_dvRo = em.sem("dvRo"); s_acV = em.sem("acV")
    s_peT = em.sem("peT"); s_dvT = em.sem("dvT"); s_acT = em.sem("acT")
    s_peN = em.sem("peN"); s_acN = em.sem("acN")
    s_peK = em.sem("peK"); s_acK = em.sem("acK")
    s_peV = em.sem("peV"); s_dvK = em.sem("dvK")
    s_peB = em.sem("peB"); s_acB = em.sem("acB"); s_dvB = em.sem("dvB"); s_poB = em.sem("poB")
    s_x2 = [em.sem(f"xb{i}") for i in range(2)]
    s_peO = em.sem("peO"); s_dvO = em.sem("dvO")
    s_ln2A = em.sem("ln2A")
    s_lg = em.sem("lg"); s_lga = em.sem("lga")
    s_cwD = em.sem("cwD"); s_cwA = em.sem("cwA"); s_cwr = em.sem("cwr")
    s_w1 = [em.sem(f"w1_{i}") for i in range(2)]
    s_w3 = [em.sem(f"w3_{i}") for i in range(2)]
    s_w2 = em.sem("w2s")
    s_peG = em.sem("peG"); s_peU = em.sem("peU")
    s_acF = em.sem("acF"); s_dvF = em.sem("dvF"); s_poF = em.sem("poF")
    s_peY = em.sem("peY"); s_acY = em.sem("acY")
    s_acc = [em.sem(f"acc{i}") for i in range(NTC)]
    s_fl = em.sem("fl")
    s_sv = em.sem("sv"); s_rl = em.sem("rl")

    def psw(b):
        return G.w(("ps", b))

    def pss(b, sem):
        G.s(("ps", b), sem)

    def emit_setup():
        for dst, src in [
            (convb[0:128, 0:24], CB.ap()),
            (bias1[0:128, 0:18], RB.ap()),
            (lmt_f[0:64, 0:64], LMT.ap()),
            (expb[0:8, 0:1], EBT.ap()),
            (pbst[0:1, 0:3072], PB.ap()),
            (sst[0:128, 0:1024], SST.ap()),
            (cct[0:128, 0:1024], CCT.ap()),
        ]:
            E("sp", lambda e, d=dst, s=src: e.dma_start(d, s), incs=[(s_ld, 16)], dma=True)
        nsp = s_ld.n
        for dst, src in [
            (aug_om.ap(0, 64, [[1, 256]]), OM.ap()),
            (om_dup.ap(0, 64, [[1, 256]]), OM.ap()),
            (om_dup.ap(64, 64, [[1, 256]]), OM.ap()),
            (rwt_b[0:128, 0:64], RT.ap()),
            (outbr[0:1, 0:1024], OB.ap()),
        ]:
            E("pool", lambda e, d=dst, s=src: e.dma_start(d, s), incs=[(s_ldp, 16)], dma=True)
        npl = s_ldp.n

        def pool_sets(e):
            e.memset(ones1[0:128, 0:128], 1.0)
            e.memset(onecol[0:128, 0:1], 1.0)
            e.memset(mhalf[0:128, 0:128], -0.5)
            e.memset(aug_om[64:128, 0:256], -0.5)
            e.memset(nylhs[64:128, 0:64], -1.0 / 64.0)
            e.memset(bln16[0:128, 0:1], -LN16)
            e.memset(epsln[0:128, 0:1], 1e-5)
            e.memset(eps6[0:128, 0:1], 1e-6)
            e.memset(eps65[0:64, 0:1], 0.0)
            e.memset(eps65[64:65, 0:1], 1e-6)
            e.memset(m64t[0:128, 0:1], -1.0 / 64.0)
            e.memset(ident[0:128, 0:128], 0.0)
            return e.memset(xn1[0:128, 0:16400], 0.0)
        E("pool", pool_sets, incs=[(s_setP, 1)])
        E("pool", lambda e: e.affine_select(out=ident[0:128, 0:128],
                                            in_=ident[0:128, 0:128],
                                            compare_op=ALU.not_equal, fill=1.0, base=0,
                                            pattern=[[-1, 128]], channel_multiplier=1),
          waits=[(s_setP, s_setP.n)], incs=[(s_setP, 1)])
        E("act", lambda e: e.activation(nylhs[0:64, 0:64], lmt_f[0:64, 0:64], AF.Identity,
                                        scale=2.0 / 64.0),
          waits=[(s_ld, nsp)], incs=[(s_setA, 1)])
        E("dve", lambda e: e.tensor_tensor(lmsq[0:64, 0:64], lmt_f[0:64, 0:64],
                                           lmt_f[0:64, 0:64], ALU.mult),
          waits=[(s_ld, nsp)], incs=[(s_setD, 1)])
        E("pe", lambda e: e.matmul(PS[7][0:64, 0:1], lmsq[0:64, 0:64], m64t[0:64, 0:1],
                                   start=True, stop=True),
          waits=[(s_setP, 1), (s_setD, 1), (s_ldp, npl)], incs=[(s_setPE, 1)])
        E("act", lambda e: e.activation(biasny[0:64, 0:1], PS[7][0:64, 0:1], AF.Identity),
          waits=[(s_setPE, 1)], incs=[(s_setA, 1)])
        pss(7, s_setA)
        E("dve", lambda e: e.tensor_single_scalar(bias2[0:128, 0:18], bias1[0:128, 0:18],
                                                  math.pi / 2, ALU.add),
          waits=[(s_ld, nsp)], incs=[(s_setD, 1)])
        E("dve", lambda e: e.tensor_single_scalar(bias2[0:128, 0:18], bias2[0:128, 0:18],
                                                  SINSHRINK, ALU.mult),
          waits=[(s_setD, s_setD.n)], incs=[(s_setD, 1)])
        E("dve", lambda e: e.tensor_scalar(bias1[0:128, 0:18], bias1[0:128, 0:18],
                                           1.0 / TWO_PI, 0.25, ALU.mult, ALU.add),
          incs=[(s_setD, 1)])
        E("dve", lambda e: e.tensor_single_scalar(pbrow[0:1, 0:3072], pbst[0:1, 0:3072],
                                                  1.0 / CSC, ALU.mult),
          waits=[(s_ldp, npl), (s_ld, nsp)], incs=[(s_setD, 1)])
        em.barrier(s_bar)
        setw = [(s_setP, s_setP.n), (s_setA, s_setA.n), (s_setD, s_setD.n),
                (s_setPE, s_setPE.n), (s_ld, s_ld.n), (s_ldp, s_ldp.n)]
        for st_ in ("pe", "act", "dve", "pool", "sp"):
            E(st_, lambda e: e.nop(), waits=setw)
        G.m[("xtil", 0)] = (s_setD, s_setD.n)
        G.m[("xtil", 1)] = (s_setD, s_setD.n)

    def emit_ln_tc(src_fn, dst_fm, dst_stride, dst_pad, tc, sem_done, xsq=None, xnt=None):
        xnt = xnt if xnt is not None else xn_tm
        dv1 = emit_ln_stats(src_fn, xsq, xnt)
        emit_ln_tp(dst_fm, dst_stride, dst_pad, tc, sem_done, xnt, dv1)

    def emit_ln_stats(src_fn, xsq=None, xnt=None):
        st = small
        xsq = xsq if xsq is not None else xsq_t
        xnt = xnt if xnt is not None else xn_tm
        E("dve", lambda e: e.reduce_sum(st[0:128, 0:1], src_fn(0, 1024), AX.X),
          incs=[(s_lnD, 1)])
        E("dve", lambda e: e.tensor_tensor(xsq[0:128, 0:1024], src_fn(0, 1024),
                                           src_fn(0, 1024), ALU.mult), incs=[(s_lnD, 1)])
        E("dve", lambda e: e.reduce_sum(st[0:128, 1:2], xsq[0:128, 0:1024], AX.X),
          waits=[(s_lnD, s_lnD.n)], incs=[(s_lnD, 1)])
        E("dve", lambda e: e.tensor_scalar(st[0:128, 2:3], st[0:128, 0:1], 1.0 / D, None,
                                           ALU.mult),
          waits=[(s_lnD, s_lnD.n)], incs=[(s_lnD, 1)])
        E("dve", lambda e: e.tensor_tensor(st[0:128, 3:4], st[0:128, 2:3], st[0:128, 2:3],
                                           ALU.mult),
          waits=[(s_lnD, s_lnD.n)], incs=[(s_lnD, 1)])
        E("dve", lambda e: e.scalar_tensor_tensor(st[0:128, 4:5], st[0:128, 1:2], 1.0 / D,
                                                  st[0:128, 3:4], ALU.mult, ALU.subtract),
          waits=[(s_lnD, s_lnD.n)], incs=[(s_lnD, 1)])
        dv0 = s_lnD.n
        E("act", lambda e: e.activation(st[0:128, 5:6], st[0:128, 4:5], AF.Ln,
                                        bias=epsln[0:128, 0:1]),
          waits=[(s_lnD, dv0)], incs=[(s_lnA, 1)])
        E("act", lambda e: e.activation(st[0:128, 6:7], st[0:128, 5:6], AF.Exp, scale=-0.5),
          waits=[(s_lnA, s_lnA.n)], incs=[(s_lnA, 1)])
        ac0 = s_lnA.n
        E("dve", lambda e: e.tensor_scalar(xnt[0:128, 0:1024], src_fn(0, 1024),
                                           st[0:128, 2:3], st[0:128, 6:7], ALU.subtract, ALU.mult),
          waits=[(s_lnA, ac0), G.w(("xntW", id(xnt)))], incs=[(s_lnD, 1)])
        return s_lnD.n

    def emit_ln_tp(dst_fm, dst_stride, dst_pad, tc, sem_done, xnt, dv1):
        for half in range(2):
            bk = 6 + half
            E("pe", lambda e, h=half, p=PSB[bk]: [
                e.transpose(p[0:128, 128 * i:128 * (i + 1)],
                            xnt[0:128, (h * 4 + i) * 128:(h * 4 + i + 1) * 128],
                            ident[0:128, 0:128]) for i in range(4)][-1],
              waits=[(s_lnD, dv1), psw(bk)], incs=[(s_peT, 1)])
            if half == 1:
                G.s(("xntW", id(xnt)), s_peT)
            pt = s_peT.n
            E("act", lambda e, h=half, p=PSB[bk]: e.activation(
                dst_fm.ap(0, 128, [[dst_stride, 4], [1, 128]],
                          off=(h * 4) * dst_stride + dst_pad + tc * 128),
                p[0:128, 0:512], AF.Identity),
              waits=[(s_peT, pt)], incs=[(sem_done, 1)])
            pss(bk, sem_done)

    ln_dv1 = {}

    def emit_ln1_stats(seq, tc):
        par = tc % 2
        E("sp", lambda e, t=tc, p=par: e.dma_start(
            x_til[p][0:128, 0:1024],
            bass.AP(X, (seq * S + t * 128) * D, [[D, 128], [1, D]])),
          waits=[G.w(("xtil", par))], incs=[(s_x[par], 16)], dma=True)
        nx = s_x[par].n
        em.pending["dve"].append((s_x[par], nx))
        ln_dv1[tc] = emit_ln_stats(lambda a, b, p=par: x_til[p][0:128, a:b],
                                   xnt=[xn_tm, xnt_b][par])
        G.s(("xtil", par), s_lnD)

    def emit_ln1_tp(tc):
        emit_ln_tp(xn1, 2050, 1, tc, s_lnA, [xn_tm, xnt_b][tc % 2], ln_dv1[tc])

    def emit_ln1_tc(seq, tc):
        emit_ln1_stats(seq, tc)
        emit_ln1_tp(tc)

    def emit_aw(p):
        E("pool", lambda e, pp=p: e.dma_start(
            ckt.ap(0, 128, [[2048, 3], [1024, 2], [1, 1024]]),
            bass.AP(CK, pp * 3 * 256 * 1024,
                    [[1024, 128], [256 * 1024, 3], [128 * 1024, 2], [1, 1024]])),
          waits=[G.w("ck"), G.w("awx")], incs=[(s_awc, 16)], dma=True)
        E("pool", lambda e, pp=p: e.dma_start(
            rwt.ap(0, 128, [[2048, 3], [256, 8], [1, 256]]),
            bass.AP(RW, pp * 3 * 1024 * 256,
                    [[256, 128], [1024 * 256, 3], [128 * 256, 8], [1, 256]])),
          waits=[G.w("rw"), G.w("awx")], incs=[(s_awr, 16)], dma=True)
        E("pool", lambda e, pp=p: e.dma_start(
            pwt.ap(0, 128, [[1024, 6], [1, 1024]]),
            bass.AP(PW, pp * 768 * 1024, [[1024, 128], [128 * 1024, 6], [1, 1024]])),
          waits=[G.w("pw"), G.w("awx")], incs=[(s_awp, 16)], dma=True)
        return s_awc.n, s_awr.n, s_awp.n

    def emit_conv_group(p, blk, oc, lnA_need, awc_n):
        bk = oc % 2
        gco = oc // 2
        E("pe", lambda e, oc=oc, g=gco, ps=PS[bk], b=blk: [
            e.matmul(ps[0:128, 0:512],
                     ckt.ap(0, 128, [[1, 128]], off=w * 2048 + cic * 1024 + oc * 128),
                     xn1.ap(0, 128, [[1, 512]], off=(g * 2 + cic) * 2050 + w + b * 512),
                     start=(w == 0 and cic == 0), stop=(w == 2 and cic == 1))
            for w in range(3) for cic in range(2)][-1],
          waits=[(s_awc, awc_n), (s_lnA, lnA_need), G.w("pads"), psw(bk)],
          incs=[(s_peC, 1)])
        pc = s_peC.n
        E("act", lambda e, oc=oc, ps=PS[bk], pp=p: e.activation(
            h_fm[0:128, oc * 512:(oc + 1) * 512], ps[0:128, 0:512], AF.Identity,
            bias=convb[0:128, pp * 8 + oc:pp * 8 + oc + 1]),
          waits=[(s_peC, pc), G.w("hW")], incs=[(s_acC, 1)])
        pss(bk, s_acC)

    def emit_convrff(p, blk, lnA_need, awc_n, awr_n):
        for oc in range(8):
            emit_conv_group(p, blk, oc, lnA_need, awc_n)
        acC_blk = s_acC.n
        emit_rff(p, acC_blk, awr_n)

    def emit_rff(p, acC_blk, awr_n):
        for gi in range(3):
            spc = float(np.float32(math.sqrt(2 * GAMMAS[gi]) / TWO_PI))
            for mc in range(2):
                idx = gi * 2 + mc
                bk = 2 + idx % 2
                t1 = t1_t[idx % 2]
                dd = t1
                E("pe", lambda e, gi=gi, mc=mc, ps=PS[bk]: [
                    e.matmul(ps[0:128, 0:512],
                             rwt.ap(0, 128, [[1, 128]], off=gi * 2048 + dc * 256 + mc * 128),
                             h_fm[0:128, dc * 512:(dc + 1) * 512],
                             start=(dc == 0), stop=(dc == 7)) for dc in range(8)][-1],
                  waits=[(s_awr, awr_n), (s_acC, acC_blk), psw(bk)], incs=[(s_peR, 1)])
                pr = s_peR.n
                bcol = p * 6 + idx
                E("act", lambda e, ps=PS[bk], t1=t1, sc=spc, bc=bcol: e.activation(
                    t1[0:128, 0:512], ps[0:128, 0:512], AF.Identity,
                    bias=bias1[0:128, bc:bc + 1], scale=sc),
                  waits=[(s_peR, pr), G.w(("t1", idx % 2))], incs=[(s_acR, 1)])
                ar = s_acR.n
                E("dve", lambda e, t1=t1: e.tensor_single_scalar(
                    t1[0:128, 0:512], t1[0:128, 0:512], MAGIC, ALU.add),
                  waits=[(s_acR, ar)], incs=[(s_dvR, 1)])
                E("dve", lambda e, t1=t1: e.tensor_single_scalar(
                    t1[0:128, 0:512], t1[0:128, 0:512], MAGIC, ALU.subtract),
                  waits=[(s_dvR, s_dvR.n)], incs=[(s_dvR, 1)])
                E("dve", lambda e, ps=PS[bk], t1=t1, dd=dd, sc=spc: e.scalar_tensor_tensor(
                    dd[0:128, 0:512], ps[0:128, 0:512], sc, t1[0:128, 0:512],
                    ALU.mult, ALU.subtract),
                  waits=[(s_dvR, s_dvR.n)], incs=[(s_dvR, 1)])
                pss(bk, s_dvR)
                G.s(("t1", idx % 2), s_dvR)
                dr = s_dvR.n
                E("act", lambda e, dd=dd, idx=idx, bc=bcol: e.activation(
                    feats[0:128, idx * 512:(idx + 1) * 512], dd[0:128, 0:512], AF.Sin,
                    bias=bias2[0:128, bc:bc + 1], scale=float(TWO_PI * SINSHRINK)),
                  waits=[(s_dvR, dr), G.w("featsW")], incs=[(s_acS, 1)])
                G.s(("t1", idx % 2), s_acS)
        G.s("hW", s_peR)

    def emit_projmm(p, i, awp_n, acs_n, bks=(4, 5)):
        E("pe", lambda e, i=i, p=p, bb=bks: [
            y for half in range(2) for y in (
                [e.matmul(PS[bb[half]][0:128, 0:512],
                          feats[0:128, fc * 512 + i * 128: fc * 512 + (i + 1) * 128],
                          pwt.ap(0, 128, [[1, 512]], off=fc * 1024 + half * 512),
                          start=(fc == 0), stop=False) for fc in range(6)] +
                [e.matmul(PS[bb[half]][0:128, 0:512], ones1[0:1, 0:128],
                          pbrow[0:1, p * 1024 + half * 512: p * 1024 + (half + 1) * 512],
                          start=False, stop=True)])][-1],
          waits=[(s_acS, acs_n), (s_awp, awp_n), psw(bks[0]), psw(bks[1])], incs=[(s_peP, 1)])
        return s_peP.n

    def emit_v_tc(tc, ppn):
        for half in range(2):
            w = [(s_peP, ppn)]
            if tc == 0 and half == 0:
                w.append(G.w("xn2sv"))
            E("act", lambda e, h=half, t=tc: e.activation(
                v_sb[0:128, t * 1024 + h * 512: t * 1024 + (h + 1) * 512],
                PS[4 + h][0:128, 0:512], AF.Identity, scale=CSC),
              waits=w, incs=[(s_acV, 1)])
            pss(4 + half, s_acV)

    def emit_rope(tc, dest, kind, ppn, destW, bks=(4, 5)):
        for half in range(2):
            ps = PS[bks[half]]
            tmp = t1_t[half]
            sstap = sst.ap(0, 128, [[0, 8], [1, 64]], off=tc * 64)
            cctap = cct.ap(0, 128, [[0, 8], [1, 64]], off=tc * 64)
            E("dve", lambda e, ps=ps, tmp=tmp, ap=sstap: e.tensor_tensor(
                tmp[0:128, 0:512], ps[0:128, 0:512], ap, ALU.mult),
              waits=[(s_peP, ppn), G.w(("t1", half))], incs=[(s_dvRo, 1)])
            if kind == "q":
                outap = dest.ap(0, 128, [[1, 512]], off=half * 512)
                outsw = dest.ap(0, 128, [[64, 8], [-32, 2], [1, 32]], off=half * 512 + 32)
            else:
                outap = dest.ap(0, 128, [[128, 8], [1, 64]], off=half * 8 * 128)
                outsw = dest.ap(0, 128, [[128, 8], [-32, 2], [1, 32]], off=half * 8 * 128 + 32)
            E("dve", lambda e, ps=ps, oa=outap, ap=cctap: e.tensor_tensor(
                oa, ps[0:128, 0:512], ap, ALU.mult),
              waits=[G.w(destW) if half == 0 else None], incs=[(s_dvRo, 1)])
            tmpsw = tmp.ap(0, 128, [[64, 8], [-32, 2], [1, 32]], off=32)
            E("dve", lambda e, osw=outsw, tsw=tmpsw: e.tensor_tensor(osw, osw, tsw, ALU.add),
              waits=[(s_dvRo, s_dvRo.n)], incs=[(s_dvRo, 1)])
            G.s(("t1", half), s_dvRo)
            pss(bks[half], s_dvRo)

    def emit_q_tc(tc, ppn):
        emit_rope(tc, qr_tm, "q", ppn, "qrW")
        dro = s_dvRo.n
        E("pe", lambda e: [
            e.transpose(PSB[6][0:128, dc * 128:(dc + 1) * 128],
                        qr_tm[0:128, dc * 128:(dc + 1) * 128], ident[0:128, 0:128])
            for dc in range(8)][-1],
          waits=[(s_dvRo, dro), psw(6)], incs=[(s_peT, 1)])
        G.s("qrW", s_peT)
        pt = s_peT.n
        E("dve", lambda e, t=tc: e.tensor_copy(
            q_fm.ap(0, 128, [[2048, 8], [1, 128]], off=t * 128), PSB[6][0:128, 0:1024]),
          waits=[(s_peT, pt)], incs=[(s_dvT, 1)])
        pss(6, s_dvT)

    rope_done = {}

    def emit_q_rope(tc, ppn):
        par = tc % 2
        emit_rope(tc, [qr_tm, qr_tm2][par], "q", ppn, ("qrW", par))
        rope_done[tc] = s_dvRo.n

    def emit_q_tp(tc):
        par = tc % 2
        bk = 6 + par
        qrb = [qr_tm, qr_tm2][par]
        E("pe", lambda e, b=bk, q=qrb: [
            e.transpose(PSB[b][0:128, dc * 128:(dc + 1) * 128],
                        q[0:128, dc * 128:(dc + 1) * 128], ident[0:128, 0:128])
            for dc in range(8)][-1],
          waits=[(s_dvRo, rope_done[tc]), psw(bk)], incs=[(s_peT, 1)])
        G.s(("qrW", par), s_peT)
        pt = s_peT.n
        E("act", lambda e, t=tc, b=bk: e.activation(
            q_fm.ap(0, 128, [[2048, 8], [1, 128]], off=t * 128),
            PSB[b][0:128, 0:1024], AF.Identity),
          waits=[(s_peT, pt)], incs=[(s_acT, 1)])
        pss(bk, s_acT)

    def emit_k_rope(tc, ppn, bks=(4, 5)):
        par = tc % 2
        kkb = [kk_tm, kk_tm2][par]
        emit_rope(tc, kkb, "k", ppn, ("kkW", par), bks)
        kin = kkb.ap(0, 128, [[128, 16], [1, 64]], off=0)
        kout = kkb.ap(0, 128, [[128, 16], [1, 64]], off=64)
        E("dve", lambda e, ki=kin, ko=kout: e.tensor_tensor(ko, ki, ki, ALU.mult),
          waits=[(s_dvRo, s_dvRo.n)], incs=[(s_dvRo, 1)])
        rope_done[tc] = s_dvRo.n

    def emit_k_rest(tc):
        par = tc % 2
        kkb = [kk_tm, kk_tm2][par]
        tbks = (4, 5) if par == 0 else (2, 3)
        dro = rope_done[tc]
        dvt_h = []
        for half in range(2):
            bk = tbks[half]
            E("pe", lambda e, h=half, p=PSB[bk], kb=kkb: [
                e.transpose(p[0:128, j * 128:(j + 1) * 128],
                            kb[0:128, (h * 8 + j) * 128:(h * 8 + j + 1) * 128],
                            ident[0:128, 0:128]) for j in range(8)][-1],
              waits=[(s_dvRo, dro), psw(bk)], incs=[(s_peT, 1)])
            pt = s_peT.n
            E("dve", lambda e, h=half, p=PSB[bk]: e.tensor_copy(
                kkfm[0:128, h * 1024:(h + 1) * 1024], p[0:128, 0:1024]),
              waits=[(s_peT, pt), G.w("kkfmW") if half == 0 else None],
              incs=[(s_dvT, 1)])
            dvt_h.append(s_dvT.n)
            pss(bk, s_dvT)
        G.s(("kkW", par), s_peT)
        dvt1 = s_dvT.n
        for grp in range(4):
            bk = 6 + grp % 2
            E("pe", lambda e, g=grp, ps=PS[bk]: [
                e.matmul(ps[0:64, j * 128:(j + 1) * 128], nylhs[0:128, 0:64],
                         kkfm[0:128, (g * 4 + j) * 128:(g * 4 + j + 1) * 128],
                         start=True, stop=True) for j in range(4)][-1],
              waits=[(s_dvT, dvt_h[0] if grp < 2 else dvt1), psw(bk)], incs=[(s_peN, 1)])
            pn = s_peN.n
            E("act", lambda e, g=grp, ps=PS[bk]: e.activation(
                kny.ap(0, 64, [[1, 512]], off=g * 512), ps[0:64, 0:512], AF.Exp,
                bias=biasny[0:64, 0:1]),
              waits=[(s_peN, pn), G.w("knyW")], incs=[(s_acN, 1)])
            pss(bk, s_acN)
        G.s("kkfmW", s_peN)
        acn1 = s_acN.n
        E("dve", lambda e: e.tensor_tensor(kny.ap(64, 64, [[1, 2048]]),
                                           kny.ap(0, 64, [[1, 2048]]),
                                           kny.ap(0, 64, [[1, 2048]]), ALU.mult),
          waits=[(s_acN, acn1)], incs=[(s_dvT, 1)])
        dvt2 = s_dvT.n
        for pr in range(8):
            bk = tbks[pr % 2]
            E("pe", lambda e, pr=pr, ps=PS[bk]: [
                e.matmul(ps[0:128, j * 256:(j + 1) * 256],
                         kny[0:128, (pr * 2 + j) * 128:(pr * 2 + j + 1) * 128],
                         aug_om[0:128, 0:256], start=True, stop=True) for j in range(2)][-1],
              waits=[(s_dvT, dvt2), psw(bk)], incs=[(s_peK, 1)])
            pk = s_peK.n
            E("act", lambda e, pr=pr, ps=PS[bk]: e.activation(
                kp_all[0:128, pr * 512:(pr + 1) * 512], ps[0:128, 0:512], AF.Exp,
                bias=bln16[0:128, 0:1]),
              waits=[(s_peK, pk), G.w("kpW")], incs=[(s_acK, 1)])
            pss(bk, s_acK)
        G.s("knyW", s_peK)
        ack = s_acK.n
        acv = s_acV.n
        E("pe", lambda e, t=tc: [
            e.matmul(PS[(h * 2 + ec) // 8][0:128,
                     ((h * 2 + ec) % 8) * 64:((h * 2 + ec) % 8) * 64 + 64],
                     kp_all[0:128, h * 256 + ec * 128: h * 256 + (ec + 1) * 128],
                     v_sb[0:128, t * 1024 + h * 64: t * 1024 + (h + 1) * 64],
                     start=True, stop=True)
            for h in range(16) for ec in range(2)][-1],
          waits=[(s_acK, ack), (s_acV, acv), psw(0), psw(1), psw(2), psw(3)],
          incs=[(s_peV, 1)])
        E("pe", lambda e: [
            e.matmul(PS[6][0:128, sl:sl + 1],
                     kp_all[0:128, (sl // 2) * 256 + (sl % 2) * 128:
                            (sl // 2) * 256 + (sl % 2 + 1) * 128],
                     onecol[0:128, 0:1], start=True, stop=True) for sl in range(32)][-1],
          waits=[psw(6)], incs=[(s_peV, 1)])
        G.s("kpW", s_peV)
        pv = s_peV.n
        first = (tc == 0)
        for b_ in range(4):
            if first:
                E("dve", lambda e, b=b_: e.tensor_copy(kvacc[0:128, b * 512:(b + 1) * 512],
                                                       PS[b][0:128, 0:512]),
                  waits=[(s_peV, pv)])
            else:
                E("dve", lambda e, b=b_: e.tensor_tensor(kvacc[0:128, b * 512:(b + 1) * 512],
                                                         kvacc[0:128, b * 512:(b + 1) * 512],
                                                         PS[b][0:128, 0:512], ALU.add),
                  waits=[(s_peV, pv)])
        if first:
            E("dve", lambda e: e.tensor_copy(ksum[0:128, 0:32], PS[6][0:128, 0:32]),
              incs=[(s_dvK, 1)])
        else:
            E("dve", lambda e: e.tensor_tensor(ksum[0:128, 0:32], ksum[0:128, 0:32],
                                               PS[6][0:128, 0:32], ALU.add), incs=[(s_dvK, 1)])
        for b_ in range(4):
            pss(b_, s_dvK)
        pss(6, s_dvK)

    def emit_k_tc(tc, ppn):
        emit_k_rope(tc, ppn)
        emit_k_rest(tc)

    def emit_attn(seq):
        ln_done = {}
        dvo_done = {}
        if seq > 0:
            E("pool", lambda e: [
                e.memset(xn1.ap(0, 128, [[2050, 8], [1, 1]], off=0), 0.0),
                e.memset(xn1.ap(0, 128, [[2050, 8], [1, 1]], off=2049), 0.0)][-1],
              incs=[(s_setP, 1)])
        G.m["pads"] = (s_setP, s_setP.n)
        awc_n, awr_n, awp_n = emit_aw(2)
        for tc in range(4):
            emit_ln1_tc(seq, tc)
            ln_done[tc] = s_lnA.n
        for p_i, (p, kind) in enumerate([(2, "v"), (0, "q"), (1, "k")][:_DBG_NP]):
            if p_i > 0:
                awc_n, awr_n, awp_n = emit_aw(p)
            if kind in ("v", "q") and _DBG_NBLK == 4 and _DBG_SERIAL == 0:
                if p_i == 0:
                    for tc in range(4, 8):
                        emit_ln1_tc(seq, tc)
                        ln_done[tc] = s_lnA.n
                for oc in range(8):
                    emit_conv_group(p, 0, oc, ln_done[4], awc_n)
                conv_done = s_acC.n
                ln_q = []
                ln_sq = []
                for blk in range(4):
                    emit_rff(p, conv_done, awr_n)
                    if p_i == 0 and blk < 2:
                        t0 = 4 * blk + 8
                        emit_ln1_stats(seq, t0)
                        emit_ln1_stats(seq, t0 + 1)
                        emit_ln1_tp(t0)
                        ln_done[t0] = s_lnA.n
                        ln_sq = [t0 + 2, t0 + 3]
                        ln_q = [t0 + 1, t0 + 2, t0 + 3]
                    if blk + 1 < 4:
                        lnn = ln_done[min(4 * (blk + 1) + 4, NTC - 1)]
                        emit_conv_group(p, blk + 1, 0, lnn, awc_n)
                        emit_conv_group(p, blk + 1, 1, lnn, awc_n)
                    if kind == "q" and blk > 0:
                        emit_q_tp(blk * 4 - 1)
                    acs = s_acS.n
                    for i in range(4):
                        tc = blk * 4 + i
                        ppn = emit_projmm(p, i, awp_n, acs)
                        if kind == "v":
                            emit_v_tc(tc, ppn)
                        else:
                            emit_q_rope(tc, ppn)
                        if ln_sq:
                            emit_ln1_stats(seq, ln_sq.pop(0))
                        if blk + 1 < 4 and i < 3:
                            lnn = ln_done[min(4 * (blk + 1) + 4, NTC - 1)]
                            emit_conv_group(p, blk + 1, 2 + 2 * i, lnn, awc_n)
                            emit_conv_group(p, blk + 1, 3 + 2 * i, lnn, awc_n)
                        if ln_q:
                            t_ = ln_q.pop(0)
                            emit_ln1_tp(t_)
                            ln_done[t_] = s_lnA.n
                        if kind == "q" and i > 0:
                            emit_q_tp(tc - 1)
                    if blk + 1 < 4:
                        conv_done = s_acC.n
                    G.s("featsW", s_peP)
                if kind == "q":
                    emit_q_tp(NTC - 1)
            elif kind == "k" and _DBG_NBLK == 4 and _DBG_SERIAL == 0:
                for blk in range(4):
                    emit_convrff(p, blk, ln_done[min(4 * blk + 4, NTC - 1)],
                                 awc_n, awr_n)
                    if blk > 0:
                        emit_k_rest(blk * 4 - 1)
                    acs = s_acS.n
                    for i in range(4):
                        tc = blk * 4 + i
                        bks = (4, 5) if i % 2 == 0 else (2, 3)
                        ppn = emit_projmm(p, i, awp_n, acs, bks)
                        emit_k_rope(tc, ppn, bks)
                        if i > 0:
                            emit_k_rest(tc - 1)
                    G.s("featsW", s_peP)
                emit_k_rest(NTC - 1)
            else:
                for blk in range(_DBG_NBLK):
                    if p_i == 0:
                        for tc in range(4 * blk + 4, min(4 * blk + 8, NTC)):
                            emit_ln1_tc(seq, tc)
                            ln_done[tc] = s_lnA.n
                    lnA_need = ln_done[min(4 * blk + 4, NTC - 1)]
                    emit_convrff(p, blk, lnA_need, awc_n, awr_n)
                    if _DBG_SERIAL in (1, 2):
                        em.barrier(s_bar)
                    acs = s_acS.n
                    for i in range(4):
                        tc = blk * 4 + i
                        ppn = emit_projmm(p, i, awp_n, acs)
                        if kind == "v":
                            emit_v_tc(tc, ppn)
                        elif kind == "q":
                            emit_q_tc(tc, ppn)
                        else:
                            emit_k_tc(tc, ppn)
                        if _DBG_SERIAL in (1, 3):
                            em.barrier(s_bar)
                    G.s("featsW", s_peP)
            if kind == "v":
                G.s("ck", s_peC)
                G.s("rw", s_peR)
                G.s("pw", s_peP)
            elif kind == "q":
                G.s("ck", s_peC)
                G.s("rw", s_peR)
                G.s("pw", s_peP)
        E("pool", lambda e: e.dma_start(
            outw_t.ap(0, 128, [[1024, 8], [1, 1024]]),
            bass.AP(OW, 0, [[1024, 128], [128 * 1024, 8], [1, 1024]])),
          waits=[(s_peT, s_peT.n)], incs=[(s_ow, 16)], dma=True)
        ow_n = s_ow.n
        if _DBG_NP < 3:
            return
        E("dve", lambda e: e.tensor_copy(
            kvks.ap(0, 128, [[65, 32], [1, 64]], off=0),
            kvacc.ap(0, 128, [[64, 32], [1, 64]], off=0)),
          waits=[(s_dvK, s_dvK.n), G.w("cwsv")], incs=[(s_dvB, 1)])
        E("dve", lambda e: e.tensor_copy(
            kvks.ap(0, 128, [[65, 32], [1, 1]], off=64),
            ksum.ap(0, 128, [[1, 32]], off=0)), incs=[(s_dvB, 1)])
        kvready = s_dvB.n
        pev_all = s_peV.n
        attnpre_done = {}
        iters = [(h, tt) for h in range(NH) for tt in range(4)]
        NIT = len(iters)
        st_fav = {}
        st_nd = {}
        st_rbc = {}
        st_exp = {}
        st_evac = {}
        st_recip = {}
        st_mult = {}
        st_qsq = {}

        def q_slice(i):
            h, tt = iters[i]
            return q_fm.ap((h % 2) * 64, 64, [[1, 512]], off=(h // 2) * 2048 + tt * 512)

        qf_ready = s_acT.n

        def emit_qsq(i):
            par = i % 2
            bb = (iters[i][0] % 2) * 64
            w = ([(s_peB, st_fav[i - 2])] if i - 2 in st_fav
                 else [(s_peV, pev_all), (s_acT, qf_ready)])
            E("pool", lambda e, qs=q_slice(i), p=par, b=bb: e.tensor_tensor(
                qsq_t[p].ap(b, 64, [[1, 512]]), qs, qs, ALU.mult), waits=w, incs=[(s_poB, 1)])
            st_qsq[i] = s_poB.n

        def emit_fav(i):
            par = i % 2
            h, tt = iters[i]
            bb = (h % 2) * 64
            w = [(s_poB, st_qsq[i])]
            if i - 2 in st_exp:
                w.append((s_acB, st_exp[i - 2]))
            else:
                w += [psw(2 * par), psw(2 * par + 1)]
            E("pe", lambda e, qs=q_slice(i), b=bb, p=par: [
                y for ec in range(2) for y in (
                    e.matmul(PS[2 * p + ec][0:128, 0:512],
                             om_dup.ap(b, 64, [[1, 128]], off=ec * 128), qs,
                             start=True, stop=False),
                    e.matmul(PS[2 * p + ec][0:128, 0:512], mhalf.ap(b, 64, [[1, 128]]),
                             qsq_t[p].ap(b, 64, [[1, 512]]), start=False, stop=True))][-1],
              waits=w, incs=[(s_peB, 1)])
            st_fav[i] = s_peB.n

        def emit_exp(i):
            par = i % 2
            w = [(s_peB, st_fav[i])]
            if i - 2 in st_nd:
                w.append((s_peB, st_nd[i - 2]))
            for ec in range(2):
                E("act", lambda e, ec=ec, p=par: e.activation(
                    qp_t[p][0:128, ec * 512:(ec + 1) * 512], PS[2 * p + ec][0:128, 0:512],
                    AF.Exp, bias=bln16[0:128, 0:1]),
                  waits=w if ec == 0 else [], incs=[(s_acB, 1)])
            st_exp[i] = s_acB.n

        def emit_nd(i):
            par = i % 2
            h, tt = iters[i]
            w = [(s_acB, st_exp[i]), (s_dvB, kvready)]
            if i - 2 in st_evac:
                w.append((s_acB, st_evac[i - 2]))
            else:
                w.append(psw(4 + par))
            E("pe", lambda e, hh=h, p=par: [
                e.matmul(PS[4 + p][0:65, 0:512], kvks[0:128, (2 * hh) * 65:(2 * hh + 1) * 65],
                         qp_t[p][0:128, 0:512], start=True, stop=False),
                e.matmul(PS[4 + p][0:65, 0:512], kvks[0:128, (2 * hh + 1) * 65:(2 * hh + 2) * 65],
                         qp_t[p][0:128, 512:1024], start=False, stop=True)][-1],
              waits=w, incs=[(s_peB, 1)])
            st_nd[i] = s_peB.n

        def emit_evac(i):
            p3 = i % 3
            par = i % 2
            w = [(s_peB, st_nd[i])]
            if i - 3 in st_mult:
                w.append((s_dvB, st_mult[i - 3]))
            if i - 3 in st_recip:
                w.append((s_dvB, st_recip[i - 3]))
            E("act", lambda e, p=p3, pp=par: e.activation(
                nd_t[p][0:65, 0:512], PS[4 + pp][0:65, 0:512], AF.Identity,
                bias=eps65[0:65, 0:1]), waits=w, incs=[(s_acB, 1)])
            st_evac[i] = s_acB.n

        def emit_recip(i):
            p3 = i % 3
            w = [(s_acB, st_evac[i])]
            def _recip(e, p=p3):
                with nc.allow_low_precision(reason="bf16 denom reciprocal (operand-level rounding)"):
                    return e.reciprocal(nd_t[p].ap(64, 1, [[1, 512]]),
                                        nd_t[p].ap(64, 1, [[1, 512]]))
            E("dve", _recip, waits=w, incs=[(s_dvB, 1)])
            st_recip[i] = s_dvB.n

        def emit_rbc(i):
            p3 = i % 3
            par = i % 2
            w = [(s_dvB, st_recip[i])]
            if i - 2 in st_mult:
                w.append((s_dvB, st_mult[i - 2]))
            else:
                w.append(psw(6 + par))
            E("pe", lambda e, p=p3, pp=par: e.matmul(
                PS[6 + pp][0:64, 0:512], ones1.ap(64, 1, [[1, 64]]),
                nd_t[p].ap(64, 1, [[1, 512]]), start=True, stop=True),
              waits=w, incs=[(s_peB, 1)])
            st_rbc[i] = s_peB.n

        def emit_mult(i):
            p3 = i % 3
            par = i % 2
            h, tt = iters[i]
            bb = (h % 2) * 64
            dc = h // 2
            E("dve", lambda e, p=p3, pp=par, b=bb, d=dc, t=tt: e.tensor_tensor(
                attnpre.ap(b, 64, [[1, 512]], off=d * 2048 + t * 512),
                nd_t[p].ap(0, 64, [[1, 512]]), PS[6 + pp][0:64, 0:512], ALU.mult),
              waits=[(s_peB, st_rbc[i])], incs=[(s_dvB, 1)])
            st_mult[i] = s_dvB.n
            h2, tt2 = iters[i]
            if h2 == NH - 1:
                attnpre_done[tt2] = s_dvB.n

        emit_qsq(0)
        for i in range(NIT):
            if i + 1 < NIT:
                emit_qsq(i + 1)
            emit_fav(i)
            emit_exp(i)
            if i - 2 >= 0:
                emit_rbc(i - 2)
                emit_mult(i - 2)
            if i - 1 >= 0:
                emit_nd(i - 1)
                emit_evac(i - 1)
                emit_recip(i - 1)
        emit_nd(NIT - 1)
        emit_evac(NIT - 1)
        emit_recip(NIT - 1)
        for i_ in (NIT - 2, NIT - 1):
            emit_rbc(i_)
            emit_mult(i_)
        for b_ in range(6):
            G.m[("ps", b_)] = (s_acB, s_acB.n)
        for b_ in (6, 7):
            G.m[("ps", b_)] = (s_dvB, s_dvB.n)
        G.s(("xb2", 0), s_peB)
        G.s(("xb2", 1), s_peB)
        for tc in range(NTC):
            par = tc % 2
            if tc >= 2:
                emit_ln2_tc(seq, tc - 2, dvo_done[tc - 2])
            E("sp", lambda e, t=tc, pq=par: e.dma_start(
                xb2[pq][0:128, 0:1024],
                bass.AP(X, (seq * S + t * 128) * D, [[D, 128], [1, D]])),
              waits=[G.w(("xb2", par)), G.w("cwsv") if tc < 2 else None],
              incs=[(s_x2[par], 16)], dma=True)
            nx = s_x2[par].n
            bks = (0, 1) if tc % 2 == 0 else (2, 3)
            E("pe", lambda e, t=tc, pp=(PS[bks[0]], PS[bks[1]]): [
                [e.matmul(pp[half][0:128, 0:512],
                          attnpre[0:128, dc2 * 2048 + t * 128: dc2 * 2048 + (t + 1) * 128],
                          outw_t.ap(0, 128, [[1, 512]], off=dc2 * 1024 + half * 512),
                          start=(dc2 == 0), stop=False)
                 for half in range(2) for dc2 in range(8)],
                e.matmul(pp[0][0:128, 0:512], ones1[0:1, 0:128], outbr[0:1, 0:512],
                         start=False, stop=True),
                e.matmul(pp[1][0:128, 0:512], ones1[0:1, 0:128], outbr[0:1, 512:1024],
                         start=False, stop=True)][-1],
              waits=[(s_dvB, attnpre_done[tc // 4]), (s_ow, ow_n), psw(bks[0]), psw(bks[1])],
              incs=[(s_peO, 1)])
            po = s_peO.n
            E("dve", lambda e, t=tc, pq=par, pp=(PS[bks[0]], PS[bks[1]]): [
                e.tensor_tensor(x2ap(t, 0, 512),
                                xb2[pq][0:128, 0:512], pp[0][0:128, 0:512], ALU.add),
                e.tensor_tensor(x2ap(t, 512, 1024),
                                xb2[pq][0:128, 512:1024], pp[1][0:128, 0:512], ALU.add)][-1],
              waits=[(s_peO, po), (s_x2[par], nx)], incs=[(s_dvO, 1)])
            pss(bks[0], s_dvO)
            pss(bks[1], s_dvO)
            G.s(("xb2", par), s_dvO)
            dvo_done[tc] = s_dvO.n
        emit_ln2_tc(seq, NTC - 2, dvo_done[NTC - 2])
        emit_ln2_tc(seq, NTC - 1, dvo_done[NTC - 1])
        E("pool", lambda e: e.nop(),
          waits=[(s_acc[t2], s_acc[t2].n) for t2 in range(NTC)], incs=[(s_fl, 1)])

    def emit_ln2_tc(seq, tc, dvo_wait):
        em.pending["dve"].append((s_dvO, dvo_wait))
        emit_ln_tc(lambda a, b, t=tc: x2ap(t, a, b), xn2, 2048, 0, tc, s_ln2A,
                   xsq=xsq2, xnt=xn_tm2)
        dvl = s_lnD.n
        E("pool", lambda e, t=tc: e.dma_start(
            bass.AP(OUT, (seq * S + t * 128) * D, [[D, 128], [1, D]]),
            x2ap(t, 0, 1024)),
          waits=[(s_lnD, dvl)], incs=[(s_acc[tc], 16)], dma=True)

    def emit_router():
        for tt in range(4):
            bk = tt % 2
            E("pe", lambda e, t=tt, ps=PS[bk]: [
                e.matmul(ps[0:8, 0:512], rwt_b[0:128, dc * 8:(dc + 1) * 8],
                         xn2.ap(0, 128, [[1, 512]], off=dc * 2048 + t * 512),
                         start=(dc == 0), stop=(dc == 7)) for dc in range(8)][-1],
              waits=[(s_ln2A, s_ln2A.n), psw(bk)], incs=[(s_lg, 1)])
            pl = s_lg.n
            E("act", lambda e, t=tt, ps=PS[bk]: e.activation(
                logits.ap(0, 8, [[1, 512]], off=t * 512), ps[0:8, 0:512], AF.Identity,
                bias=expb[0:8, 0:1]),
              waits=[(s_lg, pl)], incs=[(s_lga, 1)])
            pss(bk, s_lga)
        lgall = s_lga.n
        E("pe", lambda e: [
            e.transpose(PSB[2][0:128, t * 8:(t + 1) * 8],
                        logits.ap(0, 8, [[1, 128]], off=t * 128), ident[0:8, 0:8])
            for t in range(16)][-1],
          waits=[(s_lga, lgall), psw(2)], incs=[(s_lg, 1)])
        pl = s_lg.n
        E("act", lambda e: e.activation(rB1[0:128, 0:128], PSB[2][0:128, 0:128],
                                        AF.Identity),
          waits=[(s_lg, pl)], incs=[(s_cwA, 1)])
        pss(2, s_cwA)
        v8 = lambda t_: t_.ap(0, 128, [[8, 16], [1, 8]])
        bc = lambda off: rsc.ap(0, 128, [[1, 16], [0, 8]], off=off)
        E("dve", lambda e: e.tensor_reduce(rsc[0:128, 0:16], v8(rB1), AX.X, ALU.max),
          waits=[(s_cwA, s_cwA.n)], incs=[(s_cwD, 1)])
        E("dve", lambda e: e.tensor_tensor(rB2[0:128, 0:128], rB1[0:128, 0:128],
                                           bc(0), ALU.subtract),
          waits=[(s_cwD, s_cwD.n)], incs=[(s_cwD, 1)])
        E("act", lambda e: e.activation(rB2[0:128, 0:128], rB2[0:128, 0:128], AF.Exp),
          waits=[(s_cwD, s_cwD.n)], incs=[(s_cwA, 1)])
        E("dve", lambda e: e.tensor_single_scalar(rB3[0:128, 0:128], rB2[0:128, 0:128],
                                                  1.0, ALU.is_ge),
          waits=[(s_cwA, s_cwA.n)], incs=[(s_cwD, 1)])
        E("dve", lambda e: e.tensor_tensor(rB1[0:128, 0:128], rB2[0:128, 0:128],
                                           rB3[0:128, 0:128], ALU.subtract),
          waits=[(s_cwD, s_cwD.n)], incs=[(s_cwD, 1)])
        E("dve", lambda e: e.tensor_reduce(rsc[0:128, 16:32], v8(rB1), AX.X, ALU.max),
          waits=[(s_cwD, s_cwD.n)], incs=[(s_cwD, 1)])
        E("dve", lambda e: e.tensor_tensor(rB1[0:128, 0:128], rB2[0:128, 0:128],
                                           bc(16), ALU.is_ge),
          waits=[(s_cwD, s_cwD.n)], incs=[(s_cwD, 1)])
        E("dve", lambda e: e.tensor_tensor(rB3[0:128, 0:128], rB2[0:128, 0:128],
                                           rB1[0:128, 0:128], ALU.mult),
          waits=[(s_cwD, s_cwD.n)], incs=[(s_cwD, 1)])
        E("dve", lambda e: e.tensor_single_scalar(rsc[0:128, 32:48], rsc[0:128, 16:32],
                                                  1.0, ALU.add),
          waits=[(s_cwD, s_cwD.n)], incs=[(s_cwD, 1)])
        E("dve", lambda e: e.reciprocal(rsc[0:128, 0:16], rsc[0:128, 32:48]),
          waits=[(s_cwD, s_cwD.n)], incs=[(s_cwD, 1)])
        E("dve", lambda e: e.tensor_tensor(cw_tm[0:128, 0:128], rB3[0:128, 0:128],
                                           bc(0), ALU.mult),
          waits=[(s_cwD, s_cwD.n)], incs=[(s_cwD, 1)])
        dcw = s_cwD.n
        for tc in range(NTC):
            bk2 = 4 + tc % 2
            E("pe", lambda e, t=tc, ps=PSB[bk2]: e.transpose(
                ps[0:8, 0:128], cw_tm.ap(0, 128, [[1, 8]], off=t * 8), ident[0:128, 0:128]),
              waits=[(s_cwD, dcw) if tc < 2 else None, psw(bk2)], incs=[(s_lg, 1)])
            pt2 = s_lg.n
            E("act", lambda e, t=tc, ps=PSB[bk2]: e.activation(
                cw_fm.ap(0, 8, [[1, 128]], off=t * 128), ps[0:8, 0:128], AF.Identity),
              waits=[(s_lg, pt2), G.w("cwsv") if tc == 0 else None], incs=[(s_cwA, 1)])
            pss(bk2, s_cwA)

    def emit_experts(passes, fl_all):
        experts = [(SW1, SW3, SW2, 0, False)] + [(W1, W3, W2, e, True) for e in range(8)]
        if _DBG_NE is not None:
            experts = experts[:_DBG_NE]
        sems_w = {}
        npass = len(passes)
        nfl = fl_all // 1

        def emit_w13_chunk(ei, ch, wait_w1, wait_w3):
            TW1, TW3, _, eidx, _ = experts[ei]
            E("pool", lambda e, tw=TW1, ex=eidx, c=ch: e.dma_start(
                w1b.ap(0, 128, [[2048, 8], [1, 1024]], off=c * 1024),
                bass.AP(tw, ex * 1024 * 2048 + c * 1024,
                        [[2048, 128], [128 * 2048, 8], [1, 1024]])),
              waits=[wait_w1, (s_fl, fl_all)], incs=[(s_w1[ch], 16)], dma=True)
            E("pool", lambda e, tw=TW3, ex=eidx, c=ch: e.dma_start(
                w3b.ap(0, 128, [[2048, 8], [1, 1024]], off=c * 1024),
                bass.AP(tw, ex * 1024 * 2048 + c * 1024,
                        [[2048, 128], [128 * 2048, 8], [1, 1024]])),
              waits=[wait_w3, (s_fl, fl_all)], incs=[(s_w3[ch], 16)], dma=True)
            a, b, c2, d2 = sems_w.setdefault(ei, ([0, 0], [0, 0], 0, 0))
            a[ch] = s_w1[ch].n
            b[ch] = s_w3[ch].n

        def emit_w2(ei, wait_w2):
            _, _, TW2, eidx, routed = experts[ei]
            E("pool", lambda e, t2w=TW2, ex=eidx: e.dma_start(
                w2b.ap(0, 128, [[1024, 16], [1, 1024]]),
                bass.AP(t2w, ex * 2048 * 1024, [[1024, 128], [128 * 1024, 16], [1, 1024]])),
              waits=[wait_w2, (s_ln2A, s_ln2A.n), (s_cwA, s_cwA.n), (s_fl, fl_all)],
              incs=[(s_w2, 16)], dma=True)
            a, b, c2, d2 = sems_w.setdefault(ei, ([0, 0], [0, 0], 0, 0))
            sems_w[ei] = (a, b, s_w2.n, d2)

        cwr_n = {}

        def emit_cwrow(ei, P):
            _, _, _, eidx, routed = experts[ei]
            if routed:
                cpar = P["pi"]
                E("sp", lambda e, ex=eidx, cp=cpar, src=P["cw"]: e.dma_start(
                    cwrow.ap(0, 1, [[1, 2048]], off=cp * 2048),
                    src.ap(ex, 1, [[1, 2048]])),
                  waits=[P["cw_wait"], G.w(("cwrowW", cpar))],
                  incs=[(s_cwr, 16)], dma=True)
            cwr_n[(ei, P["pi"])] = s_cwr.n

        def emit_cwb(ei, P):
            _, _, _, eidx, routed = experts[ei]
            if not routed:
                return
            cpar = P["pi"]
            ncr = cwr_n[(ei, cpar)]
            for t2 in range(4):
                E("pe", lambda e, t=t2, cp=cpar: e.matmul(
                    PS[t % 2][0:128, 0:512], ones1[0:1, 0:128],
                    cwrow.ap(0, 1, [[1, 512]], off=cp * 2048 + t * 512),
                    start=True, stop=True),
                  waits=[(s_cwr, ncr), psw(t2 % 2)], incs=[(s_peG, 1)])
                if t2 == 3:
                    G.s(("cwrowW", cpar), s_peG)
                pgc = s_peG.n
                E("act", lambda e, t=t2, cp=cpar: e.activation(
                    cwb.ap(0, 128, [[1, 512]], off=cp * 2048 + t * 512),
                    PS[t % 2][0:128, 0:512], AF.Identity),
                  waits=[(s_peG, pgc), G.w(("cwbW", cpar)) if t2 == 0 else None],
                  incs=[(s_acF, 1)])
                pss(t2 % 2, s_acF)

        emit_w13_chunk(0, 0, G.w(("w1c", 0)), G.w(("w3c", 0)))
        emit_w13_chunk(0, 1, G.w(("w1c", 1)), G.w(("w3c", 1)))
        emit_w2(0, G.w("w2"))
        for P in passes:
            emit_cwrow(0, P)
        for ei, (TW1, TW3, TW2, eidx, routed) in enumerate(experts):
            nw1c, nw3c, nw2, _ = sems_w[ei]
            for pi_, P in enumerate(passes):
                last = pi_ == npass - 1
                cpar = P["pi"]
                peG0, peU0, peY0 = s_peG.n, s_peU.n, s_peY.n
                emit_cwb(ei, P)
                if ei + 1 < len(experts):
                    emit_cwrow(ei + 1, P)
                cwb_ready = s_acF.n
                for tt in range(4 if _DBG_NTT is None else _DBG_NTT):
                    for hc in range(16):
                        if last and tt == 3 and hc == 8 and ei + 1 < len(experts):
                            nc4 = 4 if routed else 0
                            emit_w13_chunk(ei + 1, 0, (s_peG, peG0 + nc4 + 56),
                                           (s_peU, peU0 + 56))
                        gbk = hc % 2
                        ubk = 2 + hc % 2
                        xw = P["xw"] if (tt == 0 and hc == 0) else []
                        E("pe", lambda e, hcv=hc, t=tt, ps=PS[gbk], xnt=P["xnt"]: [
                            e.matmul(ps[0:128, 0:512],
                                     w1b.ap(0, 128, [[1, 128]], off=dc * 2048 + hcv * 128),
                                     xnt.ap(0, 128, [[1, 512]], off=dc * 2048 + t * 512),
                                     start=(dc == 0), stop=(dc == 7)) for dc in range(8)][-1],
                          waits=[(s_w1[hc // 8], nw1c[hc // 8]), psw(gbk)] + xw,
                          incs=[(s_peG, 1)])
                        pg = s_peG.n
                        if last and tt == 3 and hc == 7:
                            G.s(("w1c", 0), s_peG)
                        if last and tt == 3 and hc == 15:
                            G.s(("w1c", 1), s_peG)
                        E("pe", lambda e, hcv=hc, t=tt, ps=PS[ubk], xnt=P["xnt"]: [
                            e.matmul(ps[0:128, 0:512],
                                     w3b.ap(0, 128, [[1, 128]], off=dc * 2048 + hcv * 128),
                                     xnt.ap(0, 128, [[1, 512]], off=dc * 2048 + t * 512),
                                     start=(dc == 0), stop=(dc == 7)) for dc in range(8)][-1],
                          waits=[(s_w3[hc // 8], nw3c[hc // 8]), psw(ubk)] + xw,
                          incs=[(s_peU, 1)])
                        pu = s_peU.n
                        if last and tt == 3 and hc == 7:
                            G.s(("w3c", 0), s_peU)
                        if last and tt == 3 and hc == 15:
                            G.s(("w3c", 1), s_peU)
                        E("act", lambda e, hcv=hc, ps=PS[gbk]: e.activation(
                            gs_t[hcv % 2][0:128, 0:512], ps[0:128, 0:512], AF.Silu),
                          waits=[(s_peG, pg), G.w(("gsW", hc % 2))], incs=[(s_acF, 1)])
                        pss(gbk, s_acF)
                        if routed:
                            E("dve", lambda e, hcv=hc, ps=PS[ubk], t=tt, cp=cpar: e.scalar_tensor_tensor(
                                us_t[hcv % 2][0:128, 0:512], ps[0:128, 0:512], 1.0,
                                cwb.ap(0, 128, [[1, 512]], off=cp * 2048 + t * 512),
                                ALU.mult, ALU.mult),
                              waits=[(s_peU, pu), (s_acF, cwb_ready), G.w(("usW", hc % 2))],
                              incs=[(s_dvF, 1)])
                        else:
                            E("dve", lambda e, hcv=hc, ps=PS[ubk]: e.tensor_copy(
                                us_t[hcv % 2][0:128, 0:512], ps[0:128, 0:512]),
                              waits=[(s_peU, pu), G.w(("usW", hc % 2))], incs=[(s_dvF, 1)])
                        pss(ubk, s_dvF)
                        af, df = s_acF.n, s_dvF.n
                        E("pool", lambda e, hcv=hc: e.tensor_tensor(
                            eo_all[0:128, hcv * 512:(hcv + 1) * 512],
                            gs_t[hcv % 2][0:128, 0:512], us_t[hcv % 2][0:128, 0:512], ALU.mult),
                          waits=[(s_acF, af), (s_dvF, df), G.w(("eoW", hc))], incs=[(s_poF, 1)])
                        G.s(("gsW", hc % 2), s_poF)
                        G.s(("usW", hc % 2), s_poF)
                        if hc == 7:
                            po7 = s_poF.n
                    if last and tt == 3 and ei + 1 < len(experts):
                        nc4 = 4 if routed else 0
                        emit_w13_chunk(ei + 1, 1, (s_peG, peG0 + nc4 + 64),
                                       (s_peU, peU0 + 64))
                    po = s_poF.n
                    for i in range(4):
                        tc = tt * 4 + i
                        ybk = (4, 5) if i % 2 == 0 else (6, 7)
                        E("pe", lambda e, t=tc, ii=i, pp=(PS[ybk[0]], PS[ybk[1]]): [
                            e.matmul(pp[half][0:128, 0:512],
                                     eo_all[0:128, hc2 * 512 + ii * 128: hc2 * 512 + (ii + 1) * 128],
                                     w2b.ap(0, 128, [[1, 512]], off=hc2 * 1024 + half * 512),
                                     start=(hc2 == 0), stop=False)
                            for half in range(2) for hc2 in range(8)][-1],
                          waits=[(s_poF, po7), (s_w2, nw2), psw(ybk[0]), psw(ybk[1])],
                          incs=[(s_peY, 1)])
                        E("pe", lambda e, t=tc, ii=i, pp=(PS[ybk[0]], PS[ybk[1]]): [
                            e.matmul(pp[half][0:128, 0:512],
                                     eo_all[0:128, hc2 * 512 + ii * 128: hc2 * 512 + (ii + 1) * 128],
                                     w2b.ap(0, 128, [[1, 512]], off=hc2 * 1024 + half * 512),
                                     start=False, stop=(hc2 == 15))
                            for half in range(2) for hc2 in range(8, 16)][-1],
                          waits=[(s_poF, po)], incs=[(s_peY, 1)])
                        py = s_peY.n
                        E("act", lambda e, pp=(PS[ybk[0]], PS[ybk[1]]), px=px_t[i % 2]: [
                            e.activation(px[0:128, 0:512], pp[0][0:128, 0:512], AF.Identity),
                            e.activation(px[0:128, 512:1024], pp[1][0:128, 0:512],
                                         AF.Identity)][-1],
                          waits=[(s_peY, py), G.w(("pxW", i % 2))], incs=[(s_acY, 1)])
                        pss(ybk[0], s_acY)
                        pss(ybk[1], s_acY)
                        ay = s_acY.n
                        accv = 16 * (nfl + npass * ei + pi_)
                        E("pool", lambda e, t=tc, px=px_t[i % 2], sq=P["sq"]: e.dma_start(
                            bass.AP(OUT, (sq * S + t * 128) * D, [[D, 128], [1, D]]),
                            px[0:128, 0:1024], accum_op=ALU.add),
                          waits=[(s_acY, ay), (s_acc[tc], accv)], incs=[(s_acc[tc], 16)], dma=True)
                        G.m[("pxW", i % 2)] = (s_acc[tc], s_acc[tc].n)
                    for hc in range(16):
                        G.m[("eoW", hc)] = (s_peY, s_peY.n)
                if routed:
                    G.s(("cwbW", cpar), s_dvF)
                ncwb = 4 if routed else 0
                assert s_peG.n == peG0 + ncwb + 64, (s_peG.n, peG0, ncwb)
                assert s_peU.n == peU0 + 64
                assert s_peY.n == peY0 + 32
            G.s("w2", s_peY)
            if ei + 1 < len(experts):
                emit_w2(ei + 1, (s_peY, peY0 + 32))
        E("pool", lambda e: e.nop(),
          waits=[(s_acc[t2], s_acc[t2].n) for t2 in range(NTC)], incs=[(s_fl, 1)])
        done = (s_fl, s_fl.n)
        for key in ("awx", "ck", "rw", "pw", ("xntW", id(xnt_b)),
                    ("kkW", 0), ("kkW", 1), ("qrW", 0), ("qrW", 1),
                    ("xtil", 0), ("xtil", 1)):
            G.m[key] = done

    emit_setup()
    if mode == "ffn1":
        s_dbg = em.sem("dbg")
        s_dbgp = em.sem("dbgp")
        if _DBG_PRE >= 1:
            E("sp", lambda e: e.dma_start(xn2[0:128, 0:16384], dbg["xn2_in"].ap()),
              incs=[(s_dbg, 16)], dma=True)
            E("pool", lambda e: e.dma_start(cw_fm[0:8, 0:2048], dbg["cw_in"].ap()),
              incs=[(s_dbgp, 16)], dma=True)
        for tc in range(NTC if _DBG_PRE >= 2 else 0):
            E("sp", lambda e, t=tc: e.dma_start(
                px_t[0][0:128, 0:1024], dbg["x2_in"].ap()[0:128, t * 1024:(t + 1) * 1024]),
              waits=[(s_acc[tc - 1], s_acc[tc - 1].n)] if tc > 0 else [],
              incs=[(s_dbg, 16)], dma=True)
            nd = s_dbg.n
            E("pool", lambda e, t=tc: e.dma_start(
                bass.AP(OUT, (t * 128) * D, [[D, 128], [1, D]]), px_t[0][0:128, 0:1024]),
              waits=[(s_dbg, nd)], incs=[(s_acc[tc], 16)], dma=True)
        if _DBG_PRE >= 2:
            E("pool", lambda e: e.nop(),
              waits=[(s_acc[t2], s_acc[t2].n) for t2 in range(NTC)], incs=[(s_fl, 1)])
        E("sp", lambda e: e.nop(), waits=[(s_fl, s_fl.n), (s_dbgp, s_dbgp.n)])
        em.barrier(s_bar)
        emit_experts([dict(pi=0, sq=0, xnt=xn2, cw=cw_fm,
                           cw_wait=(s_cwA, s_cwA.n), xw=[])], s_fl.n)
        em.barrier(s_bar)
        E("sp", lambda e: e.nop(), waits=[(s_acc[i], s_acc[i].n) for i in range(NTC)])
    elif mode == "attn1":
        emit_attn(0)
        dvo = s_dvO.n
        for tc in range(NTC):
            E("sp", lambda e, t=tc: e.dma_start(
                bass.AP(dbg["x2_out"], t * 1024, [[16384, 128], [1, 1024]]),
                x2ap(t, 0, 1024)),
              waits=[(s_dvO, dvo)], incs=[(s_fl, 16)], dma=True)
        allw = [(s, s.n) for s in em.sems if s.n > 0]
        for sc, v in allw:
            em.pending["sp"].append((sc, v))
        E("sp", lambda e: e.dma_start(dbg["arena_out"].ap(), a32h[:, :]),
          incs=[(s_fl, 16)], dma=True)
        E("sp", lambda e: e.nop(), waits=[(s_fl, s_fl.n)])
    else:
        for seq in range(BPC):
            emit_attn(seq)
            emit_router()
            emit_experts([dict(pi=0, sq=seq, xnt=xn2, cw=cw_fm,
                               cw_wait=(s_cwA, s_cwA.n), xw=[])], s_fl.n)
            em.barrier(s_bar)
        E("sp", lambda e: e.nop(), waits=[(s_acc[i], s_acc[i].n) for i in range(NTC)])

    em.finalize(es)
    es.close()
    return nc


def _rope_tables():
    inv = 1.0 / 10000.0 ** (np.arange(0, HD, 2, dtype=np.float32) / HD)
    f = np.arange(S, dtype=np.float32)[:, None] * inv[None, :]
    sn = (np.sin(f) * CSC).astype(np.float32)
    cs = (np.cos(f) * CSC).astype(np.float32)
    sstab = np.zeros((128, NTC * 64), np.float32)
    cctab = np.zeros((128, NTC * 64), np.float32)
    for tc in range(NTC):
        t0 = tc * 128
        sstab[:, tc * 64:tc * 64 + 32] = sn[t0:t0 + 128, :]
        sstab[:, tc * 64 + 32:tc * 64 + 64] = -sn[t0:t0 + 128, :]
        cctab[:, tc * 64:tc * 64 + 32] = cs[t0:t0 + 128, :]
        cctab[:, tc * 64 + 32:tc * 64 + 64] = cs[t0:t0 + 128, :]
    return (np.ascontiguousarray(sstab.astype(ml_dtypes.bfloat16)),
            np.ascontiguousarray(cctab.astype(ml_dtypes.bfloat16)))


def host_inputs(inp):
    f32c = lambda a: np.ascontiguousarray(np.asarray(a, dtype=np.float32))
    cb = f32c(np.asarray(inp["L0_attn_conv_b"]).reshape(3, 8, 128).transpose(2, 0, 1)
              .reshape(128, 24))
    rb = f32c(np.asarray(inp["L0_attn_rff_b"]).reshape(3, 3, 2, 128).transpose(3, 0, 1, 2)
              .reshape(128, 18))
    pb = f32c(np.asarray(inp["L0_attn_proj_b"]).reshape(1, 3072))
    ob = f32c(np.asarray(inp["L0_attn_out_b"]).reshape(1, 1024))
    lmT = f32c(np.asarray(inp["L0_attn_landmarks"]).T)
    rwT = f32c(np.asarray(inp["L0_ffn_router_w"]).T.reshape(8, 128, 8).transpose(1, 0, 2)
               .reshape(128, 64))
    eb = f32c(np.asarray(inp["L0_ffn_experts_bias"]).reshape(8, 1))
    sstab, cctab = _rope_tables()
    return {
        "conv_k": f32c(inp["L0_attn_conv_k"]), "conv_b": cb,
        "rff_w": f32c(inp["L0_attn_rff_w"]), "rff_b": rb,
        "proj_w": f32c(inp["L0_attn_proj_w"]), "proj_b": pb,
        "omega": f32c(inp["L0_attn_omega"]), "lmT": lmT,
        "out_w": f32c(inp["L0_attn_out_w"]), "out_b": ob,
        "rwT": rwT, "expb": eb,
        "w1": f32c(inp["L0_ffn_w1"]), "w3": f32c(inp["L0_ffn_w3"]),
        "w2": f32c(inp["L0_ffn_w2"]),
        "sw1": f32c(inp["L0_ffn_sw1"]), "sw3": f32c(inp["L0_ffn_sw3"]),
        "sw2": f32c(inp["L0_ffn_sw2"]),
        "sstab": sstab, "cctab": cctab,
    }


def kernel(**inputs):
    base = host_inputs(inputs)
    x = np.asarray(inputs["L0_attn_x"], dtype=np.float32)
    nc = build_program("full")
    in_maps = []
    for c in range(NCORES):
        m = dict(base)
        m["x"] = np.ascontiguousarray(x[c * BPC:(c + 1) * BPC])
        in_maps.append(m)
    res = run_bass_kernel_spmd(nc, in_maps, core_ids=list(range(NCORES)))
    return np.concatenate([r["out"] for r in res.results], axis=0)


# version 21
# speedup vs baseline: 1.1170x; 1.1170x over previous
import math
import numpy as np
import ml_dtypes
import concourse.bass as bass
import concourse.mybir as mybir
from concourse.bass_utils import run_bass_kernel_spmd

F32 = mybir.dt.float32
BF16 = mybir.dt.bfloat16
AF = mybir.ActivationFunctionType
ALU = mybir.AluOpType
AX = mybir.AxisListType

B, S, D = 16, 2048, 1024
NCORES = 8
BPC = B // NCORES
NH, HD = 16, 64
NTC = 16
GAMMAS = (0.5, 1.0, 2.0)
CSC = float(np.float32(math.sqrt(2.0 / 256.0)))
LN16 = float(np.float32(math.log(16.0)))
MAGIC = 12582912.0
TWO_PI = float(np.float32(2 * math.pi))
SINSHRINK = float(np.float32(1.0 - 4e-6))
AWID = 52000
_DBG_NE = None
_DBG_NTT = None
_DBG_PRE = 2
_DBG_NP = 3
_DBG_NBLK = 4
_DBG_SERIAL = 0


class SemC:
    __slots__ = ("name", "h", "n", "per")

    def __init__(self, name):
        self.name, self.h, self.n = name, None, 0


class Em:
    STREAMS = ("pe", "act", "dve", "pool", "sp")

    def __init__(self, nc):
        self.nc = nc
        self.ops = {s: [] for s in self.STREAMS}
        self.sems = []
        self.dirty = {s: False for s in self.STREAMS}
        self.pending = {s: [] for s in self.STREAMS}

    def sem(self, name):
        s = SemC(name)
        self.sems.append(s)
        return s

    def op(self, stream, body, waits=(), incs=(), dma=False):
        w = self.pending[stream] + [x for x in waits if x is not None]
        self.pending[stream] = []
        for s, n in incs:
            s.n += n
        self.ops[stream].append([w, body, list(incs), dma])
        self.dirty[stream] = True

    def barrier(self, bar):
        if not hasattr(bar, "per"):
            bar.per = {s: self.sem(f"{bar.name}_{s}") for s in self.STREAMS}
        edges = []
        for s in self.STREAMS:
            if self.ops[s]:
                bs = bar.per[s]
                self.ops[s].append([[], lambda e: e.nop(), [(bs, 1)], False])
                bs.n += 1
            edges.append((bar.per[s], bar.per[s].n))
        for s in self.STREAMS:
            for e_ in edges:
                if e_[0] is not bar.per[s]:
                    self.pending[s].append(e_)

    def n_ops(self):
        return sum(len(v) for v in self.ops.values())

    def finalize(self, es):
        for s in self.sems:
            s.h = es.enter_context(self.nc.semaphore(s.name))
        with self.nc.Block() as blk:
            def runner(stream):
                def run(eng):
                    for waits, body, incs, _dma in self.ops[stream]:
                        for sc, v in waits:
                            if v > 0:
                                eng.wait_ge(sc.h, v)
                        inst = body(eng)
                        for sc, n in incs:
                            inst.then_inc(sc.h, n)
                return run
            blk.tensor(runner("pe"))
            blk.scalar(runner("act"))
            blk.vector(runner("dve"))
            blk.gpsimd(runner("pool"))
            blk.sync(runner("sp"))


class Guard:
    def __init__(self):
        self.m = {}

    def w(self, key):
        return self.m.get(key)

    def s(self, key, sem):
        self.m[key] = (sem, sem.n)


class Tile:
    __slots__ = ("h", "wid", "c0", "cols")

    def __init__(self, h, wid, c0, cols):
        self.h, self.wid, self.c0, self.cols = h, wid, c0, cols

    def ap(self, p0=0, np_=128, fdims=None, off=0):
        base = p0 * self.wid + self.c0 + off
        if fdims is None:
            fdims = [[1, self.cols]]
        return bass.AP(self.h, base, [[self.wid, np_]] + [list(d) for d in fdims])

    def __getitem__(self, idx):
        ps, cs = idx
        p0 = ps.start or 0
        p1 = ps.stop if ps.stop is not None else 128
        c0 = cs.start or 0
        c1 = cs.stop if cs.stop is not None else self.cols
        return self.ap(p0, p1 - p0, [[1, c1 - c0]], off=c0)


def build_program(mode="full"):
    nc = bass.Bass("TRN2", target_bir_lowering=False)

    X = nc.dram_tensor("x", [BPC, S, D], F32, kind="ExternalInput")
    CK = nc.dram_tensor("conv_k", [3, 3, 256, 1024], F32, kind="ExternalInput")
    CB = nc.dram_tensor("conv_b", [128, 24], F32, kind="ExternalInput")
    RW = nc.dram_tensor("rff_w", [3, 3, 1024, 256], F32, kind="ExternalInput")
    RB = nc.dram_tensor("rff_b", [128, 18], F32, kind="ExternalInput")
    PW = nc.dram_tensor("proj_w", [3, 768, 1024], F32, kind="ExternalInput")
    PB = nc.dram_tensor("proj_b", [1, 3072], F32, kind="ExternalInput")
    OM = nc.dram_tensor("omega", [64, 256], F32, kind="ExternalInput")
    LMT = nc.dram_tensor("lmT", [64, 64], F32, kind="ExternalInput")
    OW = nc.dram_tensor("out_w", [1024, 1024], F32, kind="ExternalInput")
    OB = nc.dram_tensor("out_b", [1, 1024], F32, kind="ExternalInput")
    RT = nc.dram_tensor("rwT", [128, 64], F32, kind="ExternalInput")
    EBT = nc.dram_tensor("expb", [8, 1], F32, kind="ExternalInput")
    W1 = nc.dram_tensor("w1", [8, 1024, 2048], F32, kind="ExternalInput")
    W3 = nc.dram_tensor("w3", [8, 1024, 2048], F32, kind="ExternalInput")
    W2 = nc.dram_tensor("w2", [8, 2048, 1024], F32, kind="ExternalInput")
    SW1 = nc.dram_tensor("sw1", [1, 1024, 2048], F32, kind="ExternalInput")
    SW3 = nc.dram_tensor("sw3", [1, 1024, 2048], F32, kind="ExternalInput")
    SW2 = nc.dram_tensor("sw2", [1, 2048, 1024], F32, kind="ExternalInput")
    SST = nc.dram_tensor("sstab", [128, 1024], BF16, kind="ExternalInput")
    CCT = nc.dram_tensor("cctab", [128, 1024], BF16, kind="ExternalInput")
    OUT = nc.dram_tensor("out", [BPC, S, D], F32, kind="ExternalOutput")
    dbg = {}
    if mode == "attn1":
        dbg["x2_out"] = nc.dram_tensor("x2_out", [128, 16384], F32, kind="ExternalOutput")
        dbg["arena_out"] = nc.dram_tensor("arena_out", [128, AWID], F32, kind="ExternalOutput")
    if mode == "ffn1":
        dbg["xn2_in"] = nc.dram_tensor("xn2_in", [128, 16384], BF16, kind="ExternalInput")
        dbg["cw_in"] = nc.dram_tensor("cw_in", [8, 2048], F32, kind="ExternalInput")
        dbg["x2_in"] = nc.dram_tensor("x2_in", [128, 16384], F32, kind="ExternalInput")

    from contextlib import ExitStack
    es = ExitStack()
    a32h = es.enter_context(nc.sbuf_tensor("arena", [128, AWID], F32))
    abfh = a32h.bitcast(BF16)
    PS = [es.enter_context(nc.psum_tensor(f"ps{i}", [128, 512], F32)) for i in range(8)]
    PSB = [p.bitcast(BF16) for p in PS]

    def T32(c0, cols):
        return Tile(a32h, AWID, c0, cols)

    def TBF(c0, cols):
        return Tile(abfh, 2 * AWID, 2 * c0, cols)

    off_ = [0]
    def alloc(n):
        c = off_[0]
        off_[0] += n
        assert off_[0] <= AWID, off_[0]
        return c
    sst = TBF(alloc(512), 1024)
    cct = TBF(alloc(512), 1024)
    ident = TBF(alloc(64), 128)
    aug_om = TBF(alloc(128), 256)
    om_dup = TBF(alloc(128), 256)
    mhalf = TBF(alloc(64), 128)
    nylhs = TBF(alloc(32), 64)
    biasny = T32(alloc(1), 1)
    bias1 = T32(alloc(18), 18)
    bias2 = T32(alloc(18), 18)
    convb = T32(alloc(24), 24)
    rwt_b = TBF(alloc(32), 64)
    expb = T32(alloc(1), 1)
    pbrow = TBF(alloc(1536), 3072)
    outbr = TBF(alloc(512), 1024)
    ones1 = TBF(alloc(64), 128)
    onecol = TBF(alloc(1), 1)
    bln16 = T32(alloc(1), 1)
    epsln = T32(alloc(1), 1)
    eps6 = T32(alloc(1), 1)
    m64t = T32(alloc(1), 1)
    small = T32(alloc(16), 16)
    rsc = T32(alloc(64), 64)
    eps65 = Tile(rsc.h, rsc.wid, rsc.c0 + 48, 1)
    c_xn1 = alloc(8200)
    c_ck = alloc(3072)
    c_rw = alloc(3072)
    c_pw = alloc(3072)
    c_q = alloc(8192)
    c_v = alloc(8192)
    c_kv = alloc(2080)
    c_ow = alloc(4096)
    c_tz = alloc(6656)
    c_fz = alloc(1536)
    assert off_[0] <= AWID, off_[0]

    xn1 = TBF(c_xn1, 16400)
    ckt = TBF(c_ck, 6144)
    rwt = TBF(c_rw, 6144)
    pwt = TBF(c_pw, 6144)
    q_fm = TBF(c_q, 16384)
    v_sb = TBF(c_v, 16384)
    attnpre = TBF(c_v, 16384)
    kvacc = T32(c_kv, 2048)
    ksum = T32(c_kv + 2048, 32)
    outw_t = TBF(c_ow, 8192)
    kvks = TBF(c_tz + 5120, 2080)
    x_til = [T32(c_tz + 1024 * i, 1024) for i in range(2)]
    xsq_t = T32(c_tz + 2048, 1024)
    xn_tm = TBF(c_tz + 3072, 1024)
    h_fm = TBF(c_tz + 3584, 4096)
    qr_tm = TBF(c_tz, 2048)
    qr_tm2 = TBF(c_tz + 1024, 2048)
    kk_tm2 = TBF(c_ow, 2048)
    xnt_b = TBF(c_ow + 1024, 1024)
    pbst = T32(c_tz, 3072)
    lmt_f = T32(c_tz + 3072, 64)
    lmsq = T32(c_tz + 3136, 64)
    feats = TBF(c_fz, 3072)
    t1_t = [T32(c_tz + 5632 + 512 * i, 512) for i in range(2)]
    kk_tm = TBF(c_tz + 0, 2048)
    kkfm = TBF(c_tz + 1024, 2048)
    kny = TBF(c_tz + 2048, 2048)
    kp_all = TBF(c_tz + 3072, 4096)
    qp_t = [TBF(c_tz + 1024 * i, 2048) for i in range(2)]
    qsq_t = [TBF(c_tz + 2048 + 256 * i, 512) for i in range(2)]
    nd_t = [TBF(c_tz + 2560 + 256 * i, 512) for i in range(3)]
    xb2 = [T32(c_tz + 4096, 1024), T32(c_tz + 5632, 1024)]
    xsq2 = T32(c_fz, 1024)
    xn_tm2 = TBF(c_fz + 1024, 1024)
    x2lo = T32(c_xn1, 12288)
    x2hi = T32(c_tz, 4096)

    def x2ap(tc, a, b):
        if tc < 12:
            return x2lo[0:128, tc * 1024 + a: tc * 1024 + b]
        return x2hi[0:128, (tc - 12) * 1024 + a: (tc - 12) * 1024 + b]
    w1b = TBF(c_xn1, 16384)
    w3b = TBF(c_q, 16384)
    eo_all = TBF(c_xn1 + 8192, 8192)
    gs_t = [TBF(c_xn1 + 12288 + 256 * i, 512) for i in range(2)]
    us_t = [TBF(c_xn1 + 12800 + 256 * i, 512) for i in range(2)]
    px_t = [T32(c_xn1 + 13312 + 1024 * i, 1024) for i in range(2)]
    cwrow = TBF(c_xn1 + 15360, 4096)
    cwb = TBF(c_kv, 4096)
    w2b = TBF(c_ow, 16384)
    xn2 = TBF(c_v, 16384)
    logits = TBF(c_fz, 2048)
    rB1 = T32(c_fz + 1024, 128)
    rB2 = T32(c_fz + 1152, 128)
    rB3 = T32(c_fz + 1280, 128)
    cw_tm = TBF(c_tz + 4096, 256)
    cw_fm = TBF(c_tz + 5632, 2048)

    import kernel as _K
    _K.LAYOUT = dict(c_xn1=c_xn1, c_ck=c_ck, c_rw=c_rw, c_pw=c_pw, c_q=c_q, c_v=c_v,
                     c_kv=c_kv, c_ow=c_ow, c_tz=c_tz, c_fz=c_fz)
    em = Em(nc)
    G = Guard()

    def E(stream, body, waits=(), incs=(), dma=False):
        em.op(stream, body, waits, incs, dma)

    s_ld = em.sem("ld"); s_ldp = em.sem("ldp"); s_bar = em.sem("bar")
    s_setP = em.sem("setP"); s_setA = em.sem("setA"); s_setD = em.sem("setD")
    s_setPE = em.sem("setPE")
    s_x = [em.sem(f"x{i}") for i in range(2)]
    s_awc = em.sem("awc"); s_awr = em.sem("awr"); s_awp = em.sem("awp"); s_ow = em.sem("ow")
    s_lnA = em.sem("lnA"); s_lnD = em.sem("lnD")
    s_peC = em.sem("peC"); s_acC = em.sem("acC")
    s_peR = em.sem("peR"); s_acR = em.sem("acR"); s_dvR = em.sem("dvR"); s_acS = em.sem("acS")
    s_peP = em.sem("peP"); s_dvRo = em.sem("dvRo"); s_acV = em.sem("acV")
    s_peT = em.sem("peT"); s_dvT = em.sem("dvT"); s_acT = em.sem("acT")
    s_peN = em.sem("peN"); s_acN = em.sem("acN")
    s_peK = em.sem("peK"); s_acK = em.sem("acK")
    s_peV = em.sem("peV"); s_dvK = em.sem("dvK")
    s_peB = em.sem("peB"); s_acB = em.sem("acB"); s_dvB = em.sem("dvB"); s_poB = em.sem("poB")
    s_x2 = [em.sem(f"xb{i}") for i in range(2)]
    s_peO = em.sem("peO"); s_dvO = em.sem("dvO")
    s_ln2A = em.sem("ln2A")
    s_lg = em.sem("lg"); s_lga = em.sem("lga")
    s_cwD = em.sem("cwD"); s_cwA = em.sem("cwA"); s_cwr = em.sem("cwr")
    s_w1 = [em.sem(f"w1_{i}") for i in range(2)]
    s_w3 = [em.sem(f"w3_{i}") for i in range(2)]
    s_w2 = em.sem("w2s")
    s_peG = em.sem("peG"); s_peU = em.sem("peU")
    s_acF = em.sem("acF"); s_dvF = em.sem("dvF"); s_poF = em.sem("poF")
    s_peY = em.sem("peY"); s_acY = em.sem("acY")
    s_acc = [em.sem(f"acc{i}") for i in range(NTC)]
    s_fl = em.sem("fl")
    s_sv = em.sem("sv"); s_rl = em.sem("rl")

    def psw(b):
        return G.w(("ps", b))

    def pss(b, sem):
        G.s(("ps", b), sem)

    def emit_setup():
        for dst, src in [
            (convb[0:128, 0:24], CB.ap()),
            (bias1[0:128, 0:18], RB.ap()),
            (lmt_f[0:64, 0:64], LMT.ap()),
            (expb[0:8, 0:1], EBT.ap()),
            (pbst[0:1, 0:3072], PB.ap()),
            (sst[0:128, 0:1024], SST.ap()),
            (cct[0:128, 0:1024], CCT.ap()),
        ]:
            E("sp", lambda e, d=dst, s=src: e.dma_start(d, s), incs=[(s_ld, 16)], dma=True)
        nsp = s_ld.n
        for dst, src in [
            (aug_om.ap(0, 64, [[1, 256]]), OM.ap()),
            (om_dup.ap(0, 64, [[1, 256]]), OM.ap()),
            (om_dup.ap(64, 64, [[1, 256]]), OM.ap()),
            (rwt_b[0:128, 0:64], RT.ap()),
            (outbr[0:1, 0:1024], OB.ap()),
        ]:
            E("pool", lambda e, d=dst, s=src: e.dma_start(d, s), incs=[(s_ldp, 16)], dma=True)
        npl = s_ldp.n

        def pool_sets(e):
            e.memset(ones1[0:128, 0:128], 1.0)
            e.memset(onecol[0:128, 0:1], 1.0)
            e.memset(mhalf[0:128, 0:128], -0.5)
            e.memset(aug_om[64:128, 0:256], -0.5)
            e.memset(nylhs[64:128, 0:64], -1.0 / 64.0)
            e.memset(bln16[0:128, 0:1], -LN16)
            e.memset(epsln[0:128, 0:1], 1e-5)
            e.memset(eps6[0:128, 0:1], 1e-6)
            e.memset(eps65[0:64, 0:1], 0.0)
            e.memset(eps65[64:65, 0:1], 1e-6)
            e.memset(m64t[0:128, 0:1], -1.0 / 64.0)
            e.memset(ident[0:128, 0:128], 0.0)
            return e.memset(xn1[0:128, 0:16400], 0.0)
        E("pool", pool_sets, incs=[(s_setP, 1)])
        E("pool", lambda e: e.affine_select(out=ident[0:128, 0:128],
                                            in_=ident[0:128, 0:128],
                                            compare_op=ALU.not_equal, fill=1.0, base=0,
                                            pattern=[[-1, 128]], channel_multiplier=1),
          waits=[(s_setP, s_setP.n)], incs=[(s_setP, 1)])
        E("act", lambda e: e.activation(nylhs[0:64, 0:64], lmt_f[0:64, 0:64], AF.Identity,
                                        scale=2.0 / 64.0),
          waits=[(s_ld, nsp)], incs=[(s_setA, 1)])
        E("dve", lambda e: e.tensor_tensor(lmsq[0:64, 0:64], lmt_f[0:64, 0:64],
                                           lmt_f[0:64, 0:64], ALU.mult),
          waits=[(s_ld, nsp)], incs=[(s_setD, 1)])
        E("pe", lambda e: e.matmul(PS[7][0:64, 0:1], lmsq[0:64, 0:64], m64t[0:64, 0:1],
                                   start=True, stop=True),
          waits=[(s_setP, 1), (s_setD, 1), (s_ldp, npl)], incs=[(s_setPE, 1)])
        E("act", lambda e: e.activation(biasny[0:64, 0:1], PS[7][0:64, 0:1], AF.Identity),
          waits=[(s_setPE, 1)], incs=[(s_setA, 1)])
        pss(7, s_setA)
        E("dve", lambda e: e.tensor_single_scalar(bias2[0:128, 0:18], bias1[0:128, 0:18],
                                                  math.pi / 2, ALU.add),
          waits=[(s_ld, nsp)], incs=[(s_setD, 1)])
        E("dve", lambda e: e.tensor_single_scalar(bias2[0:128, 0:18], bias2[0:128, 0:18],
                                                  SINSHRINK, ALU.mult),
          waits=[(s_setD, s_setD.n)], incs=[(s_setD, 1)])
        E("dve", lambda e: e.tensor_scalar(bias1[0:128, 0:18], bias1[0:128, 0:18],
                                           1.0 / TWO_PI, 0.25, ALU.mult, ALU.add),
          incs=[(s_setD, 1)])
        E("dve", lambda e: e.tensor_single_scalar(pbrow[0:1, 0:3072], pbst[0:1, 0:3072],
                                                  1.0 / CSC, ALU.mult),
          waits=[(s_ldp, npl), (s_ld, nsp)], incs=[(s_setD, 1)])
        em.barrier(s_bar)
        setw = [(s_setP, s_setP.n), (s_setA, s_setA.n), (s_setD, s_setD.n),
                (s_setPE, s_setPE.n), (s_ld, s_ld.n), (s_ldp, s_ldp.n)]
        for st_ in ("pe", "act", "dve", "pool", "sp"):
            E(st_, lambda e: e.nop(), waits=setw)
        G.m[("xtil", 0)] = (s_setD, s_setD.n)
        G.m[("xtil", 1)] = (s_setD, s_setD.n)

    def emit_ln_tc(src_fn, dst_fm, dst_stride, dst_pad, tc, sem_done, xsq=None, xnt=None):
        xnt = xnt if xnt is not None else xn_tm
        dv1 = emit_ln_stats(src_fn, xsq, xnt)
        emit_ln_tp(dst_fm, dst_stride, dst_pad, tc, sem_done, xnt, dv1)

    def emit_ln_stats(src_fn, xsq=None, xnt=None):
        st = small
        xsq = xsq if xsq is not None else xsq_t
        xnt = xnt if xnt is not None else xn_tm
        E("dve", lambda e: e.reduce_sum(st[0:128, 0:1], src_fn(0, 1024), AX.X),
          incs=[(s_lnD, 1)])
        E("dve", lambda e: e.tensor_tensor(xsq[0:128, 0:1024], src_fn(0, 1024),
                                           src_fn(0, 1024), ALU.mult), incs=[(s_lnD, 1)])
        E("dve", lambda e: e.reduce_sum(st[0:128, 1:2], xsq[0:128, 0:1024], AX.X),
          waits=[(s_lnD, s_lnD.n)], incs=[(s_lnD, 1)])
        E("dve", lambda e: e.tensor_scalar(st[0:128, 2:3], st[0:128, 0:1], 1.0 / D, None,
                                           ALU.mult),
          waits=[(s_lnD, s_lnD.n)], incs=[(s_lnD, 1)])
        E("dve", lambda e: e.tensor_tensor(st[0:128, 3:4], st[0:128, 2:3], st[0:128, 2:3],
                                           ALU.mult),
          waits=[(s_lnD, s_lnD.n)], incs=[(s_lnD, 1)])
        E("dve", lambda e: e.scalar_tensor_tensor(st[0:128, 4:5], st[0:128, 1:2], 1.0 / D,
                                                  st[0:128, 3:4], ALU.mult, ALU.subtract),
          waits=[(s_lnD, s_lnD.n)], incs=[(s_lnD, 1)])
        dv0 = s_lnD.n
        E("act", lambda e: e.activation(st[0:128, 5:6], st[0:128, 4:5], AF.Ln,
                                        bias=epsln[0:128, 0:1]),
          waits=[(s_lnD, dv0)], incs=[(s_lnA, 1)])
        E("act", lambda e: e.activation(st[0:128, 6:7], st[0:128, 5:6], AF.Exp, scale=-0.5),
          waits=[(s_lnA, s_lnA.n)], incs=[(s_lnA, 1)])
        ac0 = s_lnA.n
        E("dve", lambda e: e.tensor_scalar(xnt[0:128, 0:1024], src_fn(0, 1024),
                                           st[0:128, 2:3], st[0:128, 6:7], ALU.subtract, ALU.mult),
          waits=[(s_lnA, ac0), G.w(("xntW", id(xnt)))], incs=[(s_lnD, 1)])
        return s_lnD.n

    def emit_ln_tp(dst_fm, dst_stride, dst_pad, tc, sem_done, xnt, dv1):
        for half in range(2):
            bk = 6 + half
            E("pe", lambda e, h=half, p=PSB[bk]: [
                e.transpose(p[0:128, 128 * i:128 * (i + 1)],
                            xnt[0:128, (h * 4 + i) * 128:(h * 4 + i + 1) * 128],
                            ident[0:128, 0:128]) for i in range(4)][-1],
              waits=[(s_lnD, dv1), psw(bk)], incs=[(s_peT, 1)])
            if half == 1:
                G.s(("xntW", id(xnt)), s_peT)
            pt = s_peT.n
            E("act", lambda e, h=half, p=PSB[bk]: e.activation(
                dst_fm.ap(0, 128, [[dst_stride, 4], [1, 128]],
                          off=(h * 4) * dst_stride + dst_pad + tc * 128),
                p[0:128, 0:512], AF.Identity),
              waits=[(s_peT, pt)], incs=[(sem_done, 1)])
            pss(bk, sem_done)

    ln_dv1 = {}

    def emit_ln1_stats(seq, tc):
        par = tc % 2
        E("sp", lambda e, t=tc, p=par: e.dma_start(
            x_til[p][0:128, 0:1024],
            bass.AP(X, (seq * S + t * 128) * D, [[D, 128], [1, D]])),
          waits=[G.w(("xtil", par))], incs=[(s_x[par], 16)], dma=True)
        nx = s_x[par].n
        em.pending["dve"].append((s_x[par], nx))
        ln_dv1[tc] = emit_ln_stats(lambda a, b, p=par: x_til[p][0:128, a:b],
                                   xnt=[xn_tm, xnt_b][par])
        G.s(("xtil", par), s_lnD)

    def emit_ln1_tp(tc):
        emit_ln_tp(xn1, 2050, 1, tc, s_lnA, [xn_tm, xnt_b][tc % 2], ln_dv1[tc])

    def emit_ln1_tc(seq, tc):
        emit_ln1_stats(seq, tc)
        emit_ln1_tp(tc)

    def emit_aw(p):
        E("pool", lambda e, pp=p: e.dma_start(
            ckt.ap(0, 128, [[2048, 3], [1024, 2], [1, 1024]]),
            bass.AP(CK, pp * 3 * 256 * 1024,
                    [[1024, 128], [256 * 1024, 3], [128 * 1024, 2], [1, 1024]])),
          waits=[G.w("ck")], incs=[(s_awc, 16)], dma=True)
        E("pool", lambda e, pp=p: e.dma_start(
            rwt.ap(0, 128, [[2048, 3], [256, 8], [1, 256]]),
            bass.AP(RW, pp * 3 * 1024 * 256,
                    [[256, 128], [1024 * 256, 3], [128 * 256, 8], [1, 256]])),
          waits=[G.w("rw"), G.w("awx")], incs=[(s_awr, 16)], dma=True)
        E("pool", lambda e, pp=p: e.dma_start(
            pwt.ap(0, 128, [[1024, 6], [1, 1024]]),
            bass.AP(PW, pp * 768 * 1024, [[1024, 128], [128 * 1024, 6], [1, 1024]])),
          waits=[G.w("pw"), G.w("awx")], incs=[(s_awp, 16)], dma=True)
        return s_awc.n, s_awr.n, s_awp.n

    def emit_conv_group(p, blk, oc, lnA_need, awc_n):
        bk = oc % 2
        gco = oc // 2
        E("pe", lambda e, oc=oc, g=gco, ps=PS[bk], b=blk: [
            e.matmul(ps[0:128, 0:512],
                     ckt.ap(0, 128, [[1, 128]], off=w * 2048 + cic * 1024 + oc * 128),
                     xn1.ap(0, 128, [[1, 512]], off=(g * 2 + cic) * 2050 + w + b * 512),
                     start=(w == 0 and cic == 0), stop=(w == 2 and cic == 1))
            for w in range(3) for cic in range(2)][-1],
          waits=[(s_awc, awc_n), (s_lnA, lnA_need), G.w("pads"), psw(bk)],
          incs=[(s_peC, 1)])
        pc = s_peC.n
        E("act", lambda e, oc=oc, ps=PS[bk], pp=p: e.activation(
            h_fm[0:128, oc * 512:(oc + 1) * 512], ps[0:128, 0:512], AF.Identity,
            bias=convb[0:128, pp * 8 + oc:pp * 8 + oc + 1]),
          waits=[(s_peC, pc), G.w("hW")], incs=[(s_acC, 1)])
        pss(bk, s_acC)

    def emit_convrff(p, blk, lnA_need, awc_n, awr_n):
        for oc in range(8):
            emit_conv_group(p, blk, oc, lnA_need, awc_n)
        acC_blk = s_acC.n
        emit_rff(p, acC_blk, awr_n)

    def emit_rff(p, acC_blk, awr_n):
        for gi in range(3):
            spc = float(np.float32(math.sqrt(2 * GAMMAS[gi]) / TWO_PI))
            for mc in range(2):
                idx = gi * 2 + mc
                bk = 2 + idx % 2
                t1 = t1_t[idx % 2]
                dd = t1
                E("pe", lambda e, gi=gi, mc=mc, ps=PS[bk]: [
                    e.matmul(ps[0:128, 0:512],
                             rwt.ap(0, 128, [[1, 128]], off=gi * 2048 + dc * 256 + mc * 128),
                             h_fm[0:128, dc * 512:(dc + 1) * 512],
                             start=(dc == 0), stop=(dc == 7)) for dc in range(8)][-1],
                  waits=[(s_awr, awr_n), (s_acC, acC_blk), psw(bk)], incs=[(s_peR, 1)])
                pr = s_peR.n
                bcol = p * 6 + idx
                E("act", lambda e, ps=PS[bk], t1=t1, sc=spc, bc=bcol: e.activation(
                    t1[0:128, 0:512], ps[0:128, 0:512], AF.Identity,
                    bias=bias1[0:128, bc:bc + 1], scale=sc),
                  waits=[(s_peR, pr), G.w(("t1", idx % 2))], incs=[(s_acR, 1)])
                ar = s_acR.n
                E("dve", lambda e, t1=t1: e.tensor_single_scalar(
                    t1[0:128, 0:512], t1[0:128, 0:512], MAGIC, ALU.add),
                  waits=[(s_acR, ar)], incs=[(s_dvR, 1)])
                E("dve", lambda e, t1=t1: e.tensor_single_scalar(
                    t1[0:128, 0:512], t1[0:128, 0:512], MAGIC, ALU.subtract),
                  waits=[(s_dvR, s_dvR.n)], incs=[(s_dvR, 1)])
                E("dve", lambda e, ps=PS[bk], t1=t1, dd=dd, sc=spc: e.scalar_tensor_tensor(
                    dd[0:128, 0:512], ps[0:128, 0:512], sc, t1[0:128, 0:512],
                    ALU.mult, ALU.subtract),
                  waits=[(s_dvR, s_dvR.n)], incs=[(s_dvR, 1)])
                pss(bk, s_dvR)
                G.s(("t1", idx % 2), s_dvR)
                dr = s_dvR.n
                E("act", lambda e, dd=dd, idx=idx, bc=bcol: e.activation(
                    feats[0:128, idx * 512:(idx + 1) * 512], dd[0:128, 0:512], AF.Sin,
                    bias=bias2[0:128, bc:bc + 1], scale=float(TWO_PI * SINSHRINK)),
                  waits=[(s_dvR, dr), G.w("featsW")], incs=[(s_acS, 1)])
                G.s(("t1", idx % 2), s_acS)
        G.s("hW", s_peR)

    def emit_projmm(p, i, awp_n, acs_n, bks=(4, 5)):
        E("pe", lambda e, i=i, p=p, bb=bks: [
            y for half in range(2) for y in (
                [e.matmul(PS[bb[half]][0:128, 0:512],
                          feats[0:128, fc * 512 + i * 128: fc * 512 + (i + 1) * 128],
                          pwt.ap(0, 128, [[1, 512]], off=fc * 1024 + half * 512),
                          start=(fc == 0), stop=False) for fc in range(6)] +
                [e.matmul(PS[bb[half]][0:128, 0:512], ones1[0:1, 0:128],
                          pbrow[0:1, p * 1024 + half * 512: p * 1024 + (half + 1) * 512],
                          start=False, stop=True)])][-1],
          waits=[(s_acS, acs_n), (s_awp, awp_n), psw(bks[0]), psw(bks[1])], incs=[(s_peP, 1)])
        return s_peP.n

    def emit_v_tc(tc, ppn):
        for half in range(2):
            w = [(s_peP, ppn)]
            if tc == 0 and half == 0:
                w.append(G.w("xn2sv"))
            E("act", lambda e, h=half, t=tc: e.activation(
                v_sb[0:128, t * 1024 + h * 512: t * 1024 + (h + 1) * 512],
                PS[4 + h][0:128, 0:512], AF.Identity, scale=CSC),
              waits=w, incs=[(s_acV, 1)])
            pss(4 + half, s_acV)

    def emit_rope(tc, dest, kind, ppn, destW, bks=(4, 5)):
        for half in range(2):
            ps = PS[bks[half]]
            tmp = t1_t[half]
            sstap = sst.ap(0, 128, [[0, 8], [1, 64]], off=tc * 64)
            cctap = cct.ap(0, 128, [[0, 8], [1, 64]], off=tc * 64)
            E("dve", lambda e, ps=ps, tmp=tmp, ap=sstap: e.tensor_tensor(
                tmp[0:128, 0:512], ps[0:128, 0:512], ap, ALU.mult),
              waits=[(s_peP, ppn), G.w(("t1", half))], incs=[(s_dvRo, 1)])
            if kind == "q":
                outap = dest.ap(0, 128, [[1, 512]], off=half * 512)
                outsw = dest.ap(0, 128, [[64, 8], [-32, 2], [1, 32]], off=half * 512 + 32)
            else:
                outap = dest.ap(0, 128, [[128, 8], [1, 64]], off=half * 8 * 128)
                outsw = dest.ap(0, 128, [[128, 8], [-32, 2], [1, 32]], off=half * 8 * 128 + 32)
            E("dve", lambda e, ps=ps, oa=outap, ap=cctap: e.tensor_tensor(
                oa, ps[0:128, 0:512], ap, ALU.mult),
              waits=[G.w(destW) if half == 0 else None], incs=[(s_dvRo, 1)])
            tmpsw = tmp.ap(0, 128, [[64, 8], [-32, 2], [1, 32]], off=32)
            E("dve", lambda e, osw=outsw, tsw=tmpsw: e.tensor_tensor(osw, osw, tsw, ALU.add),
              waits=[(s_dvRo, s_dvRo.n)], incs=[(s_dvRo, 1)])
            G.s(("t1", half), s_dvRo)
            pss(bks[half], s_dvRo)

    def emit_q_tc(tc, ppn):
        emit_rope(tc, qr_tm, "q", ppn, "qrW")
        dro = s_dvRo.n
        E("pe", lambda e: [
            e.transpose(PSB[6][0:128, dc * 128:(dc + 1) * 128],
                        qr_tm[0:128, dc * 128:(dc + 1) * 128], ident[0:128, 0:128])
            for dc in range(8)][-1],
          waits=[(s_dvRo, dro), psw(6)], incs=[(s_peT, 1)])
        G.s("qrW", s_peT)
        pt = s_peT.n
        E("dve", lambda e, t=tc: e.tensor_copy(
            q_fm.ap(0, 128, [[2048, 8], [1, 128]], off=t * 128), PSB[6][0:128, 0:1024]),
          waits=[(s_peT, pt)], incs=[(s_dvT, 1)])
        pss(6, s_dvT)

    rope_done = {}

    def emit_q_rope(tc, ppn):
        par = tc % 2
        emit_rope(tc, [qr_tm, qr_tm2][par], "q", ppn, ("qrW", par))
        rope_done[tc] = s_dvRo.n

    def emit_q_tp(tc):
        par = tc % 2
        bk = 6 + par
        qrb = [qr_tm, qr_tm2][par]
        E("pe", lambda e, b=bk, q=qrb: [
            e.transpose(PSB[b][0:128, dc * 128:(dc + 1) * 128],
                        q[0:128, dc * 128:(dc + 1) * 128], ident[0:128, 0:128])
            for dc in range(8)][-1],
          waits=[(s_dvRo, rope_done[tc]), psw(bk)], incs=[(s_peT, 1)])
        G.s(("qrW", par), s_peT)
        pt = s_peT.n
        E("act", lambda e, t=tc, b=bk: e.activation(
            q_fm.ap(0, 128, [[2048, 8], [1, 128]], off=t * 128),
            PSB[b][0:128, 0:1024], AF.Identity),
          waits=[(s_peT, pt)], incs=[(s_acT, 1)])
        pss(bk, s_acT)

    def emit_k_rope(tc, ppn, bks=(4, 5)):
        par = tc % 2
        kkb = [kk_tm, kk_tm2][par]
        emit_rope(tc, kkb, "k", ppn, ("kkW", par), bks)
        kin = kkb.ap(0, 128, [[128, 16], [1, 64]], off=0)
        kout = kkb.ap(0, 128, [[128, 16], [1, 64]], off=64)
        E("dve", lambda e, ki=kin, ko=kout: e.tensor_tensor(ko, ki, ki, ALU.mult),
          waits=[(s_dvRo, s_dvRo.n)], incs=[(s_dvRo, 1)])
        rope_done[tc] = s_dvRo.n

    def emit_k_rest(tc):
        par = tc % 2
        kkb = [kk_tm, kk_tm2][par]
        tbks = (4, 5) if par == 0 else (2, 3)
        dro = rope_done[tc]
        dvt_h = []
        for half in range(2):
            bk = tbks[half]
            E("pe", lambda e, h=half, p=PSB[bk], kb=kkb: [
                e.transpose(p[0:128, j * 128:(j + 1) * 128],
                            kb[0:128, (h * 8 + j) * 128:(h * 8 + j + 1) * 128],
                            ident[0:128, 0:128]) for j in range(8)][-1],
              waits=[(s_dvRo, dro), psw(bk)], incs=[(s_peT, 1)])
            pt = s_peT.n
            E("dve", lambda e, h=half, p=PSB[bk]: e.tensor_copy(
                kkfm[0:128, h * 1024:(h + 1) * 1024], p[0:128, 0:1024]),
              waits=[(s_peT, pt), G.w("kkfmW") if half == 0 else None],
              incs=[(s_dvT, 1)])
            dvt_h.append(s_dvT.n)
            pss(bk, s_dvT)
        G.s(("kkW", par), s_peT)
        dvt1 = s_dvT.n
        for grp in range(4):
            bk = 6 + grp % 2
            E("pe", lambda e, g=grp, ps=PS[bk]: [
                e.matmul(ps[0:64, j * 128:(j + 1) * 128], nylhs[0:128, 0:64],
                         kkfm[0:128, (g * 4 + j) * 128:(g * 4 + j + 1) * 128],
                         start=True, stop=True) for j in range(4)][-1],
              waits=[(s_dvT, dvt_h[0] if grp < 2 else dvt1), psw(bk)], incs=[(s_peN, 1)])
            pn = s_peN.n
            E("act", lambda e, g=grp, ps=PS[bk]: e.activation(
                kny.ap(0, 64, [[1, 512]], off=g * 512), ps[0:64, 0:512], AF.Exp,
                bias=biasny[0:64, 0:1]),
              waits=[(s_peN, pn), G.w("knyW")], incs=[(s_acN, 1)])
            pss(bk, s_acN)
        G.s("kkfmW", s_peN)
        acn1 = s_acN.n
        E("dve", lambda e: e.tensor_tensor(kny.ap(64, 64, [[1, 2048]]),
                                           kny.ap(0, 64, [[1, 2048]]),
                                           kny.ap(0, 64, [[1, 2048]]), ALU.mult),
          waits=[(s_acN, acn1)], incs=[(s_dvT, 1)])
        dvt2 = s_dvT.n
        for pr in range(8):
            bk = tbks[pr % 2]
            E("pe", lambda e, pr=pr, ps=PS[bk]: [
                e.matmul(ps[0:128, j * 256:(j + 1) * 256],
                         kny[0:128, (pr * 2 + j) * 128:(pr * 2 + j + 1) * 128],
                         aug_om[0:128, 0:256], start=True, stop=True) for j in range(2)][-1],
              waits=[(s_dvT, dvt2), psw(bk)], incs=[(s_peK, 1)])
            pk = s_peK.n
            E("act", lambda e, pr=pr, ps=PS[bk]: e.activation(
                kp_all[0:128, pr * 512:(pr + 1) * 512], ps[0:128, 0:512], AF.Exp,
                bias=bln16[0:128, 0:1]),
              waits=[(s_peK, pk), G.w("kpW")], incs=[(s_acK, 1)])
            pss(bk, s_acK)
        G.s("knyW", s_peK)
        ack = s_acK.n
        acv = s_acV.n
        E("pe", lambda e, t=tc: [
            e.matmul(PS[(h * 2 + ec) // 8][0:128,
                     ((h * 2 + ec) % 8) * 64:((h * 2 + ec) % 8) * 64 + 64],
                     kp_all[0:128, h * 256 + ec * 128: h * 256 + (ec + 1) * 128],
                     v_sb[0:128, t * 1024 + h * 64: t * 1024 + (h + 1) * 64],
                     start=True, stop=True)
            for h in range(16) for ec in range(2)][-1],
          waits=[(s_acK, ack), (s_acV, acv), psw(0), psw(1), psw(2), psw(3)],
          incs=[(s_peV, 1)])
        E("pe", lambda e: [
            e.matmul(PS[6][0:128, sl:sl + 1],
                     kp_all[0:128, (sl // 2) * 256 + (sl % 2) * 128:
                            (sl // 2) * 256 + (sl % 2 + 1) * 128],
                     onecol[0:128, 0:1], start=True, stop=True) for sl in range(32)][-1],
          waits=[psw(6)], incs=[(s_peV, 1)])
        G.s("kpW", s_peV)
        pv = s_peV.n
        first = (tc == 0)
        for b_ in range(4):
            if first:
                E("dve", lambda e, b=b_: e.tensor_copy(kvacc[0:128, b * 512:(b + 1) * 512],
                                                       PS[b][0:128, 0:512]),
                  waits=[(s_peV, pv)])
            else:
                E("dve", lambda e, b=b_: e.tensor_tensor(kvacc[0:128, b * 512:(b + 1) * 512],
                                                         kvacc[0:128, b * 512:(b + 1) * 512],
                                                         PS[b][0:128, 0:512], ALU.add),
                  waits=[(s_peV, pv)])
        if first:
            E("dve", lambda e: e.tensor_copy(ksum[0:128, 0:32], PS[6][0:128, 0:32]),
              incs=[(s_dvK, 1)])
        else:
            E("dve", lambda e: e.tensor_tensor(ksum[0:128, 0:32], ksum[0:128, 0:32],
                                               PS[6][0:128, 0:32], ALU.add), incs=[(s_dvK, 1)])
        for b_ in range(4):
            pss(b_, s_dvK)
        pss(6, s_dvK)

    def emit_k_tc(tc, ppn):
        emit_k_rope(tc, ppn)
        emit_k_rest(tc)

    def emit_attn(seq):
        ln_done = {}
        dvo_done = {}
        if seq > 0:
            E("pool", lambda e: [
                e.memset(xn1.ap(0, 128, [[2050, 8], [1, 1]], off=0), 0.0),
                e.memset(xn1.ap(0, 128, [[2050, 8], [1, 1]], off=2049), 0.0)][-1],
              incs=[(s_setP, 1)])
        G.m["pads"] = (s_setP, s_setP.n)
        awc_n, awr_n, awp_n = emit_aw(2)
        for tc in range(4):
            emit_ln1_tc(seq, tc)
            ln_done[tc] = s_lnA.n
        for p_i, (p, kind) in enumerate([(2, "v"), (0, "q"), (1, "k")][:_DBG_NP]):
            if p_i > 0:
                awc_n, awr_n, awp_n = emit_aw(p)
            if kind in ("v", "q") and _DBG_NBLK == 4 and _DBG_SERIAL == 0:
                if p_i == 0:
                    for tc in range(4, 8):
                        emit_ln1_tc(seq, tc)
                        ln_done[tc] = s_lnA.n
                for oc in range(8):
                    emit_conv_group(p, 0, oc, ln_done[4], awc_n)
                conv_done = s_acC.n
                ln_q = []
                ln_sq = []
                for blk in range(4):
                    emit_rff(p, conv_done, awr_n)
                    if p_i == 0 and blk < 2:
                        t0 = 4 * blk + 8
                        emit_ln1_stats(seq, t0)
                        emit_ln1_stats(seq, t0 + 1)
                        emit_ln1_tp(t0)
                        ln_done[t0] = s_lnA.n
                        ln_sq = [t0 + 2, t0 + 3]
                        ln_q = [t0 + 1, t0 + 2, t0 + 3]
                    if blk + 1 < 4:
                        lnn = ln_done[min(4 * (blk + 1) + 4, NTC - 1)]
                        emit_conv_group(p, blk + 1, 0, lnn, awc_n)
                        emit_conv_group(p, blk + 1, 1, lnn, awc_n)
                    if kind == "q" and blk > 0:
                        emit_q_tp(blk * 4 - 1)
                    acs = s_acS.n
                    for i in range(4):
                        tc = blk * 4 + i
                        ppn = emit_projmm(p, i, awp_n, acs)
                        if kind == "v":
                            emit_v_tc(tc, ppn)
                        else:
                            emit_q_rope(tc, ppn)
                        if ln_sq:
                            emit_ln1_stats(seq, ln_sq.pop(0))
                        if blk + 1 < 4 and i < 3:
                            lnn = ln_done[min(4 * (blk + 1) + 4, NTC - 1)]
                            emit_conv_group(p, blk + 1, 2 + 2 * i, lnn, awc_n)
                            emit_conv_group(p, blk + 1, 3 + 2 * i, lnn, awc_n)
                        if ln_q:
                            t_ = ln_q.pop(0)
                            emit_ln1_tp(t_)
                            ln_done[t_] = s_lnA.n
                        if kind == "q" and i > 0:
                            emit_q_tp(tc - 1)
                    if blk + 1 < 4:
                        conv_done = s_acC.n
                    G.s("featsW", s_peP)
                if kind == "q":
                    emit_q_tp(NTC - 1)
            elif kind == "k" and _DBG_NBLK == 4 and _DBG_SERIAL == 0:
                for blk in range(4):
                    emit_convrff(p, blk, ln_done[min(4 * blk + 4, NTC - 1)],
                                 awc_n, awr_n)
                    if blk > 0:
                        emit_k_rest(blk * 4 - 1)
                    acs = s_acS.n
                    for i in range(4):
                        tc = blk * 4 + i
                        bks = (4, 5) if i % 2 == 0 else (2, 3)
                        ppn = emit_projmm(p, i, awp_n, acs, bks)
                        emit_k_rope(tc, ppn, bks)
                        if i > 0:
                            emit_k_rest(tc - 1)
                    G.s("featsW", s_peP)
                emit_k_rest(NTC - 1)
            else:
                for blk in range(_DBG_NBLK):
                    if p_i == 0:
                        for tc in range(4 * blk + 4, min(4 * blk + 8, NTC)):
                            emit_ln1_tc(seq, tc)
                            ln_done[tc] = s_lnA.n
                    lnA_need = ln_done[min(4 * blk + 4, NTC - 1)]
                    emit_convrff(p, blk, lnA_need, awc_n, awr_n)
                    if _DBG_SERIAL in (1, 2):
                        em.barrier(s_bar)
                    acs = s_acS.n
                    for i in range(4):
                        tc = blk * 4 + i
                        ppn = emit_projmm(p, i, awp_n, acs)
                        if kind == "v":
                            emit_v_tc(tc, ppn)
                        elif kind == "q":
                            emit_q_tc(tc, ppn)
                        else:
                            emit_k_tc(tc, ppn)
                        if _DBG_SERIAL in (1, 3):
                            em.barrier(s_bar)
                    G.s("featsW", s_peP)
            if kind == "v":
                G.s("ck", s_peC)
                G.s("rw", s_peR)
                G.s("pw", s_peP)
            elif kind == "q":
                G.s("ck", s_peC)
                G.s("rw", s_peR)
                G.s("pw", s_peP)
        E("pool", lambda e: e.dma_start(
            outw_t.ap(0, 128, [[1024, 8], [1, 1024]]),
            bass.AP(OW, 0, [[1024, 128], [128 * 1024, 8], [1, 1024]])),
          waits=[(s_peT, s_peT.n)], incs=[(s_ow, 16)], dma=True)
        ow_n = s_ow.n
        if _DBG_NP < 3:
            return
        E("dve", lambda e: e.tensor_copy(
            kvks.ap(0, 128, [[65, 32], [1, 64]], off=0),
            kvacc.ap(0, 128, [[64, 32], [1, 64]], off=0)),
          waits=[(s_dvK, s_dvK.n), G.w("cwsv")], incs=[(s_dvB, 1)])
        E("dve", lambda e: e.tensor_copy(
            kvks.ap(0, 128, [[65, 32], [1, 1]], off=64),
            ksum.ap(0, 128, [[1, 32]], off=0)), incs=[(s_dvB, 1)])
        kvready = s_dvB.n
        pev_all = s_peV.n
        attnpre_done = {}
        iters = [(h, tt) for h in range(NH) for tt in range(4)]
        NIT = len(iters)
        st_fav = {}
        st_nd = {}
        st_rbc = {}
        st_exp = {}
        st_evac = {}
        st_recip = {}
        st_mult = {}
        st_qsq = {}

        def q_slice(i):
            h, tt = iters[i]
            return q_fm.ap((h % 2) * 64, 64, [[1, 512]], off=(h // 2) * 2048 + tt * 512)

        qf_ready = s_acT.n

        def emit_qsq(i):
            par = i % 2
            bb = (iters[i][0] % 2) * 64
            w = ([(s_peB, st_fav[i - 2])] if i - 2 in st_fav
                 else [(s_peV, pev_all), (s_acT, qf_ready)])
            E("pool", lambda e, qs=q_slice(i), p=par, b=bb: e.tensor_tensor(
                qsq_t[p].ap(b, 64, [[1, 512]]), qs, qs, ALU.mult), waits=w, incs=[(s_poB, 1)])
            st_qsq[i] = s_poB.n

        def emit_fav(i):
            par = i % 2
            h, tt = iters[i]
            bb = (h % 2) * 64
            w = [(s_poB, st_qsq[i])]
            if i - 2 in st_exp:
                w.append((s_acB, st_exp[i - 2]))
            else:
                w += [psw(2 * par), psw(2 * par + 1)]
            E("pe", lambda e, qs=q_slice(i), b=bb, p=par: [
                y for ec in range(2) for y in (
                    e.matmul(PS[2 * p + ec][0:128, 0:512],
                             om_dup.ap(b, 64, [[1, 128]], off=ec * 128), qs,
                             start=True, stop=False),
                    e.matmul(PS[2 * p + ec][0:128, 0:512], mhalf.ap(b, 64, [[1, 128]]),
                             qsq_t[p].ap(b, 64, [[1, 512]]), start=False, stop=True))][-1],
              waits=w, incs=[(s_peB, 1)])
            st_fav[i] = s_peB.n

        def emit_exp(i):
            par = i % 2
            w = [(s_peB, st_fav[i])]
            if i - 2 in st_nd:
                w.append((s_peB, st_nd[i - 2]))
            for ec in range(2):
                E("act", lambda e, ec=ec, p=par: e.activation(
                    qp_t[p][0:128, ec * 512:(ec + 1) * 512], PS[2 * p + ec][0:128, 0:512],
                    AF.Exp, bias=bln16[0:128, 0:1]),
                  waits=w if ec == 0 else [], incs=[(s_acB, 1)])
            st_exp[i] = s_acB.n

        def emit_nd(i):
            par = i % 2
            h, tt = iters[i]
            w = [(s_acB, st_exp[i]), (s_dvB, kvready)]
            if i - 2 in st_evac:
                w.append((s_acB, st_evac[i - 2]))
            else:
                w.append(psw(4 + par))
            E("pe", lambda e, hh=h, p=par: [
                e.matmul(PS[4 + p][0:65, 0:512], kvks[0:128, (2 * hh) * 65:(2 * hh + 1) * 65],
                         qp_t[p][0:128, 0:512], start=True, stop=False),
                e.matmul(PS[4 + p][0:65, 0:512], kvks[0:128, (2 * hh + 1) * 65:(2 * hh + 2) * 65],
                         qp_t[p][0:128, 512:1024], start=False, stop=True)][-1],
              waits=w, incs=[(s_peB, 1)])
            st_nd[i] = s_peB.n

        def emit_evac(i):
            p3 = i % 3
            par = i % 2
            w = [(s_peB, st_nd[i])]
            if i - 3 in st_mult:
                w.append((s_dvB, st_mult[i - 3]))
            if i - 3 in st_recip:
                w.append((s_dvB, st_recip[i - 3]))
            E("act", lambda e, p=p3, pp=par: e.activation(
                nd_t[p][0:65, 0:512], PS[4 + pp][0:65, 0:512], AF.Identity,
                bias=eps65[0:65, 0:1]), waits=w, incs=[(s_acB, 1)])
            st_evac[i] = s_acB.n

        def emit_recip(i):
            p3 = i % 3
            w = [(s_acB, st_evac[i])]
            def _recip(e, p=p3):
                with nc.allow_low_precision(reason="bf16 denom reciprocal (operand-level rounding)"):
                    return e.reciprocal(nd_t[p].ap(64, 1, [[1, 512]]),
                                        nd_t[p].ap(64, 1, [[1, 512]]))
            E("dve", _recip, waits=w, incs=[(s_dvB, 1)])
            st_recip[i] = s_dvB.n

        def emit_rbc(i):
            p3 = i % 3
            par = i % 2
            w = [(s_dvB, st_recip[i])]
            if i - 2 in st_mult:
                w.append((s_dvB, st_mult[i - 2]))
            else:
                w.append(psw(6 + par))
            E("pe", lambda e, p=p3, pp=par: e.matmul(
                PS[6 + pp][0:64, 0:512], ones1.ap(64, 1, [[1, 64]]),
                nd_t[p].ap(64, 1, [[1, 512]]), start=True, stop=True),
              waits=w, incs=[(s_peB, 1)])
            st_rbc[i] = s_peB.n

        def emit_mult(i):
            p3 = i % 3
            par = i % 2
            h, tt = iters[i]
            bb = (h % 2) * 64
            dc = h // 2
            E("dve", lambda e, p=p3, pp=par, b=bb, d=dc, t=tt: e.tensor_tensor(
                attnpre.ap(b, 64, [[1, 512]], off=d * 2048 + t * 512),
                nd_t[p].ap(0, 64, [[1, 512]]), PS[6 + pp][0:64, 0:512], ALU.mult),
              waits=[(s_peB, st_rbc[i])], incs=[(s_dvB, 1)])
            st_mult[i] = s_dvB.n
            h2, tt2 = iters[i]
            if h2 == NH - 1:
                attnpre_done[tt2] = s_dvB.n

        emit_qsq(0)
        for i in range(NIT):
            if i + 1 < NIT:
                emit_qsq(i + 1)
            emit_fav(i)
            emit_exp(i)
            if i - 2 >= 0:
                emit_rbc(i - 2)
                emit_mult(i - 2)
            if i - 1 >= 0:
                emit_nd(i - 1)
                emit_evac(i - 1)
                emit_recip(i - 1)
        emit_nd(NIT - 1)
        emit_evac(NIT - 1)
        emit_recip(NIT - 1)
        for i_ in (NIT - 2, NIT - 1):
            emit_rbc(i_)
            emit_mult(i_)
        for b_ in range(6):
            G.m[("ps", b_)] = (s_acB, s_acB.n)
        for b_ in (6, 7):
            G.m[("ps", b_)] = (s_dvB, s_dvB.n)
        G.s(("xb2", 0), s_peB)
        G.s(("xb2", 1), s_peB)
        for tc in range(NTC):
            par = tc % 2
            if tc >= 2:
                emit_ln2_tc(seq, tc - 2, dvo_done[tc - 2])
            E("sp", lambda e, t=tc, pq=par: e.dma_start(
                xb2[pq][0:128, 0:1024],
                bass.AP(X, (seq * S + t * 128) * D, [[D, 128], [1, D]])),
              waits=[G.w(("xb2", par)), G.w("cwsv") if tc < 2 else None],
              incs=[(s_x2[par], 16)], dma=True)
            nx = s_x2[par].n
            bks = (0, 1) if tc % 2 == 0 else (2, 3)
            E("pe", lambda e, t=tc, pp=(PS[bks[0]], PS[bks[1]]): [
                [e.matmul(pp[half][0:128, 0:512],
                          attnpre[0:128, dc2 * 2048 + t * 128: dc2 * 2048 + (t + 1) * 128],
                          outw_t.ap(0, 128, [[1, 512]], off=dc2 * 1024 + half * 512),
                          start=(dc2 == 0), stop=False)
                 for half in range(2) for dc2 in range(8)],
                e.matmul(pp[0][0:128, 0:512], ones1[0:1, 0:128], outbr[0:1, 0:512],
                         start=False, stop=True),
                e.matmul(pp[1][0:128, 0:512], ones1[0:1, 0:128], outbr[0:1, 512:1024],
                         start=False, stop=True)][-1],
              waits=[(s_dvB, attnpre_done[tc // 4]), (s_ow, ow_n), psw(bks[0]), psw(bks[1])],
              incs=[(s_peO, 1)])
            po = s_peO.n
            E("dve", lambda e, t=tc, pq=par, pp=(PS[bks[0]], PS[bks[1]]): [
                e.tensor_tensor(x2ap(t, 0, 512),
                                xb2[pq][0:128, 0:512], pp[0][0:128, 0:512], ALU.add),
                e.tensor_tensor(x2ap(t, 512, 1024),
                                xb2[pq][0:128, 512:1024], pp[1][0:128, 0:512], ALU.add)][-1],
              waits=[(s_peO, po), (s_x2[par], nx)], incs=[(s_dvO, 1)])
            pss(bks[0], s_dvO)
            pss(bks[1], s_dvO)
            G.s(("xb2", par), s_dvO)
            dvo_done[tc] = s_dvO.n
        emit_ln2_tc(seq, NTC - 2, dvo_done[NTC - 2])
        emit_ln2_tc(seq, NTC - 1, dvo_done[NTC - 1])
        E("pool", lambda e: e.nop(),
          waits=[(s_acc[t2], s_acc[t2].n) for t2 in range(NTC)], incs=[(s_fl, 1)])

    def emit_ln2_tc(seq, tc, dvo_wait):
        em.pending["dve"].append((s_dvO, dvo_wait))
        emit_ln_tc(lambda a, b, t=tc: x2ap(t, a, b), xn2, 2048, 0, tc, s_ln2A,
                   xsq=xsq2, xnt=xn_tm2)
        dvl = s_lnD.n
        E("pool", lambda e, t=tc: e.dma_start(
            bass.AP(OUT, (seq * S + t * 128) * D, [[D, 128], [1, D]]),
            x2ap(t, 0, 1024)),
          waits=[(s_lnD, dvl)], incs=[(s_acc[tc], 16)], dma=True)

    def emit_router():
        for tt in range(4):
            bk = tt % 2
            E("pe", lambda e, t=tt, ps=PS[bk]: [
                e.matmul(ps[0:8, 0:512], rwt_b[0:128, dc * 8:(dc + 1) * 8],
                         xn2.ap(0, 128, [[1, 512]], off=dc * 2048 + t * 512),
                         start=(dc == 0), stop=(dc == 7)) for dc in range(8)][-1],
              waits=[(s_ln2A, s_ln2A.n), psw(bk)], incs=[(s_lg, 1)])
            pl = s_lg.n
            E("act", lambda e, t=tt, ps=PS[bk]: e.activation(
                logits.ap(0, 8, [[1, 512]], off=t * 512), ps[0:8, 0:512], AF.Identity,
                bias=expb[0:8, 0:1]),
              waits=[(s_lg, pl)], incs=[(s_lga, 1)])
            pss(bk, s_lga)
        lgall = s_lga.n
        E("pe", lambda e: [
            e.transpose(PSB[2][0:128, t * 8:(t + 1) * 8],
                        logits.ap(0, 8, [[1, 128]], off=t * 128), ident[0:8, 0:8])
            for t in range(16)][-1],
          waits=[(s_lga, lgall), psw(2)], incs=[(s_lg, 1)])
        pl = s_lg.n
        E("act", lambda e: e.activation(rB1[0:128, 0:128], PSB[2][0:128, 0:128],
                                        AF.Identity),
          waits=[(s_lg, pl)], incs=[(s_cwA, 1)])
        pss(2, s_cwA)
        v8 = lambda t_: t_.ap(0, 128, [[8, 16], [1, 8]])
        bc = lambda off: rsc.ap(0, 128, [[1, 16], [0, 8]], off=off)
        E("dve", lambda e: e.tensor_reduce(rsc[0:128, 0:16], v8(rB1), AX.X, ALU.max),
          waits=[(s_cwA, s_cwA.n)], incs=[(s_cwD, 1)])
        E("dve", lambda e: e.tensor_tensor(rB2[0:128, 0:128], rB1[0:128, 0:128],
                                           bc(0), ALU.subtract),
          waits=[(s_cwD, s_cwD.n)], incs=[(s_cwD, 1)])
        E("act", lambda e: e.activation(rB2[0:128, 0:128], rB2[0:128, 0:128], AF.Exp),
          waits=[(s_cwD, s_cwD.n)], incs=[(s_cwA, 1)])
        E("dve", lambda e: e.tensor_single_scalar(rB3[0:128, 0:128], rB2[0:128, 0:128],
                                                  1.0, ALU.is_ge),
          waits=[(s_cwA, s_cwA.n)], incs=[(s_cwD, 1)])
        E("dve", lambda e: e.tensor_tensor(rB1[0:128, 0:128], rB2[0:128, 0:128],
                                           rB3[0:128, 0:128], ALU.subtract),
          waits=[(s_cwD, s_cwD.n)], incs=[(s_cwD, 1)])
        E("dve", lambda e: e.tensor_reduce(rsc[0:128, 16:32], v8(rB1), AX.X, ALU.max),
          waits=[(s_cwD, s_cwD.n)], incs=[(s_cwD, 1)])
        E("dve", lambda e: e.tensor_tensor(rB1[0:128, 0:128], rB2[0:128, 0:128],
                                           bc(16), ALU.is_ge),
          waits=[(s_cwD, s_cwD.n)], incs=[(s_cwD, 1)])
        E("dve", lambda e: e.tensor_tensor(rB3[0:128, 0:128], rB2[0:128, 0:128],
                                           rB1[0:128, 0:128], ALU.mult),
          waits=[(s_cwD, s_cwD.n)], incs=[(s_cwD, 1)])
        E("dve", lambda e: e.tensor_single_scalar(rsc[0:128, 32:48], rsc[0:128, 16:32],
                                                  1.0, ALU.add),
          waits=[(s_cwD, s_cwD.n)], incs=[(s_cwD, 1)])
        E("dve", lambda e: e.reciprocal(rsc[0:128, 0:16], rsc[0:128, 32:48]),
          waits=[(s_cwD, s_cwD.n)], incs=[(s_cwD, 1)])
        E("dve", lambda e: e.tensor_tensor(cw_tm[0:128, 0:128], rB3[0:128, 0:128],
                                           bc(0), ALU.mult),
          waits=[(s_cwD, s_cwD.n)], incs=[(s_cwD, 1)])
        dcw = s_cwD.n
        for tc in range(NTC):
            bk2 = 4 + tc % 2
            E("pe", lambda e, t=tc, ps=PSB[bk2]: e.transpose(
                ps[0:8, 0:128], cw_tm.ap(0, 128, [[1, 8]], off=t * 8), ident[0:128, 0:128]),
              waits=[(s_cwD, dcw) if tc < 2 else None, psw(bk2)], incs=[(s_lg, 1)])
            pt2 = s_lg.n
            E("act", lambda e, t=tc, ps=PSB[bk2]: e.activation(
                cw_fm.ap(0, 8, [[1, 128]], off=t * 128), ps[0:8, 0:128], AF.Identity),
              waits=[(s_lg, pt2), G.w("cwsv") if tc == 0 else None], incs=[(s_cwA, 1)])
            pss(bk2, s_cwA)

    def emit_experts(passes, fl_all):
        experts = [(SW1, SW3, SW2, 0, False)] + [(W1, W3, W2, e, True) for e in range(8)]
        if _DBG_NE is not None:
            experts = experts[:_DBG_NE]
        sems_w = {}
        npass = len(passes)
        nfl = fl_all // 1

        def emit_w13_chunk(ei, ch, wait_w1, wait_w3):
            TW1, TW3, _, eidx, _ = experts[ei]
            E("pool", lambda e, tw=TW1, ex=eidx, c=ch: e.dma_start(
                w1b.ap(0, 128, [[2048, 8], [1, 1024]], off=c * 1024),
                bass.AP(tw, ex * 1024 * 2048 + c * 1024,
                        [[2048, 128], [128 * 2048, 8], [1, 1024]])),
              waits=[wait_w1, (s_fl, fl_all)], incs=[(s_w1[ch], 16)], dma=True)
            E("pool", lambda e, tw=TW3, ex=eidx, c=ch: e.dma_start(
                w3b.ap(0, 128, [[2048, 8], [1, 1024]], off=c * 1024),
                bass.AP(tw, ex * 1024 * 2048 + c * 1024,
                        [[2048, 128], [128 * 2048, 8], [1, 1024]])),
              waits=[wait_w3, (s_fl, fl_all)], incs=[(s_w3[ch], 16)], dma=True)
            a, b, c2, d2 = sems_w.setdefault(ei, ([0, 0], [0, 0], 0, 0))
            a[ch] = s_w1[ch].n
            b[ch] = s_w3[ch].n

        def emit_w2(ei, wait_w2):
            _, _, TW2, eidx, routed = experts[ei]
            E("pool", lambda e, t2w=TW2, ex=eidx: e.dma_start(
                w2b.ap(0, 128, [[1024, 16], [1, 1024]]),
                bass.AP(t2w, ex * 2048 * 1024, [[1024, 128], [128 * 1024, 16], [1, 1024]])),
              waits=[wait_w2, (s_ln2A, s_ln2A.n), (s_cwA, s_cwA.n), (s_fl, fl_all)],
              incs=[(s_w2, 16)], dma=True)
            a, b, c2, d2 = sems_w.setdefault(ei, ([0, 0], [0, 0], 0, 0))
            sems_w[ei] = (a, b, s_w2.n, d2)

        cwr_n = {}

        def emit_cwrow(ei, P):
            _, _, _, eidx, routed = experts[ei]
            if routed:
                cpar = P["pi"]
                E("sp", lambda e, ex=eidx, cp=cpar, src=P["cw"]: e.dma_start(
                    cwrow.ap(0, 1, [[1, 2048]], off=cp * 2048),
                    src.ap(ex, 1, [[1, 2048]])),
                  waits=[P["cw_wait"], G.w(("cwrowW", cpar))],
                  incs=[(s_cwr, 16)], dma=True)
            cwr_n[(ei, P["pi"])] = s_cwr.n

        def emit_cwb(ei, P):
            _, _, _, eidx, routed = experts[ei]
            if not routed:
                return
            cpar = P["pi"]
            ncr = cwr_n[(ei, cpar)]
            for t2 in range(4):
                E("pe", lambda e, t=t2, cp=cpar: e.matmul(
                    PS[t % 2][0:128, 0:512], ones1[0:1, 0:128],
                    cwrow.ap(0, 1, [[1, 512]], off=cp * 2048 + t * 512),
                    start=True, stop=True),
                  waits=[(s_cwr, ncr), psw(t2 % 2)], incs=[(s_peG, 1)])
                if t2 == 3:
                    G.s(("cwrowW", cpar), s_peG)
                pgc = s_peG.n
                E("act", lambda e, t=t2, cp=cpar: e.activation(
                    cwb.ap(0, 128, [[1, 512]], off=cp * 2048 + t * 512),
                    PS[t % 2][0:128, 0:512], AF.Identity),
                  waits=[(s_peG, pgc), G.w(("cwbW", cpar)) if t2 == 0 else None],
                  incs=[(s_acF, 1)])
                pss(t2 % 2, s_acF)

        emit_w13_chunk(0, 0, G.w(("w1c", 0)), G.w(("w3c", 0)))
        emit_w13_chunk(0, 1, G.w(("w1c", 1)), G.w(("w3c", 1)))
        emit_w2(0, G.w("w2"))
        for P in passes:
            emit_cwrow(0, P)
        for ei, (TW1, TW3, TW2, eidx, routed) in enumerate(experts):
            nw1c, nw3c, nw2, _ = sems_w[ei]
            for pi_, P in enumerate(passes):
                last = pi_ == npass - 1
                cpar = P["pi"]
                peG0, peU0, peY0 = s_peG.n, s_peU.n, s_peY.n
                emit_cwb(ei, P)
                if ei + 1 < len(experts):
                    emit_cwrow(ei + 1, P)
                cwb_ready = s_acF.n
                for tt in range(4 if _DBG_NTT is None else _DBG_NTT):
                    for hc in range(16):
                        if last and tt == 3 and hc == 8 and ei + 1 < len(experts):
                            nc4 = 4 if routed else 0
                            emit_w13_chunk(ei + 1, 0, (s_peG, peG0 + nc4 + 56),
                                           (s_peU, peU0 + 56))
                        gbk = hc % 2
                        ubk = 2 + hc % 2
                        xw = P["xw"] if (tt == 0 and hc == 0) else []
                        E("pe", lambda e, hcv=hc, t=tt, ps=PS[gbk], xnt=P["xnt"]: [
                            e.matmul(ps[0:128, 0:512],
                                     w1b.ap(0, 128, [[1, 128]], off=dc * 2048 + hcv * 128),
                                     xnt.ap(0, 128, [[1, 512]], off=dc * 2048 + t * 512),
                                     start=(dc == 0), stop=(dc == 7)) for dc in range(8)][-1],
                          waits=[(s_w1[hc // 8], nw1c[hc // 8]), psw(gbk)] + xw,
                          incs=[(s_peG, 1)])
                        pg = s_peG.n
                        if last and tt == 3 and hc == 7:
                            G.s(("w1c", 0), s_peG)
                        if last and tt == 3 and hc == 15:
                            G.s(("w1c", 1), s_peG)
                        E("pe", lambda e, hcv=hc, t=tt, ps=PS[ubk], xnt=P["xnt"]: [
                            e.matmul(ps[0:128, 0:512],
                                     w3b.ap(0, 128, [[1, 128]], off=dc * 2048 + hcv * 128),
                                     xnt.ap(0, 128, [[1, 512]], off=dc * 2048 + t * 512),
                                     start=(dc == 0), stop=(dc == 7)) for dc in range(8)][-1],
                          waits=[(s_w3[hc // 8], nw3c[hc // 8]), psw(ubk)] + xw,
                          incs=[(s_peU, 1)])
                        pu = s_peU.n
                        if last and tt == 3 and hc == 7:
                            G.s(("w3c", 0), s_peU)
                        if last and tt == 3 and hc == 15:
                            G.s(("w3c", 1), s_peU)
                        E("act", lambda e, hcv=hc, ps=PS[gbk]: e.activation(
                            gs_t[hcv % 2][0:128, 0:512], ps[0:128, 0:512], AF.Silu),
                          waits=[(s_peG, pg), G.w(("gsW", hc % 2))], incs=[(s_acF, 1)])
                        pss(gbk, s_acF)
                        if routed:
                            E("dve", lambda e, hcv=hc, ps=PS[ubk], t=tt, cp=cpar: e.scalar_tensor_tensor(
                                us_t[hcv % 2][0:128, 0:512], ps[0:128, 0:512], 1.0,
                                cwb.ap(0, 128, [[1, 512]], off=cp * 2048 + t * 512),
                                ALU.mult, ALU.mult),
                              waits=[(s_peU, pu), (s_acF, cwb_ready), G.w(("usW", hc % 2))],
                              incs=[(s_dvF, 1)])
                        else:
                            E("dve", lambda e, hcv=hc, ps=PS[ubk]: e.tensor_copy(
                                us_t[hcv % 2][0:128, 0:512], ps[0:128, 0:512]),
                              waits=[(s_peU, pu), G.w(("usW", hc % 2))], incs=[(s_dvF, 1)])
                        pss(ubk, s_dvF)
                        af, df = s_acF.n, s_dvF.n
                        E("pool", lambda e, hcv=hc: e.tensor_tensor(
                            eo_all[0:128, hcv * 512:(hcv + 1) * 512],
                            gs_t[hcv % 2][0:128, 0:512], us_t[hcv % 2][0:128, 0:512], ALU.mult),
                          waits=[(s_acF, af), (s_dvF, df), G.w(("eoW", hc))], incs=[(s_poF, 1)])
                        G.s(("gsW", hc % 2), s_poF)
                        G.s(("usW", hc % 2), s_poF)
                        if hc == 7:
                            po7 = s_poF.n
                    if last and tt == 3 and ei + 1 < len(experts):
                        nc4 = 4 if routed else 0
                        emit_w13_chunk(ei + 1, 1, (s_peG, peG0 + nc4 + 64),
                                       (s_peU, peU0 + 64))
                    po = s_poF.n
                    for i in range(4):
                        tc = tt * 4 + i
                        ybk = (4, 5) if i % 2 == 0 else (6, 7)
                        E("pe", lambda e, t=tc, ii=i, pp=(PS[ybk[0]], PS[ybk[1]]): [
                            e.matmul(pp[half][0:128, 0:512],
                                     eo_all[0:128, hc2 * 512 + ii * 128: hc2 * 512 + (ii + 1) * 128],
                                     w2b.ap(0, 128, [[1, 512]], off=hc2 * 1024 + half * 512),
                                     start=(hc2 == 0), stop=False)
                            for half in range(2) for hc2 in range(8)][-1],
                          waits=[(s_poF, po7), (s_w2, nw2), psw(ybk[0]), psw(ybk[1])],
                          incs=[(s_peY, 1)])
                        E("pe", lambda e, t=tc, ii=i, pp=(PS[ybk[0]], PS[ybk[1]]): [
                            e.matmul(pp[half][0:128, 0:512],
                                     eo_all[0:128, hc2 * 512 + ii * 128: hc2 * 512 + (ii + 1) * 128],
                                     w2b.ap(0, 128, [[1, 512]], off=hc2 * 1024 + half * 512),
                                     start=False, stop=(hc2 == 15))
                            for half in range(2) for hc2 in range(8, 16)][-1],
                          waits=[(s_poF, po)], incs=[(s_peY, 1)])
                        py = s_peY.n
                        E("act", lambda e, pp=(PS[ybk[0]], PS[ybk[1]]), px=px_t[i % 2]: [
                            e.activation(px[0:128, 0:512], pp[0][0:128, 0:512], AF.Identity),
                            e.activation(px[0:128, 512:1024], pp[1][0:128, 0:512],
                                         AF.Identity)][-1],
                          waits=[(s_peY, py), G.w(("pxW", i % 2))], incs=[(s_acY, 1)])
                        pss(ybk[0], s_acY)
                        pss(ybk[1], s_acY)
                        ay = s_acY.n
                        accv = 16 * (nfl + npass * ei + pi_)
                        E("pool", lambda e, t=tc, px=px_t[i % 2], sq=P["sq"]: e.dma_start(
                            bass.AP(OUT, (sq * S + t * 128) * D, [[D, 128], [1, D]]),
                            px[0:128, 0:1024], accum_op=ALU.add),
                          waits=[(s_acY, ay), (s_acc[tc], accv)], incs=[(s_acc[tc], 16)], dma=True)
                        G.m[("pxW", i % 2)] = (s_acc[tc], s_acc[tc].n)
                    for hc in range(16):
                        G.m[("eoW", hc)] = (s_peY, s_peY.n)
                if routed:
                    G.s(("cwbW", cpar), s_dvF)
                ncwb = 4 if routed else 0
                assert s_peG.n == peG0 + ncwb + 64, (s_peG.n, peG0, ncwb)
                assert s_peU.n == peU0 + 64
                assert s_peY.n == peY0 + 32
            G.s("w2", s_peY)
            if ei + 1 < len(experts):
                emit_w2(ei + 1, (s_peY, peY0 + 32))
        E("pool", lambda e: e.nop(),
          waits=[(s_acc[t2], s_acc[t2].n) for t2 in range(NTC)], incs=[(s_fl, 1)])
        done = (s_fl, s_fl.n)
        for key in ("awx", "rw", "pw", ("xntW", id(xnt_b)),
                    ("kkW", 0), ("kkW", 1), ("qrW", 0), ("qrW", 1),
                    ("xtil", 0), ("xtil", 1)):
            G.m[key] = done
        G.m["ck"] = (s_peY, s_peY.n)

    emit_setup()
    if mode == "ffn1":
        s_dbg = em.sem("dbg")
        s_dbgp = em.sem("dbgp")
        if _DBG_PRE >= 1:
            E("sp", lambda e: e.dma_start(xn2[0:128, 0:16384], dbg["xn2_in"].ap()),
              incs=[(s_dbg, 16)], dma=True)
            E("pool", lambda e: e.dma_start(cw_fm[0:8, 0:2048], dbg["cw_in"].ap()),
              incs=[(s_dbgp, 16)], dma=True)
        for tc in range(NTC if _DBG_PRE >= 2 else 0):
            E("sp", lambda e, t=tc: e.dma_start(
                px_t[0][0:128, 0:1024], dbg["x2_in"].ap()[0:128, t * 1024:(t + 1) * 1024]),
              waits=[(s_acc[tc - 1], s_acc[tc - 1].n)] if tc > 0 else [],
              incs=[(s_dbg, 16)], dma=True)
            nd = s_dbg.n
            E("pool", lambda e, t=tc: e.dma_start(
                bass.AP(OUT, (t * 128) * D, [[D, 128], [1, D]]), px_t[0][0:128, 0:1024]),
              waits=[(s_dbg, nd)], incs=[(s_acc[tc], 16)], dma=True)
        if _DBG_PRE >= 2:
            E("pool", lambda e: e.nop(),
              waits=[(s_acc[t2], s_acc[t2].n) for t2 in range(NTC)], incs=[(s_fl, 1)])
        E("sp", lambda e: e.nop(), waits=[(s_fl, s_fl.n), (s_dbgp, s_dbgp.n)])
        em.barrier(s_bar)
        emit_experts([dict(pi=0, sq=0, xnt=xn2, cw=cw_fm,
                           cw_wait=(s_cwA, s_cwA.n), xw=[])], s_fl.n)
        em.barrier(s_bar)
        E("sp", lambda e: e.nop(), waits=[(s_acc[i], s_acc[i].n) for i in range(NTC)])
    elif mode == "attn1":
        emit_attn(0)
        dvo = s_dvO.n
        for tc in range(NTC):
            E("sp", lambda e, t=tc: e.dma_start(
                bass.AP(dbg["x2_out"], t * 1024, [[16384, 128], [1, 1024]]),
                x2ap(t, 0, 1024)),
              waits=[(s_dvO, dvo)], incs=[(s_fl, 16)], dma=True)
        allw = [(s, s.n) for s in em.sems if s.n > 0]
        for sc, v in allw:
            em.pending["sp"].append((sc, v))
        E("sp", lambda e: e.dma_start(dbg["arena_out"].ap(), a32h[:, :]),
          incs=[(s_fl, 16)], dma=True)
        E("sp", lambda e: e.nop(), waits=[(s_fl, s_fl.n)])
    else:
        for seq in range(BPC):
            emit_attn(seq)
            emit_router()
            emit_experts([dict(pi=0, sq=seq, xnt=xn2, cw=cw_fm,
                               cw_wait=(s_cwA, s_cwA.n), xw=[])], s_fl.n)
            em.barrier(s_bar)
        E("sp", lambda e: e.nop(), waits=[(s_acc[i], s_acc[i].n) for i in range(NTC)])

    em.finalize(es)
    es.close()
    return nc


def _rope_tables():
    inv = 1.0 / 10000.0 ** (np.arange(0, HD, 2, dtype=np.float32) / HD)
    f = np.arange(S, dtype=np.float32)[:, None] * inv[None, :]
    sn = (np.sin(f) * CSC).astype(np.float32)
    cs = (np.cos(f) * CSC).astype(np.float32)
    sstab = np.zeros((128, NTC * 64), np.float32)
    cctab = np.zeros((128, NTC * 64), np.float32)
    for tc in range(NTC):
        t0 = tc * 128
        sstab[:, tc * 64:tc * 64 + 32] = sn[t0:t0 + 128, :]
        sstab[:, tc * 64 + 32:tc * 64 + 64] = -sn[t0:t0 + 128, :]
        cctab[:, tc * 64:tc * 64 + 32] = cs[t0:t0 + 128, :]
        cctab[:, tc * 64 + 32:tc * 64 + 64] = cs[t0:t0 + 128, :]
    return (np.ascontiguousarray(sstab.astype(ml_dtypes.bfloat16)),
            np.ascontiguousarray(cctab.astype(ml_dtypes.bfloat16)))


def host_inputs(inp):
    f32c = lambda a: np.ascontiguousarray(np.asarray(a, dtype=np.float32))
    cb = f32c(np.asarray(inp["L0_attn_conv_b"]).reshape(3, 8, 128).transpose(2, 0, 1)
              .reshape(128, 24))
    rb = f32c(np.asarray(inp["L0_attn_rff_b"]).reshape(3, 3, 2, 128).transpose(3, 0, 1, 2)
              .reshape(128, 18))
    pb = f32c(np.asarray(inp["L0_attn_proj_b"]).reshape(1, 3072))
    ob = f32c(np.asarray(inp["L0_attn_out_b"]).reshape(1, 1024))
    lmT = f32c(np.asarray(inp["L0_attn_landmarks"]).T)
    rwT = f32c(np.asarray(inp["L0_ffn_router_w"]).T.reshape(8, 128, 8).transpose(1, 0, 2)
               .reshape(128, 64))
    eb = f32c(np.asarray(inp["L0_ffn_experts_bias"]).reshape(8, 1))
    sstab, cctab = _rope_tables()
    return {
        "conv_k": f32c(inp["L0_attn_conv_k"]), "conv_b": cb,
        "rff_w": f32c(inp["L0_attn_rff_w"]), "rff_b": rb,
        "proj_w": f32c(inp["L0_attn_proj_w"]), "proj_b": pb,
        "omega": f32c(inp["L0_attn_omega"]), "lmT": lmT,
        "out_w": f32c(inp["L0_attn_out_w"]), "out_b": ob,
        "rwT": rwT, "expb": eb,
        "w1": f32c(inp["L0_ffn_w1"]), "w3": f32c(inp["L0_ffn_w3"]),
        "w2": f32c(inp["L0_ffn_w2"]),
        "sw1": f32c(inp["L0_ffn_sw1"]), "sw3": f32c(inp["L0_ffn_sw3"]),
        "sw2": f32c(inp["L0_ffn_sw2"]),
        "sstab": sstab, "cctab": cctab,
    }


def kernel(**inputs):
    base = host_inputs(inputs)
    x = np.asarray(inputs["L0_attn_x"], dtype=np.float32)
    nc = build_program("full")
    in_maps = []
    for c in range(NCORES):
        m = dict(base)
        m["x"] = np.ascontiguousarray(x[c * BPC:(c + 1) * BPC])
        in_maps.append(m)
    res = run_bass_kernel_spmd(nc, in_maps, core_ids=list(range(NCORES)))
    return np.concatenate([r["out"] for r in res.results], axis=0)


# version 22
# speedup vs baseline: 1.1858x; 1.0616x over previous
import math
import numpy as np
import ml_dtypes
import concourse.bass as bass
import concourse.mybir as mybir
from concourse.bass_utils import run_bass_kernel_spmd

F32 = mybir.dt.float32
BF16 = mybir.dt.bfloat16
AF = mybir.ActivationFunctionType
ALU = mybir.AluOpType
AX = mybir.AxisListType

B, S, D = 16, 2048, 1024
NCORES = 8
BPC = B // NCORES
NH, HD = 16, 64
NTC = 16
GAMMAS = (0.5, 1.0, 2.0)
CSC = float(np.float32(math.sqrt(2.0 / 256.0)))
LN16 = float(np.float32(math.log(16.0)))
MAGIC = 12582912.0
TWO_PI = float(np.float32(2 * math.pi))
SINSHRINK = float(np.float32(1.0 - 4e-6))
AWID = 52000
_DBG_NE = None
_DBG_NTT = None
_DBG_PRE = 2
_DBG_NP = 3
_DBG_NBLK = 4
_DBG_SERIAL = 0


class SemC:
    __slots__ = ("name", "h", "n", "per")

    def __init__(self, name):
        self.name, self.h, self.n = name, None, 0


class Em:
    STREAMS = ("pe", "act", "dve", "pool", "sp")

    def __init__(self, nc):
        self.nc = nc
        self.ops = {s: [] for s in self.STREAMS}
        self.sems = []
        self.dirty = {s: False for s in self.STREAMS}
        self.pending = {s: [] for s in self.STREAMS}

    def sem(self, name):
        s = SemC(name)
        self.sems.append(s)
        return s

    def op(self, stream, body, waits=(), incs=(), dma=False):
        w = self.pending[stream] + [x for x in waits if x is not None]
        self.pending[stream] = []
        for s, n in incs:
            s.n += n
        self.ops[stream].append([w, body, list(incs), dma])
        self.dirty[stream] = True

    def barrier(self, bar):
        if not hasattr(bar, "per"):
            bar.per = {s: self.sem(f"{bar.name}_{s}") for s in self.STREAMS}
        edges = []
        for s in self.STREAMS:
            if self.ops[s]:
                bs = bar.per[s]
                self.ops[s].append([[], lambda e: e.nop(), [(bs, 1)], False])
                bs.n += 1
            edges.append((bar.per[s], bar.per[s].n))
        for s in self.STREAMS:
            for e_ in edges:
                if e_[0] is not bar.per[s]:
                    self.pending[s].append(e_)

    def n_ops(self):
        return sum(len(v) for v in self.ops.values())

    def finalize(self, es):
        for s in self.sems:
            s.h = es.enter_context(self.nc.semaphore(s.name))
        with self.nc.Block() as blk:
            def runner(stream):
                def run(eng):
                    for waits, body, incs, _dma in self.ops[stream]:
                        for sc, v in waits:
                            if v > 0:
                                eng.wait_ge(sc.h, v)
                        inst = body(eng)
                        for sc, n in incs:
                            inst.then_inc(sc.h, n)
                return run
            blk.tensor(runner("pe"))
            blk.scalar(runner("act"))
            blk.vector(runner("dve"))
            blk.gpsimd(runner("pool"))
            blk.sync(runner("sp"))


class Guard:
    def __init__(self):
        self.m = {}

    def w(self, key):
        return self.m.get(key)

    def s(self, key, sem):
        self.m[key] = (sem, sem.n)


class Tile:
    __slots__ = ("h", "wid", "c0", "cols")

    def __init__(self, h, wid, c0, cols):
        self.h, self.wid, self.c0, self.cols = h, wid, c0, cols

    def ap(self, p0=0, np_=128, fdims=None, off=0):
        base = p0 * self.wid + self.c0 + off
        if fdims is None:
            fdims = [[1, self.cols]]
        return bass.AP(self.h, base, [[self.wid, np_]] + [list(d) for d in fdims])

    def __getitem__(self, idx):
        ps, cs = idx
        p0 = ps.start or 0
        p1 = ps.stop if ps.stop is not None else 128
        c0 = cs.start or 0
        c1 = cs.stop if cs.stop is not None else self.cols
        return self.ap(p0, p1 - p0, [[1, c1 - c0]], off=c0)


def build_program(mode="full"):
    nc = bass.Bass("TRN2", target_bir_lowering=False)

    X = nc.dram_tensor("x", [BPC, S, D], F32, kind="ExternalInput")
    CK = nc.dram_tensor("conv_k", [3, 3, 256, 1024], F32, kind="ExternalInput")
    CB = nc.dram_tensor("conv_b", [128, 24], F32, kind="ExternalInput")
    RW = nc.dram_tensor("rff_w", [3, 3, 1024, 256], F32, kind="ExternalInput")
    RB = nc.dram_tensor("rff_b", [128, 18], F32, kind="ExternalInput")
    PW = nc.dram_tensor("proj_w", [3, 768, 1024], F32, kind="ExternalInput")
    PB = nc.dram_tensor("proj_b", [1, 3072], F32, kind="ExternalInput")
    OM = nc.dram_tensor("omega", [64, 256], F32, kind="ExternalInput")
    LMT = nc.dram_tensor("lmT", [64, 64], F32, kind="ExternalInput")
    OW = nc.dram_tensor("out_w", [1024, 1024], F32, kind="ExternalInput")
    OB = nc.dram_tensor("out_b", [1, 1024], F32, kind="ExternalInput")
    RT = nc.dram_tensor("rwT", [128, 64], F32, kind="ExternalInput")
    EBT = nc.dram_tensor("expb", [8, 1], F32, kind="ExternalInput")
    W1 = nc.dram_tensor("w1", [8, 1024, 2048], F32, kind="ExternalInput")
    W3 = nc.dram_tensor("w3", [8, 1024, 2048], F32, kind="ExternalInput")
    W2 = nc.dram_tensor("w2", [8, 2048, 1024], F32, kind="ExternalInput")
    SW1 = nc.dram_tensor("sw1", [1, 1024, 2048], F32, kind="ExternalInput")
    SW3 = nc.dram_tensor("sw3", [1, 1024, 2048], F32, kind="ExternalInput")
    SW2 = nc.dram_tensor("sw2", [1, 2048, 1024], F32, kind="ExternalInput")
    SST = nc.dram_tensor("sstab", [128, 1024], BF16, kind="ExternalInput")
    CCT = nc.dram_tensor("cctab", [128, 1024], BF16, kind="ExternalInput")
    OUT = nc.dram_tensor("out", [BPC, S, D], F32, kind="ExternalOutput")
    dbg = {}
    if mode == "attn1":
        dbg["x2_out"] = nc.dram_tensor("x2_out", [128, 16384], F32, kind="ExternalOutput")
        dbg["arena_out"] = nc.dram_tensor("arena_out", [128, AWID], F32, kind="ExternalOutput")
    if mode == "ffn1":
        dbg["xn2_in"] = nc.dram_tensor("xn2_in", [128, 16384], BF16, kind="ExternalInput")
        dbg["cw_in"] = nc.dram_tensor("cw_in", [8, 2048], F32, kind="ExternalInput")
        dbg["x2_in"] = nc.dram_tensor("x2_in", [128, 16384], F32, kind="ExternalInput")

    from contextlib import ExitStack
    es = ExitStack()
    a32h = es.enter_context(nc.sbuf_tensor("arena", [128, AWID], F32))
    abfh = a32h.bitcast(BF16)
    PS = [es.enter_context(nc.psum_tensor(f"ps{i}", [128, 512], F32)) for i in range(8)]
    PSB = [p.bitcast(BF16) for p in PS]

    def T32(c0, cols):
        return Tile(a32h, AWID, c0, cols)

    def TBF(c0, cols):
        return Tile(abfh, 2 * AWID, 2 * c0, cols)

    off_ = [0]
    def alloc(n):
        c = off_[0]
        off_[0] += n
        assert off_[0] <= AWID, off_[0]
        return c
    sst = TBF(alloc(512), 1024)
    cct = TBF(alloc(512), 1024)
    ident = TBF(alloc(64), 128)
    aug_om = TBF(alloc(128), 256)
    om_dup = TBF(alloc(128), 256)
    mhalf = TBF(alloc(64), 128)
    nylhs = TBF(alloc(32), 64)
    biasny = T32(alloc(1), 1)
    bias1 = T32(alloc(18), 18)
    bias2 = T32(alloc(18), 18)
    convb = T32(alloc(24), 24)
    rwt_b = TBF(alloc(32), 64)
    expb = T32(alloc(1), 1)
    pbrow = TBF(alloc(1536), 3072)
    outbr = TBF(alloc(512), 1024)
    ones1 = TBF(alloc(64), 128)
    onecol = TBF(alloc(1), 1)
    bln16 = T32(alloc(1), 1)
    epsln = T32(alloc(1), 1)
    eps6 = T32(alloc(1), 1)
    m64t = T32(alloc(1), 1)
    small = T32(alloc(16), 16)
    rsc = T32(alloc(64), 64)
    eps65 = Tile(rsc.h, rsc.wid, rsc.c0 + 48, 1)
    c_xn1 = alloc(8200)
    c_ck = alloc(3072)
    c_rw = alloc(3072)
    c_pw = alloc(3072)
    c_q = alloc(8192)
    c_v = alloc(8192)
    c_kv = alloc(2080)
    c_ow = alloc(4096)
    c_tz = alloc(6656)
    c_fz = alloc(1536)
    assert off_[0] <= AWID, off_[0]

    xn1 = TBF(c_xn1, 16400)
    ckt = TBF(c_ck, 6144)
    rwt = TBF(c_rw, 6144)
    pwt = TBF(c_pw, 6144)
    q_fm = TBF(c_q, 16384)
    v_sb = TBF(c_v, 16384)
    attnpre = TBF(c_v, 16384)
    kvacc = T32(c_kv, 2048)
    ksum = T32(c_kv + 2048, 32)
    outw_t = TBF(c_ow, 8192)
    kvks = TBF(c_tz + 5120, 2080)
    x_til = [T32(c_tz + 1024 * i, 1024) for i in range(2)]
    xsq_t = T32(c_tz + 2048, 1024)
    xn_tm = TBF(c_tz + 3072, 1024)
    h_fm = TBF(c_tz + 3584, 4096)
    qr_tm = TBF(c_tz, 2048)
    qr_tm2 = TBF(c_tz + 1024, 2048)
    kk_tm2 = TBF(c_ow, 2048)
    xnt_b = TBF(c_ow + 1024, 1024)
    pbst = T32(c_tz, 3072)
    lmt_f = T32(c_tz + 3072, 64)
    lmsq = T32(c_tz + 3136, 64)
    feats = TBF(c_fz, 3072)
    t1_t = [T32(c_tz + 5632 + 512 * i, 512) for i in range(2)]
    kk_tm = TBF(c_tz + 0, 2048)
    kkfm = TBF(c_tz + 1024, 2048)
    kny = TBF(c_tz + 2048, 2048)
    kp_all = TBF(c_tz + 3072, 4096)
    qp_t = [TBF(c_tz + 1024 * i, 2048) for i in range(2)]
    qsq_t = [TBF(c_tz + 2048 + 256 * i, 512) for i in range(2)]
    nd_t = [TBF(c_tz + 2560 + 256 * i, 512) for i in range(3)]
    xb2 = [T32(c_tz + 4096, 1024), T32(c_tz + 5632, 1024)]
    xsq2 = T32(c_fz, 1024)
    xn_tm2 = TBF(c_fz + 1024, 1024)
    x2lo = T32(c_xn1, 12288)
    x2hi = T32(c_tz, 4096)

    def x2ap(tc, a, b):
        if tc < 12:
            return x2lo[0:128, tc * 1024 + a: tc * 1024 + b]
        return x2hi[0:128, (tc - 12) * 1024 + a: (tc - 12) * 1024 + b]
    w1b = TBF(c_xn1, 16384)
    w3b = TBF(c_q, 16384)
    eo_all = TBF(c_xn1 + 8192, 8192)
    gs_t = [TBF(c_xn1 + 12288 + 256 * i, 512) for i in range(2)]
    us_t = [TBF(c_xn1 + 12800 + 256 * i, 512) for i in range(2)]
    px_t = [T32(c_xn1 + 13312 + 1024 * i, 1024) for i in range(2)]
    cwrow = TBF(c_xn1 + 15360, 4096)
    cwb = TBF(c_kv, 4096)
    w2b = TBF(c_ow, 16384)
    xn2 = TBF(c_v, 16384)
    logits = TBF(c_fz, 2048)
    rB1 = T32(c_fz + 1024, 128)
    rB2 = T32(c_fz + 1152, 128)
    rB3 = T32(c_fz + 1280, 128)
    cw_tm = TBF(c_tz + 4096, 256)
    cw_fm = TBF(c_tz + 5632, 2048)

    import kernel as _K
    _K.LAYOUT = dict(c_xn1=c_xn1, c_ck=c_ck, c_rw=c_rw, c_pw=c_pw, c_q=c_q, c_v=c_v,
                     c_kv=c_kv, c_ow=c_ow, c_tz=c_tz, c_fz=c_fz)
    em = Em(nc)
    G = Guard()

    def E(stream, body, waits=(), incs=(), dma=False):
        em.op(stream, body, waits, incs, dma)

    s_ld = em.sem("ld"); s_ldp = em.sem("ldp"); s_bar = em.sem("bar")
    s_setP = em.sem("setP"); s_setA = em.sem("setA"); s_setD = em.sem("setD")
    s_setPE = em.sem("setPE")
    s_x = [em.sem(f"x{i}") for i in range(2)]
    s_awc = em.sem("awc"); s_awr = em.sem("awr"); s_awp = em.sem("awp"); s_ow = em.sem("ow")
    s_lnA = em.sem("lnA"); s_lnD = em.sem("lnD")
    s_peC = em.sem("peC"); s_acC = em.sem("acC")
    s_peR = em.sem("peR"); s_acR = em.sem("acR"); s_dvR = em.sem("dvR"); s_acS = em.sem("acS")
    s_peP = em.sem("peP"); s_dvRo = em.sem("dvRo"); s_acV = em.sem("acV")
    s_peT = em.sem("peT"); s_dvT = em.sem("dvT"); s_acT = em.sem("acT")
    s_peN = em.sem("peN"); s_acN = em.sem("acN")
    s_peK = em.sem("peK"); s_acK = em.sem("acK")
    s_peV = em.sem("peV"); s_dvK = em.sem("dvK")
    s_peB = em.sem("peB"); s_acB = em.sem("acB"); s_dvB = em.sem("dvB"); s_poB = em.sem("poB")
    s_x2 = [em.sem(f"xb{i}") for i in range(2)]
    s_peO = em.sem("peO"); s_dvO = em.sem("dvO")
    s_ln2A = em.sem("ln2A")
    s_lg = em.sem("lg"); s_lga = em.sem("lga")
    s_cwD = em.sem("cwD"); s_cwA = em.sem("cwA"); s_cwr = em.sem("cwr")
    s_w1 = [em.sem(f"w1_{i}") for i in range(2)]
    s_w3 = [em.sem(f"w3_{i}") for i in range(2)]
    s_w2 = em.sem("w2s")
    s_peG = em.sem("peG"); s_peU = em.sem("peU")
    s_acF = em.sem("acF"); s_dvF = em.sem("dvF"); s_poF = em.sem("poF")
    s_peY = em.sem("peY"); s_acY = em.sem("acY")
    s_acc = [em.sem(f"acc{i}") for i in range(NTC)]
    s_fl = em.sem("fl")
    s_sv = em.sem("sv"); s_rl = em.sem("rl")

    def psw(b):
        return G.w(("ps", b))

    def pss(b, sem):
        G.s(("ps", b), sem)

    def emit_setup():
        for dst, src in [
            (convb[0:128, 0:24], CB.ap()),
            (bias1[0:128, 0:18], RB.ap()),
            (lmt_f[0:64, 0:64], LMT.ap()),
            (expb[0:8, 0:1], EBT.ap()),
            (pbst[0:1, 0:3072], PB.ap()),
            (sst[0:128, 0:1024], SST.ap()),
            (cct[0:128, 0:1024], CCT.ap()),
        ]:
            E("sp", lambda e, d=dst, s=src: e.dma_start(d, s), incs=[(s_ld, 16)], dma=True)
        nsp = s_ld.n
        for dst, src in [
            (aug_om.ap(0, 64, [[1, 256]]), OM.ap()),
            (om_dup.ap(0, 64, [[1, 256]]), OM.ap()),
            (om_dup.ap(64, 64, [[1, 256]]), OM.ap()),
            (rwt_b[0:128, 0:64], RT.ap()),
            (outbr[0:1, 0:1024], OB.ap()),
        ]:
            E("pool", lambda e, d=dst, s=src: e.dma_start(d, s), incs=[(s_ldp, 16)], dma=True)
        npl = s_ldp.n

        def pool_sets(e):
            e.memset(ones1[0:128, 0:128], 1.0)
            e.memset(onecol[0:128, 0:1], 1.0)
            e.memset(mhalf[0:128, 0:128], -0.5)
            e.memset(aug_om[64:128, 0:256], -0.5)
            e.memset(nylhs[64:128, 0:64], -1.0 / 64.0)
            e.memset(bln16[0:128, 0:1], -LN16)
            e.memset(epsln[0:128, 0:1], 1e-5)
            e.memset(eps6[0:128, 0:1], 1e-6)
            e.memset(eps65[0:64, 0:1], 0.0)
            e.memset(eps65[64:65, 0:1], 1e-6)
            e.memset(m64t[0:128, 0:1], -1.0 / 64.0)
            e.memset(ident[0:128, 0:128], 0.0)
            return e.memset(xn1[0:128, 0:16400], 0.0)
        E("pool", pool_sets, incs=[(s_setP, 1)])
        E("pool", lambda e: e.affine_select(out=ident[0:128, 0:128],
                                            in_=ident[0:128, 0:128],
                                            compare_op=ALU.not_equal, fill=1.0, base=0,
                                            pattern=[[-1, 128]], channel_multiplier=1),
          waits=[(s_setP, s_setP.n)], incs=[(s_setP, 1)])
        E("act", lambda e: e.activation(nylhs[0:64, 0:64], lmt_f[0:64, 0:64], AF.Identity,
                                        scale=2.0 / 64.0),
          waits=[(s_ld, nsp)], incs=[(s_setA, 1)])
        E("dve", lambda e: e.tensor_tensor(lmsq[0:64, 0:64], lmt_f[0:64, 0:64],
                                           lmt_f[0:64, 0:64], ALU.mult),
          waits=[(s_ld, nsp)], incs=[(s_setD, 1)])
        E("pe", lambda e: e.matmul(PS[7][0:64, 0:1], lmsq[0:64, 0:64], m64t[0:64, 0:1],
                                   start=True, stop=True),
          waits=[(s_setP, 1), (s_setD, 1), (s_ldp, npl)], incs=[(s_setPE, 1)])
        E("act", lambda e: e.activation(biasny[0:64, 0:1], PS[7][0:64, 0:1], AF.Identity),
          waits=[(s_setPE, 1)], incs=[(s_setA, 1)])
        pss(7, s_setA)
        E("dve", lambda e: e.tensor_single_scalar(bias2[0:128, 0:18], bias1[0:128, 0:18],
                                                  math.pi / 2, ALU.add),
          waits=[(s_ld, nsp)], incs=[(s_setD, 1)])
        E("dve", lambda e: e.tensor_single_scalar(bias2[0:128, 0:18], bias2[0:128, 0:18],
                                                  SINSHRINK, ALU.mult),
          waits=[(s_setD, s_setD.n)], incs=[(s_setD, 1)])
        E("dve", lambda e: e.tensor_scalar(bias1[0:128, 0:18], bias1[0:128, 0:18],
                                           1.0 / TWO_PI, 0.25, ALU.mult, ALU.add),
          incs=[(s_setD, 1)])
        E("dve", lambda e: e.tensor_single_scalar(pbrow[0:1, 0:3072], pbst[0:1, 0:3072],
                                                  1.0 / CSC, ALU.mult),
          waits=[(s_ldp, npl), (s_ld, nsp)], incs=[(s_setD, 1)])
        em.barrier(s_bar)
        setw = [(s_setP, s_setP.n), (s_setA, s_setA.n), (s_setD, s_setD.n),
                (s_setPE, s_setPE.n), (s_ld, s_ld.n), (s_ldp, s_ldp.n)]
        for st_ in ("pe", "act", "dve", "pool", "sp"):
            E(st_, lambda e: e.nop(), waits=setw)
        G.m[("xtil", 0)] = (s_setD, s_setD.n)
        G.m[("xtil", 1)] = (s_setD, s_setD.n)

    def emit_ln_tc(src_fn, dst_fm, dst_stride, dst_pad, tc, sem_done, xsq=None, xnt=None):
        xnt = xnt if xnt is not None else xn_tm
        dv1 = emit_ln_stats(src_fn, xsq, xnt)
        emit_ln_tp(dst_fm, dst_stride, dst_pad, tc, sem_done, xnt, dv1)

    def emit_ln_stats(src_fn, xsq=None, xnt=None):
        st = small
        xsq = xsq if xsq is not None else xsq_t
        xnt = xnt if xnt is not None else xn_tm
        E("dve", lambda e: e.reduce_sum(st[0:128, 0:1], src_fn(0, 1024), AX.X),
          incs=[(s_lnD, 1)])
        E("dve", lambda e: e.tensor_tensor(xsq[0:128, 0:1024], src_fn(0, 1024),
                                           src_fn(0, 1024), ALU.mult), incs=[(s_lnD, 1)])
        E("dve", lambda e: e.reduce_sum(st[0:128, 1:2], xsq[0:128, 0:1024], AX.X),
          waits=[(s_lnD, s_lnD.n)], incs=[(s_lnD, 1)])
        E("dve", lambda e: e.tensor_scalar(st[0:128, 2:3], st[0:128, 0:1], 1.0 / D, None,
                                           ALU.mult),
          waits=[(s_lnD, s_lnD.n)], incs=[(s_lnD, 1)])
        E("dve", lambda e: e.tensor_tensor(st[0:128, 3:4], st[0:128, 2:3], st[0:128, 2:3],
                                           ALU.mult),
          waits=[(s_lnD, s_lnD.n)], incs=[(s_lnD, 1)])
        E("dve", lambda e: e.scalar_tensor_tensor(st[0:128, 4:5], st[0:128, 1:2], 1.0 / D,
                                                  st[0:128, 3:4], ALU.mult, ALU.subtract),
          waits=[(s_lnD, s_lnD.n)], incs=[(s_lnD, 1)])
        dv0 = s_lnD.n
        E("act", lambda e: e.activation(st[0:128, 5:6], st[0:128, 4:5], AF.Ln,
                                        bias=epsln[0:128, 0:1]),
          waits=[(s_lnD, dv0)], incs=[(s_lnA, 1)])
        E("act", lambda e: e.activation(st[0:128, 6:7], st[0:128, 5:6], AF.Exp, scale=-0.5),
          waits=[(s_lnA, s_lnA.n)], incs=[(s_lnA, 1)])
        ac0 = s_lnA.n
        E("dve", lambda e: e.tensor_scalar(xnt[0:128, 0:1024], src_fn(0, 1024),
                                           st[0:128, 2:3], st[0:128, 6:7], ALU.subtract, ALU.mult),
          waits=[(s_lnA, ac0), G.w(("xntW", id(xnt)))], incs=[(s_lnD, 1)])
        return s_lnD.n

    def emit_ln_tp(dst_fm, dst_stride, dst_pad, tc, sem_done, xnt, dv1):
        for half in range(2):
            bk = 6 + half
            E("pe", lambda e, h=half, p=PSB[bk]: [
                e.transpose(p[0:128, 128 * i:128 * (i + 1)],
                            xnt[0:128, (h * 4 + i) * 128:(h * 4 + i + 1) * 128],
                            ident[0:128, 0:128]) for i in range(4)][-1],
              waits=[(s_lnD, dv1), psw(bk)], incs=[(s_peT, 1)])
            if half == 1:
                G.s(("xntW", id(xnt)), s_peT)
            pt = s_peT.n
            E("act", lambda e, h=half, p=PSB[bk]: e.activation(
                dst_fm.ap(0, 128, [[dst_stride, 4], [1, 128]],
                          off=(h * 4) * dst_stride + dst_pad + tc * 128),
                p[0:128, 0:512], AF.Identity),
              waits=[(s_peT, pt)], incs=[(sem_done, 1)])
            pss(bk, sem_done)

    ln_dv1 = {}

    def emit_ln1_stats(seq, tc):
        par = tc % 2
        E("sp", lambda e, t=tc, p=par: e.dma_start(
            x_til[p][0:128, 0:1024],
            bass.AP(X, (seq * S + t * 128) * D, [[D, 128], [1, D]])),
          waits=[G.w(("xtil", par))], incs=[(s_x[par], 16)], dma=True)
        nx = s_x[par].n
        em.pending["dve"].append((s_x[par], nx))
        ln_dv1[tc] = emit_ln_stats(lambda a, b, p=par: x_til[p][0:128, a:b],
                                   xnt=[xn_tm, xnt_b][par])
        G.s(("xtil", par), s_lnD)

    def emit_ln1_tp(tc):
        emit_ln_tp(xn1, 2050, 1, tc, s_lnA, [xn_tm, xnt_b][tc % 2], ln_dv1[tc])

    def emit_ln1_tc(seq, tc):
        emit_ln1_stats(seq, tc)
        emit_ln1_tp(tc)

    def emit_aw(p):
        E("pool", lambda e, pp=p: e.dma_start(
            ckt.ap(0, 128, [[2048, 3], [1024, 2], [1, 1024]]),
            bass.AP(CK, pp * 3 * 256 * 1024,
                    [[1024, 128], [256 * 1024, 3], [128 * 1024, 2], [1, 1024]])),
          waits=[G.w("ck")], incs=[(s_awc, 16)], dma=True)
        E("pool", lambda e, pp=p: e.dma_start(
            rwt.ap(0, 128, [[2048, 3], [256, 8], [1, 256]]),
            bass.AP(RW, pp * 3 * 1024 * 256,
                    [[256, 128], [1024 * 256, 3], [128 * 256, 8], [1, 256]])),
          waits=[G.w("rw"), G.w("awx")], incs=[(s_awr, 16)], dma=True)
        E("pool", lambda e, pp=p: e.dma_start(
            pwt.ap(0, 128, [[1024, 6], [1, 1024]]),
            bass.AP(PW, pp * 768 * 1024, [[1024, 128], [128 * 1024, 6], [1, 1024]])),
          waits=[G.w("pw"), G.w("awx")], incs=[(s_awp, 16)], dma=True)
        return s_awc.n, s_awr.n, s_awp.n

    def emit_conv_group(p, blk, oc, lnA_need, awc_n):
        bk = oc % 2
        gco = oc // 2
        E("pe", lambda e, oc=oc, g=gco, ps=PS[bk], b=blk: [
            e.matmul(ps[0:128, 0:512],
                     ckt.ap(0, 128, [[1, 128]], off=w * 2048 + cic * 1024 + oc * 128),
                     xn1.ap(0, 128, [[1, 512]], off=(g * 2 + cic) * 2050 + w + b * 512),
                     start=(w == 0 and cic == 0), stop=(w == 2 and cic == 1))
            for w in range(3) for cic in range(2)][-1],
          waits=[(s_awc, awc_n), (s_lnA, lnA_need), G.w("pads"), psw(bk)],
          incs=[(s_peC, 1)])
        pc = s_peC.n
        E("act", lambda e, oc=oc, ps=PS[bk], pp=p: e.activation(
            h_fm[0:128, oc * 512:(oc + 1) * 512], ps[0:128, 0:512], AF.Identity,
            bias=convb[0:128, pp * 8 + oc:pp * 8 + oc + 1]),
          waits=[(s_peC, pc), G.w("hW")], incs=[(s_acC, 1)])
        pss(bk, s_acC)

    def emit_convrff(p, blk, lnA_need, awc_n, awr_n):
        for oc in range(8):
            emit_conv_group(p, blk, oc, lnA_need, awc_n)
        acC_blk = s_acC.n
        emit_rff(p, acC_blk, awr_n)

    def emit_rff(p, acC_blk, awr_n):
        for gi in range(3):
            spc = float(np.float32(math.sqrt(2 * GAMMAS[gi]) / TWO_PI))
            for mc in range(2):
                idx = gi * 2 + mc
                bk = 2 + idx % 2
                t1 = t1_t[idx % 2]
                dd = t1
                E("pe", lambda e, gi=gi, mc=mc, ps=PS[bk]: [
                    e.matmul(ps[0:128, 0:512],
                             rwt.ap(0, 128, [[1, 128]], off=gi * 2048 + dc * 256 + mc * 128),
                             h_fm[0:128, dc * 512:(dc + 1) * 512],
                             start=(dc == 0), stop=(dc == 7)) for dc in range(8)][-1],
                  waits=[(s_awr, awr_n), (s_acC, acC_blk), psw(bk)], incs=[(s_peR, 1)])
                pr = s_peR.n
                bcol = p * 6 + idx
                E("act", lambda e, ps=PS[bk], t1=t1, sc=spc, bc=bcol: e.activation(
                    t1[0:128, 0:512], ps[0:128, 0:512], AF.Identity,
                    bias=bias1[0:128, bc:bc + 1], scale=sc),
                  waits=[(s_peR, pr), G.w(("t1", idx % 2))], incs=[(s_acR, 1)])
                ar = s_acR.n
                E("dve", lambda e, t1=t1: e.tensor_single_scalar(
                    t1[0:128, 0:512], t1[0:128, 0:512], MAGIC, ALU.add),
                  waits=[(s_acR, ar)], incs=[(s_dvR, 1)])
                E("dve", lambda e, t1=t1: e.tensor_single_scalar(
                    t1[0:128, 0:512], t1[0:128, 0:512], MAGIC, ALU.subtract),
                  waits=[(s_dvR, s_dvR.n)], incs=[(s_dvR, 1)])
                E("dve", lambda e, ps=PS[bk], t1=t1, dd=dd, sc=spc: e.scalar_tensor_tensor(
                    dd[0:128, 0:512], ps[0:128, 0:512], sc, t1[0:128, 0:512],
                    ALU.mult, ALU.subtract),
                  waits=[(s_dvR, s_dvR.n)], incs=[(s_dvR, 1)])
                pss(bk, s_dvR)
                G.s(("t1", idx % 2), s_dvR)
                dr = s_dvR.n
                E("act", lambda e, dd=dd, idx=idx, bc=bcol: e.activation(
                    feats[0:128, idx * 512:(idx + 1) * 512], dd[0:128, 0:512], AF.Sin,
                    bias=bias2[0:128, bc:bc + 1], scale=float(TWO_PI * SINSHRINK)),
                  waits=[(s_dvR, dr), G.w("featsW")], incs=[(s_acS, 1)])
                G.s(("t1", idx % 2), s_acS)
        G.s("hW", s_peR)

    def emit_projmm(p, i, awp_n, acs_n, bks=(4, 5)):
        E("pe", lambda e, i=i, p=p, bb=bks: [
            y for half in range(2) for y in (
                [e.matmul(PS[bb[half]][0:128, 0:512],
                          feats[0:128, fc * 512 + i * 128: fc * 512 + (i + 1) * 128],
                          pwt.ap(0, 128, [[1, 512]], off=fc * 1024 + half * 512),
                          start=(fc == 0), stop=False) for fc in range(6)] +
                [e.matmul(PS[bb[half]][0:128, 0:512], ones1[0:1, 0:128],
                          pbrow[0:1, p * 1024 + half * 512: p * 1024 + (half + 1) * 512],
                          start=False, stop=True)])][-1],
          waits=[(s_acS, acs_n), (s_awp, awp_n), psw(bks[0]), psw(bks[1])], incs=[(s_peP, 1)])
        return s_peP.n

    def emit_v_tc(tc, ppn):
        for half in range(2):
            w = [(s_peP, ppn)]
            if tc == 0 and half == 0:
                w.append(G.w("xn2sv"))
            E("act", lambda e, h=half, t=tc: e.activation(
                v_sb[0:128, t * 1024 + h * 512: t * 1024 + (h + 1) * 512],
                PS[4 + h][0:128, 0:512], AF.Identity, scale=CSC),
              waits=w, incs=[(s_acV, 1)])
            pss(4 + half, s_acV)

    def emit_rope(tc, dest, kind, ppn, destW, bks=(4, 5)):
        for half in range(2):
            ps = PS[bks[half]]
            tmp = t1_t[half]
            sstap = sst.ap(0, 128, [[0, 8], [1, 64]], off=tc * 64)
            cctap = cct.ap(0, 128, [[0, 8], [1, 64]], off=tc * 64)
            E("dve", lambda e, ps=ps, tmp=tmp, ap=sstap: e.tensor_tensor(
                tmp[0:128, 0:512], ps[0:128, 0:512], ap, ALU.mult),
              waits=[(s_peP, ppn), G.w(("t1", half))], incs=[(s_dvRo, 1)])
            if kind == "q":
                outap = dest.ap(0, 128, [[1, 512]], off=half * 512)
                outsw = dest.ap(0, 128, [[64, 8], [-32, 2], [1, 32]], off=half * 512 + 32)
            else:
                outap = dest.ap(0, 128, [[128, 8], [1, 64]], off=half * 8 * 128)
                outsw = dest.ap(0, 128, [[128, 8], [-32, 2], [1, 32]], off=half * 8 * 128 + 32)
            E("dve", lambda e, ps=ps, oa=outap, ap=cctap: e.tensor_tensor(
                oa, ps[0:128, 0:512], ap, ALU.mult),
              waits=[G.w(destW) if half == 0 else None], incs=[(s_dvRo, 1)])
            tmpsw = tmp.ap(0, 128, [[64, 8], [-32, 2], [1, 32]], off=32)
            E("dve", lambda e, osw=outsw, tsw=tmpsw: e.tensor_tensor(osw, osw, tsw, ALU.add),
              waits=[(s_dvRo, s_dvRo.n)], incs=[(s_dvRo, 1)])
            G.s(("t1", half), s_dvRo)
            pss(bks[half], s_dvRo)

    def emit_q_tc(tc, ppn):
        emit_rope(tc, qr_tm, "q", ppn, "qrW")
        dro = s_dvRo.n
        E("pe", lambda e: [
            e.transpose(PSB[6][0:128, dc * 128:(dc + 1) * 128],
                        qr_tm[0:128, dc * 128:(dc + 1) * 128], ident[0:128, 0:128])
            for dc in range(8)][-1],
          waits=[(s_dvRo, dro), psw(6)], incs=[(s_peT, 1)])
        G.s("qrW", s_peT)
        pt = s_peT.n
        E("dve", lambda e, t=tc: e.tensor_copy(
            q_fm.ap(0, 128, [[2048, 8], [1, 128]], off=t * 128), PSB[6][0:128, 0:1024]),
          waits=[(s_peT, pt)], incs=[(s_dvT, 1)])
        pss(6, s_dvT)

    rope_done = {}

    def emit_q_rope(tc, ppn):
        par = tc % 2
        emit_rope(tc, [qr_tm, qr_tm2][par], "q", ppn, ("qrW", par))
        rope_done[tc] = s_dvRo.n

    def emit_q_tp(tc):
        par = tc % 2
        bk = 6 + par
        qrb = [qr_tm, qr_tm2][par]
        E("pe", lambda e, b=bk, q=qrb: [
            e.transpose(PSB[b][0:128, dc * 128:(dc + 1) * 128],
                        q[0:128, dc * 128:(dc + 1) * 128], ident[0:128, 0:128])
            for dc in range(8)][-1],
          waits=[(s_dvRo, rope_done[tc]), psw(bk)], incs=[(s_peT, 1)])
        G.s(("qrW", par), s_peT)
        pt = s_peT.n
        E("act", lambda e, t=tc, b=bk: e.activation(
            q_fm.ap(0, 128, [[2048, 8], [1, 128]], off=t * 128),
            PSB[b][0:128, 0:1024], AF.Identity),
          waits=[(s_peT, pt)], incs=[(s_acT, 1)])
        pss(bk, s_acT)

    def emit_k_rope(tc, ppn, bks=(4, 5)):
        par = tc % 2
        kkb = [kk_tm, kk_tm2][par]
        emit_rope(tc, kkb, "k", ppn, ("kkW", par), bks)
        kin = kkb.ap(0, 128, [[128, 16], [1, 64]], off=0)
        kout = kkb.ap(0, 128, [[128, 16], [1, 64]], off=64)
        E("dve", lambda e, ki=kin, ko=kout: e.tensor_tensor(ko, ki, ki, ALU.mult),
          waits=[(s_dvRo, s_dvRo.n)], incs=[(s_dvRo, 1)])
        rope_done[tc] = s_dvRo.n

    def emit_k_rest(tc):
        par = tc % 2
        kkb = [kk_tm, kk_tm2][par]
        tbks = (4, 5) if par == 0 else (2, 3)
        dro = rope_done[tc]
        dvt_h = []
        for half in range(2):
            bk = tbks[half]
            E("pe", lambda e, h=half, p=PSB[bk], kb=kkb: [
                e.transpose(p[0:128, j * 128:(j + 1) * 128],
                            kb[0:128, (h * 8 + j) * 128:(h * 8 + j + 1) * 128],
                            ident[0:128, 0:128]) for j in range(8)][-1],
              waits=[(s_dvRo, dro), psw(bk)], incs=[(s_peT, 1)])
            pt = s_peT.n
            E("dve", lambda e, h=half, p=PSB[bk]: e.tensor_copy(
                kkfm[0:128, h * 1024:(h + 1) * 1024], p[0:128, 0:1024]),
              waits=[(s_peT, pt), G.w("kkfmW") if half == 0 else None],
              incs=[(s_dvT, 1)])
            dvt_h.append(s_dvT.n)
            pss(bk, s_dvT)
        G.s(("kkW", par), s_peT)
        dvt1 = s_dvT.n
        for grp in range(4):
            bk = 6 + grp % 2
            E("pe", lambda e, g=grp, ps=PS[bk]: [
                e.matmul(ps[0:64, j * 128:(j + 1) * 128], nylhs[0:128, 0:64],
                         kkfm[0:128, (g * 4 + j) * 128:(g * 4 + j + 1) * 128],
                         start=True, stop=True) for j in range(4)][-1],
              waits=[(s_dvT, dvt_h[0] if grp < 2 else dvt1), psw(bk)], incs=[(s_peN, 1)])
            pn = s_peN.n
            E("act", lambda e, g=grp, ps=PS[bk]: e.activation(
                kny.ap(0, 64, [[1, 512]], off=g * 512), ps[0:64, 0:512], AF.Exp,
                bias=biasny[0:64, 0:1]),
              waits=[(s_peN, pn), G.w("knyW")], incs=[(s_acN, 1)])
            pss(bk, s_acN)
        G.s("kkfmW", s_peN)
        acn1 = s_acN.n
        E("dve", lambda e: e.tensor_tensor(kny.ap(64, 64, [[1, 2048]]),
                                           kny.ap(0, 64, [[1, 2048]]),
                                           kny.ap(0, 64, [[1, 2048]]), ALU.mult),
          waits=[(s_acN, acn1)], incs=[(s_dvT, 1)])
        dvt2 = s_dvT.n
        for pr in range(8):
            bk = tbks[pr % 2]
            E("pe", lambda e, pr=pr, ps=PS[bk]: [
                e.matmul(ps[0:128, j * 256:(j + 1) * 256],
                         kny[0:128, (pr * 2 + j) * 128:(pr * 2 + j + 1) * 128],
                         aug_om[0:128, 0:256], start=True, stop=True) for j in range(2)][-1],
              waits=[(s_dvT, dvt2), psw(bk)], incs=[(s_peK, 1)])
            pk = s_peK.n
            E("act", lambda e, pr=pr, ps=PS[bk]: e.activation(
                kp_all[0:128, pr * 512:(pr + 1) * 512], ps[0:128, 0:512], AF.Exp,
                bias=bln16[0:128, 0:1]),
              waits=[(s_peK, pk), G.w("kpW")], incs=[(s_acK, 1)])
            pss(bk, s_acK)
        G.s("knyW", s_peK)
        ack = s_acK.n
        acv = s_acV.n
        E("pe", lambda e, t=tc: [
            e.matmul(PS[(h * 2 + ec) // 8][0:128,
                     ((h * 2 + ec) % 8) * 64:((h * 2 + ec) % 8) * 64 + 64],
                     kp_all[0:128, h * 256 + ec * 128: h * 256 + (ec + 1) * 128],
                     v_sb[0:128, t * 1024 + h * 64: t * 1024 + (h + 1) * 64],
                     start=True, stop=True)
            for h in range(16) for ec in range(2)][-1],
          waits=[(s_acK, ack), (s_acV, acv), psw(0), psw(1), psw(2), psw(3)],
          incs=[(s_peV, 1)])
        E("pe", lambda e: [
            e.matmul(PS[6][0:128, sl:sl + 1],
                     kp_all[0:128, (sl // 2) * 256 + (sl % 2) * 128:
                            (sl // 2) * 256 + (sl % 2 + 1) * 128],
                     onecol[0:128, 0:1], start=True, stop=True) for sl in range(32)][-1],
          waits=[psw(6)], incs=[(s_peV, 1)])
        G.s("kpW", s_peV)
        pv = s_peV.n
        first = (tc == 0)
        for b_ in range(4):
            if first:
                E("dve", lambda e, b=b_: e.tensor_copy(kvacc[0:128, b * 512:(b + 1) * 512],
                                                       PS[b][0:128, 0:512]),
                  waits=[(s_peV, pv)])
            else:
                E("dve", lambda e, b=b_: e.tensor_tensor(kvacc[0:128, b * 512:(b + 1) * 512],
                                                         kvacc[0:128, b * 512:(b + 1) * 512],
                                                         PS[b][0:128, 0:512], ALU.add),
                  waits=[(s_peV, pv)])
        if first:
            E("dve", lambda e: e.tensor_copy(ksum[0:128, 0:32], PS[6][0:128, 0:32]),
              incs=[(s_dvK, 1)])
        else:
            E("dve", lambda e: e.tensor_tensor(ksum[0:128, 0:32], ksum[0:128, 0:32],
                                               PS[6][0:128, 0:32], ALU.add), incs=[(s_dvK, 1)])
        for b_ in range(4):
            pss(b_, s_dvK)
        pss(6, s_dvK)

    def emit_k_tc(tc, ppn):
        emit_k_rope(tc, ppn)
        emit_k_rest(tc)

    def emit_attn(seq):
        ln_done = {}
        dvo_done = {}
        if seq > 0:
            E("pool", lambda e: [
                e.memset(xn1.ap(0, 128, [[2050, 8], [1, 1]], off=0), 0.0),
                e.memset(xn1.ap(0, 128, [[2050, 8], [1, 1]], off=2049), 0.0)][-1],
              incs=[(s_setP, 1)])
        G.m["pads"] = (s_setP, s_setP.n)
        awc_n, awr_n, awp_n = emit_aw(2)
        for tc in range(4):
            emit_ln1_tc(seq, tc)
            ln_done[tc] = s_lnA.n
        for p_i, (p, kind) in enumerate([(2, "v"), (0, "q"), (1, "k")][:_DBG_NP]):
            if p_i > 0:
                awc_n, awr_n, awp_n = emit_aw(p)
            if kind in ("v", "q") and _DBG_NBLK == 4 and _DBG_SERIAL == 0:
                if p_i == 0:
                    emit_ln1_stats(seq, 4)
                    emit_ln1_stats(seq, 5)
                    emit_ln1_tp(4)
                    ln_done[4] = s_lnA.n
                    pre_sq = [6, 7]
                    pre_tp = [5, 6, 7]
                else:
                    pre_sq = []
                    pre_tp = []
                for oc in range(8):
                    emit_conv_group(p, 0, oc, ln_done[4], awc_n)
                    if oc % 2 == 1:
                        if pre_sq:
                            emit_ln1_stats(seq, pre_sq.pop(0))
                        if pre_tp:
                            t_ = pre_tp.pop(0)
                            emit_ln1_tp(t_)
                            ln_done[t_] = s_lnA.n
                conv_done = s_acC.n
                ln_q = []
                ln_sq = []
                for blk in range(4):
                    emit_rff(p, conv_done, awr_n)
                    if p_i == 0 and blk < 2:
                        t0 = 4 * blk + 8
                        emit_ln1_stats(seq, t0)
                        emit_ln1_stats(seq, t0 + 1)
                        emit_ln1_tp(t0)
                        ln_done[t0] = s_lnA.n
                        ln_sq = [t0 + 2, t0 + 3]
                        ln_q = [t0 + 1, t0 + 2, t0 + 3]
                    if blk + 1 < 4:
                        lnn = ln_done[min(4 * (blk + 1) + 4, NTC - 1)]
                        emit_conv_group(p, blk + 1, 0, lnn, awc_n)
                        emit_conv_group(p, blk + 1, 1, lnn, awc_n)
                    if kind == "q" and blk > 0:
                        emit_q_tp(blk * 4 - 1)
                    acs = s_acS.n
                    for i in range(4):
                        tc = blk * 4 + i
                        ppn = emit_projmm(p, i, awp_n, acs)
                        if kind == "v":
                            emit_v_tc(tc, ppn)
                        else:
                            emit_q_rope(tc, ppn)
                        if ln_sq:
                            emit_ln1_stats(seq, ln_sq.pop(0))
                        if blk + 1 < 4 and i < 3:
                            lnn = ln_done[min(4 * (blk + 1) + 4, NTC - 1)]
                            emit_conv_group(p, blk + 1, 2 + 2 * i, lnn, awc_n)
                            emit_conv_group(p, blk + 1, 3 + 2 * i, lnn, awc_n)
                        if ln_q:
                            t_ = ln_q.pop(0)
                            emit_ln1_tp(t_)
                            ln_done[t_] = s_lnA.n
                        if kind == "q" and i > 0:
                            emit_q_tp(tc - 1)
                    if blk + 1 < 4:
                        conv_done = s_acC.n
                    G.s("featsW", s_peP)
                if kind == "q":
                    emit_q_tp(NTC - 1)
            elif kind == "k" and _DBG_NBLK == 4 and _DBG_SERIAL == 0:
                for blk in range(4):
                    emit_convrff(p, blk, ln_done[min(4 * blk + 4, NTC - 1)],
                                 awc_n, awr_n)
                    if blk > 0:
                        emit_k_rest(blk * 4 - 1)
                    acs = s_acS.n
                    for i in range(4):
                        tc = blk * 4 + i
                        bks = (4, 5) if i % 2 == 0 else (2, 3)
                        ppn = emit_projmm(p, i, awp_n, acs, bks)
                        emit_k_rope(tc, ppn, bks)
                        if i > 0:
                            emit_k_rest(tc - 1)
                    G.s("featsW", s_peP)
                emit_k_rest(NTC - 1)
            else:
                for blk in range(_DBG_NBLK):
                    if p_i == 0:
                        for tc in range(4 * blk + 4, min(4 * blk + 8, NTC)):
                            emit_ln1_tc(seq, tc)
                            ln_done[tc] = s_lnA.n
                    lnA_need = ln_done[min(4 * blk + 4, NTC - 1)]
                    emit_convrff(p, blk, lnA_need, awc_n, awr_n)
                    if _DBG_SERIAL in (1, 2):
                        em.barrier(s_bar)
                    acs = s_acS.n
                    for i in range(4):
                        tc = blk * 4 + i
                        ppn = emit_projmm(p, i, awp_n, acs)
                        if kind == "v":
                            emit_v_tc(tc, ppn)
                        elif kind == "q":
                            emit_q_tc(tc, ppn)
                        else:
                            emit_k_tc(tc, ppn)
                        if _DBG_SERIAL in (1, 3):
                            em.barrier(s_bar)
                    G.s("featsW", s_peP)
            if kind == "v":
                G.s("ck", s_peC)
                G.s("rw", s_peR)
                G.s("pw", s_peP)
            elif kind == "q":
                G.s("ck", s_peC)
                G.s("rw", s_peR)
                G.s("pw", s_peP)
        E("pool", lambda e: e.dma_start(
            outw_t.ap(0, 128, [[1024, 8], [1, 1024]]),
            bass.AP(OW, 0, [[1024, 128], [128 * 1024, 8], [1, 1024]])),
          waits=[(s_peT, s_peT.n)], incs=[(s_ow, 16)], dma=True)
        ow_n = s_ow.n
        if _DBG_NP < 3:
            return
        E("dve", lambda e: e.tensor_copy(
            kvks.ap(0, 128, [[65, 32], [1, 64]], off=0),
            kvacc.ap(0, 128, [[64, 32], [1, 64]], off=0)),
          waits=[(s_dvK, s_dvK.n), G.w("cwsv")], incs=[(s_dvB, 1)])
        E("dve", lambda e: e.tensor_copy(
            kvks.ap(0, 128, [[65, 32], [1, 1]], off=64),
            ksum.ap(0, 128, [[1, 32]], off=0)), incs=[(s_dvB, 1)])
        kvready = s_dvB.n
        pev_all = s_peV.n
        attnpre_done = {}
        iters = [(h, tt) for h in range(NH) for tt in range(4)]
        NIT = len(iters)
        st_fav = {}
        st_nd = {}
        st_rbc = {}
        st_exp = {}
        st_evac = {}
        st_recip = {}
        st_mult = {}
        st_qsq = {}

        def q_slice(i):
            h, tt = iters[i]
            return q_fm.ap((h % 2) * 64, 64, [[1, 512]], off=(h // 2) * 2048 + tt * 512)

        qf_ready = s_acT.n

        def emit_qsq(i):
            par = i % 2
            bb = (iters[i][0] % 2) * 64
            w = ([(s_peB, st_fav[i - 2])] if i - 2 in st_fav
                 else [(s_peV, pev_all), (s_acT, qf_ready)])
            E("pool", lambda e, qs=q_slice(i), p=par, b=bb: e.tensor_tensor(
                qsq_t[p].ap(b, 64, [[1, 512]]), qs, qs, ALU.mult), waits=w, incs=[(s_poB, 1)])
            st_qsq[i] = s_poB.n

        def emit_fav(i):
            par = i % 2
            h, tt = iters[i]
            bb = (h % 2) * 64
            w = [(s_poB, st_qsq[i])]
            if i - 2 in st_exp:
                w.append((s_acB, st_exp[i - 2]))
            else:
                w += [psw(2 * par), psw(2 * par + 1)]
            E("pe", lambda e, qs=q_slice(i), b=bb, p=par: [
                y for ec in range(2) for y in (
                    e.matmul(PS[2 * p + ec][0:128, 0:512],
                             om_dup.ap(b, 64, [[1, 128]], off=ec * 128), qs,
                             start=True, stop=False),
                    e.matmul(PS[2 * p + ec][0:128, 0:512], mhalf.ap(b, 64, [[1, 128]]),
                             qsq_t[p].ap(b, 64, [[1, 512]]), start=False, stop=True))][-1],
              waits=w, incs=[(s_peB, 1)])
            st_fav[i] = s_peB.n

        def emit_exp(i):
            par = i % 2
            w = [(s_peB, st_fav[i])]
            if i - 2 in st_nd:
                w.append((s_peB, st_nd[i - 2]))
            for ec in range(2):
                E("act", lambda e, ec=ec, p=par: e.activation(
                    qp_t[p][0:128, ec * 512:(ec + 1) * 512], PS[2 * p + ec][0:128, 0:512],
                    AF.Exp, bias=bln16[0:128, 0:1]),
                  waits=w if ec == 0 else [], incs=[(s_acB, 1)])
            st_exp[i] = s_acB.n

        def emit_nd(i):
            par = i % 2
            h, tt = iters[i]
            w = [(s_acB, st_exp[i]), (s_dvB, kvready)]
            if i - 2 in st_evac:
                w.append((s_acB, st_evac[i - 2]))
            else:
                w.append(psw(4 + par))
            E("pe", lambda e, hh=h, p=par: [
                e.matmul(PS[4 + p][0:65, 0:512], kvks[0:128, (2 * hh) * 65:(2 * hh + 1) * 65],
                         qp_t[p][0:128, 0:512], start=True, stop=False),
                e.matmul(PS[4 + p][0:65, 0:512], kvks[0:128, (2 * hh + 1) * 65:(2 * hh + 2) * 65],
                         qp_t[p][0:128, 512:1024], start=False, stop=True)][-1],
              waits=w, incs=[(s_peB, 1)])
            st_nd[i] = s_peB.n

        def emit_evac(i):
            p3 = i % 3
            par = i % 2
            w = [(s_peB, st_nd[i])]
            if i - 3 in st_mult:
                w.append((s_dvB, st_mult[i - 3]))
            if i - 3 in st_recip:
                w.append((s_dvB, st_recip[i - 3]))
            E("act", lambda e, p=p3, pp=par: e.activation(
                nd_t[p][0:65, 0:512], PS[4 + pp][0:65, 0:512], AF.Identity,
                bias=eps65[0:65, 0:1]), waits=w, incs=[(s_acB, 1)])
            st_evac[i] = s_acB.n

        def emit_recip(i):
            p3 = i % 3
            w = [(s_acB, st_evac[i])]
            def _recip(e, p=p3):
                with nc.allow_low_precision(reason="bf16 denom reciprocal (operand-level rounding)"):
                    return e.reciprocal(nd_t[p].ap(64, 1, [[1, 512]]),
                                        nd_t[p].ap(64, 1, [[1, 512]]))
            E("dve", _recip, waits=w, incs=[(s_dvB, 1)])
            st_recip[i] = s_dvB.n

        def emit_rbc(i):
            p3 = i % 3
            par = i % 2
            w = [(s_dvB, st_recip[i])]
            if i - 2 in st_mult:
                w.append((s_dvB, st_mult[i - 2]))
            else:
                w.append(psw(6 + par))
            E("pe", lambda e, p=p3, pp=par: e.matmul(
                PS[6 + pp][0:64, 0:512], ones1.ap(64, 1, [[1, 64]]),
                nd_t[p].ap(64, 1, [[1, 512]]), start=True, stop=True),
              waits=w, incs=[(s_peB, 1)])
            st_rbc[i] = s_peB.n

        def emit_mult(i):
            p3 = i % 3
            par = i % 2
            h, tt = iters[i]
            bb = (h % 2) * 64
            dc = h // 2
            E("dve", lambda e, p=p3, pp=par, b=bb, d=dc, t=tt: e.tensor_tensor(
                attnpre.ap(b, 64, [[1, 512]], off=d * 2048 + t * 512),
                nd_t[p].ap(0, 64, [[1, 512]]), PS[6 + pp][0:64, 0:512], ALU.mult),
              waits=[(s_peB, st_rbc[i])], incs=[(s_dvB, 1)])
            st_mult[i] = s_dvB.n
            h2, tt2 = iters[i]
            if h2 == NH - 1:
                attnpre_done[tt2] = s_dvB.n

        emit_qsq(0)
        for i in range(NIT):
            if i + 1 < NIT:
                emit_qsq(i + 1)
            emit_fav(i)
            emit_exp(i)
            if i - 2 >= 0:
                emit_rbc(i - 2)
                emit_mult(i - 2)
            if i - 1 >= 0:
                emit_nd(i - 1)
                emit_evac(i - 1)
                emit_recip(i - 1)
        emit_nd(NIT - 1)
        emit_evac(NIT - 1)
        emit_recip(NIT - 1)
        for i_ in (NIT - 2, NIT - 1):
            emit_rbc(i_)
            emit_mult(i_)
        for b_ in range(6):
            G.m[("ps", b_)] = (s_acB, s_acB.n)
        for b_ in (6, 7):
            G.m[("ps", b_)] = (s_dvB, s_dvB.n)
        G.s(("xb2", 0), s_peB)
        G.s(("xb2", 1), s_peB)
        for tc in range(NTC):
            par = tc % 2
            if tc >= 2:
                emit_ln2_tc(seq, tc - 2, dvo_done[tc - 2])
            E("sp", lambda e, t=tc, pq=par: e.dma_start(
                xb2[pq][0:128, 0:1024],
                bass.AP(X, (seq * S + t * 128) * D, [[D, 128], [1, D]])),
              waits=[G.w(("xb2", par)), G.w("cwsv") if tc < 2 else None],
              incs=[(s_x2[par], 16)], dma=True)
            nx = s_x2[par].n
            bks = (0, 1) if tc % 2 == 0 else (2, 3)
            E("pe", lambda e, t=tc, pp=(PS[bks[0]], PS[bks[1]]): [
                [e.matmul(pp[half][0:128, 0:512],
                          attnpre[0:128, dc2 * 2048 + t * 128: dc2 * 2048 + (t + 1) * 128],
                          outw_t.ap(0, 128, [[1, 512]], off=dc2 * 1024 + half * 512),
                          start=(dc2 == 0), stop=False)
                 for half in range(2) for dc2 in range(8)],
                e.matmul(pp[0][0:128, 0:512], ones1[0:1, 0:128], outbr[0:1, 0:512],
                         start=False, stop=True),
                e.matmul(pp[1][0:128, 0:512], ones1[0:1, 0:128], outbr[0:1, 512:1024],
                         start=False, stop=True)][-1],
              waits=[(s_dvB, attnpre_done[tc // 4]), (s_ow, ow_n), psw(bks[0]), psw(bks[1])],
              incs=[(s_peO, 1)])
            po = s_peO.n
            E("dve", lambda e, t=tc, pq=par, pp=(PS[bks[0]], PS[bks[1]]): [
                e.tensor_tensor(x2ap(t, 0, 512),
                                xb2[pq][0:128, 0:512], pp[0][0:128, 0:512], ALU.add),
                e.tensor_tensor(x2ap(t, 512, 1024),
                                xb2[pq][0:128, 512:1024], pp[1][0:128, 0:512], ALU.add)][-1],
              waits=[(s_peO, po), (s_x2[par], nx)], incs=[(s_dvO, 1)])
            pss(bks[0], s_dvO)
            pss(bks[1], s_dvO)
            G.s(("xb2", par), s_dvO)
            dvo_done[tc] = s_dvO.n
        emit_ln2_tc(seq, NTC - 2, dvo_done[NTC - 2])
        emit_ln2_tc(seq, NTC - 1, dvo_done[NTC - 1])
        E("pool", lambda e: e.nop(),
          waits=[(s_acc[t2], s_acc[t2].n) for t2 in range(NTC)], incs=[(s_fl, 1)])

    def emit_ln2_tc(seq, tc, dvo_wait):
        em.pending["dve"].append((s_dvO, dvo_wait))
        emit_ln_tc(lambda a, b, t=tc: x2ap(t, a, b), xn2, 2048, 0, tc, s_ln2A,
                   xsq=xsq2, xnt=xn_tm2)
        dvl = s_lnD.n
        E("pool", lambda e, t=tc: e.dma_start(
            bass.AP(OUT, (seq * S + t * 128) * D, [[D, 128], [1, D]]),
            x2ap(t, 0, 1024)),
          waits=[(s_lnD, dvl)], incs=[(s_acc[tc], 16)], dma=True)

    def emit_router():
        for tt in range(4):
            bk = tt % 2
            E("pe", lambda e, t=tt, ps=PS[bk]: [
                e.matmul(ps[0:8, 0:512], rwt_b[0:128, dc * 8:(dc + 1) * 8],
                         xn2.ap(0, 128, [[1, 512]], off=dc * 2048 + t * 512),
                         start=(dc == 0), stop=(dc == 7)) for dc in range(8)][-1],
              waits=[(s_ln2A, s_ln2A.n), psw(bk)], incs=[(s_lg, 1)])
            pl = s_lg.n
            E("act", lambda e, t=tt, ps=PS[bk]: e.activation(
                logits.ap(0, 8, [[1, 512]], off=t * 512), ps[0:8, 0:512], AF.Identity,
                bias=expb[0:8, 0:1]),
              waits=[(s_lg, pl)], incs=[(s_lga, 1)])
            pss(bk, s_lga)
        lgall = s_lga.n
        E("pe", lambda e: [
            e.transpose(PSB[2][0:128, t * 8:(t + 1) * 8],
                        logits.ap(0, 8, [[1, 128]], off=t * 128), ident[0:8, 0:8])
            for t in range(16)][-1],
          waits=[(s_lga, lgall), psw(2)], incs=[(s_lg, 1)])
        pl = s_lg.n
        E("act", lambda e: e.activation(rB1[0:128, 0:128], PSB[2][0:128, 0:128],
                                        AF.Identity),
          waits=[(s_lg, pl)], incs=[(s_cwA, 1)])
        pss(2, s_cwA)
        v8 = lambda t_: t_.ap(0, 128, [[8, 16], [1, 8]])
        bc = lambda off: rsc.ap(0, 128, [[1, 16], [0, 8]], off=off)
        E("dve", lambda e: e.tensor_reduce(rsc[0:128, 0:16], v8(rB1), AX.X, ALU.max),
          waits=[(s_cwA, s_cwA.n)], incs=[(s_cwD, 1)])
        E("dve", lambda e: e.tensor_tensor(rB2[0:128, 0:128], rB1[0:128, 0:128],
                                           bc(0), ALU.subtract),
          waits=[(s_cwD, s_cwD.n)], incs=[(s_cwD, 1)])
        E("act", lambda e: e.activation(rB2[0:128, 0:128], rB2[0:128, 0:128], AF.Exp),
          waits=[(s_cwD, s_cwD.n)], incs=[(s_cwA, 1)])
        E("dve", lambda e: e.tensor_single_scalar(rB3[0:128, 0:128], rB2[0:128, 0:128],
                                                  1.0, ALU.is_ge),
          waits=[(s_cwA, s_cwA.n)], incs=[(s_cwD, 1)])
        E("dve", lambda e: e.tensor_tensor(rB1[0:128, 0:128], rB2[0:128, 0:128],
                                           rB3[0:128, 0:128], ALU.subtract),
          waits=[(s_cwD, s_cwD.n)], incs=[(s_cwD, 1)])
        E("dve", lambda e: e.tensor_reduce(rsc[0:128, 16:32], v8(rB1), AX.X, ALU.max),
          waits=[(s_cwD, s_cwD.n)], incs=[(s_cwD, 1)])
        E("dve", lambda e: e.tensor_tensor(rB1[0:128, 0:128], rB2[0:128, 0:128],
                                           bc(16), ALU.is_ge),
          waits=[(s_cwD, s_cwD.n)], incs=[(s_cwD, 1)])
        E("dve", lambda e: e.tensor_tensor(rB3[0:128, 0:128], rB2[0:128, 0:128],
                                           rB1[0:128, 0:128], ALU.mult),
          waits=[(s_cwD, s_cwD.n)], incs=[(s_cwD, 1)])
        E("dve", lambda e: e.tensor_single_scalar(rsc[0:128, 32:48], rsc[0:128, 16:32],
                                                  1.0, ALU.add),
          waits=[(s_cwD, s_cwD.n)], incs=[(s_cwD, 1)])
        E("dve", lambda e: e.reciprocal(rsc[0:128, 0:16], rsc[0:128, 32:48]),
          waits=[(s_cwD, s_cwD.n)], incs=[(s_cwD, 1)])
        E("dve", lambda e: e.tensor_tensor(cw_tm[0:128, 0:128], rB3[0:128, 0:128],
                                           bc(0), ALU.mult),
          waits=[(s_cwD, s_cwD.n)], incs=[(s_cwD, 1)])
        dcw = s_cwD.n
        for tc in range(NTC):
            bk2 = 4 + tc % 2
            E("pe", lambda e, t=tc, ps=PSB[bk2]: e.transpose(
                ps[0:8, 0:128], cw_tm.ap(0, 128, [[1, 8]], off=t * 8), ident[0:128, 0:128]),
              waits=[(s_cwD, dcw) if tc < 2 else None, psw(bk2)], incs=[(s_lg, 1)])
            pt2 = s_lg.n
            E("act", lambda e, t=tc, ps=PSB[bk2]: e.activation(
                cw_fm.ap(0, 8, [[1, 128]], off=t * 128), ps[0:8, 0:128], AF.Identity),
              waits=[(s_lg, pt2), G.w("cwsv") if tc == 0 else None], incs=[(s_cwA, 1)])
            pss(bk2, s_cwA)

    def emit_experts(passes, fl_all):
        experts = [(SW1, SW3, SW2, 0, False)] + [(W1, W3, W2, e, True) for e in range(8)]
        if _DBG_NE is not None:
            experts = experts[:_DBG_NE]
        sems_w = {}
        npass = len(passes)
        nfl = fl_all // 1

        def emit_w13_chunk(ei, ch, wait_w1, wait_w3):
            TW1, TW3, _, eidx, _ = experts[ei]
            E("pool", lambda e, tw=TW1, ex=eidx, c=ch: e.dma_start(
                w1b.ap(0, 128, [[2048, 8], [1, 1024]], off=c * 1024),
                bass.AP(tw, ex * 1024 * 2048 + c * 1024,
                        [[2048, 128], [128 * 2048, 8], [1, 1024]])),
              waits=[wait_w1, (s_fl, fl_all)], incs=[(s_w1[ch], 16)], dma=True)
            E("pool", lambda e, tw=TW3, ex=eidx, c=ch: e.dma_start(
                w3b.ap(0, 128, [[2048, 8], [1, 1024]], off=c * 1024),
                bass.AP(tw, ex * 1024 * 2048 + c * 1024,
                        [[2048, 128], [128 * 2048, 8], [1, 1024]])),
              waits=[wait_w3, (s_fl, fl_all)], incs=[(s_w3[ch], 16)], dma=True)
            a, b, c2, d2 = sems_w.setdefault(ei, ([0, 0], [0, 0], 0, 0))
            a[ch] = s_w1[ch].n
            b[ch] = s_w3[ch].n

        def emit_w2(ei, wait_w2):
            _, _, TW2, eidx, routed = experts[ei]
            E("pool", lambda e, t2w=TW2, ex=eidx: e.dma_start(
                w2b.ap(0, 128, [[1024, 16], [1, 1024]]),
                bass.AP(t2w, ex * 2048 * 1024, [[1024, 128], [128 * 1024, 16], [1, 1024]])),
              waits=[wait_w2, (s_ln2A, s_ln2A.n), (s_cwA, s_cwA.n), (s_fl, fl_all)],
              incs=[(s_w2, 16)], dma=True)
            a, b, c2, d2 = sems_w.setdefault(ei, ([0, 0], [0, 0], 0, 0))
            sems_w[ei] = (a, b, s_w2.n, d2)

        cwr_n = {}

        def emit_cwrow(ei, P):
            _, _, _, eidx, routed = experts[ei]
            if routed:
                cpar = P["pi"]
                E("sp", lambda e, ex=eidx, cp=cpar, src=P["cw"]: e.dma_start(
                    cwrow.ap(0, 1, [[1, 2048]], off=cp * 2048),
                    src.ap(ex, 1, [[1, 2048]])),
                  waits=[P["cw_wait"], G.w(("cwrowW", cpar))],
                  incs=[(s_cwr, 16)], dma=True)
            cwr_n[(ei, P["pi"])] = s_cwr.n

        def emit_cwb(ei, P):
            _, _, _, eidx, routed = experts[ei]
            if not routed:
                return
            cpar = P["pi"]
            ncr = cwr_n[(ei, cpar)]
            for t2 in range(4):
                E("pe", lambda e, t=t2, cp=cpar: e.matmul(
                    PS[t % 2][0:128, 0:512], ones1[0:1, 0:128],
                    cwrow.ap(0, 1, [[1, 512]], off=cp * 2048 + t * 512),
                    start=True, stop=True),
                  waits=[(s_cwr, ncr), psw(t2 % 2)], incs=[(s_peG, 1)])
                if t2 == 3:
                    G.s(("cwrowW", cpar), s_peG)
                pgc = s_peG.n
                E("act", lambda e, t=t2, cp=cpar: e.activation(
                    cwb.ap(0, 128, [[1, 512]], off=cp * 2048 + t * 512),
                    PS[t % 2][0:128, 0:512], AF.Identity),
                  waits=[(s_peG, pgc), G.w(("cwbW", cpar)) if t2 == 0 else None],
                  incs=[(s_acF, 1)])
                pss(t2 % 2, s_acF)

        emit_w13_chunk(0, 0, G.w(("w1c", 0)), G.w(("w3c", 0)))
        emit_w13_chunk(0, 1, G.w(("w1c", 1)), G.w(("w3c", 1)))
        emit_w2(0, G.w("w2"))
        for P in passes:
            emit_cwrow(0, P)
        for ei, (TW1, TW3, TW2, eidx, routed) in enumerate(experts):
            nw1c, nw3c, nw2, _ = sems_w[ei]
            for pi_, P in enumerate(passes):
                last = pi_ == npass - 1
                cpar = P["pi"]
                peG0, peU0, peY0 = s_peG.n, s_peU.n, s_peY.n
                emit_cwb(ei, P)
                if ei + 1 < len(experts):
                    emit_cwrow(ei + 1, P)
                cwb_ready = s_acF.n
                for tt in range(4 if _DBG_NTT is None else _DBG_NTT):
                    for hc in range(16):
                        if last and tt == 3 and hc == 8 and ei + 1 < len(experts):
                            nc4 = 4 if routed else 0
                            emit_w13_chunk(ei + 1, 0, (s_peG, peG0 + nc4 + 56),
                                           (s_peU, peU0 + 56))
                        gbk = hc % 2
                        ubk = 2 + hc % 2
                        xw = P["xw"] if (tt == 0 and hc == 0) else []
                        E("pe", lambda e, hcv=hc, t=tt, ps=PS[gbk], xnt=P["xnt"]: [
                            e.matmul(ps[0:128, 0:512],
                                     w1b.ap(0, 128, [[1, 128]], off=dc * 2048 + hcv * 128),
                                     xnt.ap(0, 128, [[1, 512]], off=dc * 2048 + t * 512),
                                     start=(dc == 0), stop=(dc == 7)) for dc in range(8)][-1],
                          waits=[(s_w1[hc // 8], nw1c[hc // 8]), psw(gbk)] + xw,
                          incs=[(s_peG, 1)])
                        pg = s_peG.n
                        if last and tt == 3 and hc == 7:
                            G.s(("w1c", 0), s_peG)
                        if last and tt == 3 and hc == 15:
                            G.s(("w1c", 1), s_peG)
                        E("pe", lambda e, hcv=hc, t=tt, ps=PS[ubk], xnt=P["xnt"]: [
                            e.matmul(ps[0:128, 0:512],
                                     w3b.ap(0, 128, [[1, 128]], off=dc * 2048 + hcv * 128),
                                     xnt.ap(0, 128, [[1, 512]], off=dc * 2048 + t * 512),
                                     start=(dc == 0), stop=(dc == 7)) for dc in range(8)][-1],
                          waits=[(s_w3[hc // 8], nw3c[hc // 8]), psw(ubk)] + xw,
                          incs=[(s_peU, 1)])
                        pu = s_peU.n
                        if last and tt == 3 and hc == 7:
                            G.s(("w3c", 0), s_peU)
                        if last and tt == 3 and hc == 15:
                            G.s(("w3c", 1), s_peU)
                        E("act", lambda e, hcv=hc, ps=PS[gbk]: e.activation(
                            gs_t[hcv % 2][0:128, 0:512], ps[0:128, 0:512], AF.Silu),
                          waits=[(s_peG, pg), G.w(("gsW", hc % 2))], incs=[(s_acF, 1)])
                        pss(gbk, s_acF)
                        if routed:
                            E("dve", lambda e, hcv=hc, ps=PS[ubk], t=tt, cp=cpar: e.scalar_tensor_tensor(
                                us_t[hcv % 2][0:128, 0:512], ps[0:128, 0:512], 1.0,
                                cwb.ap(0, 128, [[1, 512]], off=cp * 2048 + t * 512),
                                ALU.mult, ALU.mult),
                              waits=[(s_peU, pu), (s_acF, cwb_ready), G.w(("usW", hc % 2))],
                              incs=[(s_dvF, 1)])
                        else:
                            E("dve", lambda e, hcv=hc, ps=PS[ubk]: e.tensor_copy(
                                us_t[hcv % 2][0:128, 0:512], ps[0:128, 0:512]),
                              waits=[(s_peU, pu), G.w(("usW", hc % 2))], incs=[(s_dvF, 1)])
                        pss(ubk, s_dvF)
                        af, df = s_acF.n, s_dvF.n
                        E("pool", lambda e, hcv=hc: e.tensor_tensor(
                            eo_all[0:128, hcv * 512:(hcv + 1) * 512],
                            gs_t[hcv % 2][0:128, 0:512], us_t[hcv % 2][0:128, 0:512], ALU.mult),
                          waits=[(s_acF, af), (s_dvF, df), G.w(("eoW", hc))], incs=[(s_poF, 1)])
                        G.s(("gsW", hc % 2), s_poF)
                        G.s(("usW", hc % 2), s_poF)
                        if hc == 7:
                            po7 = s_poF.n
                    if last and tt == 3 and ei + 1 < len(experts):
                        nc4 = 4 if routed else 0
                        emit_w13_chunk(ei + 1, 1, (s_peG, peG0 + nc4 + 64),
                                       (s_peU, peU0 + 64))
                    po = s_poF.n
                    for i in range(4):
                        tc = tt * 4 + i
                        ybk = (4, 5) if i % 2 == 0 else (6, 7)
                        E("pe", lambda e, t=tc, ii=i, pp=(PS[ybk[0]], PS[ybk[1]]): [
                            e.matmul(pp[half][0:128, 0:512],
                                     eo_all[0:128, hc2 * 512 + ii * 128: hc2 * 512 + (ii + 1) * 128],
                                     w2b.ap(0, 128, [[1, 512]], off=hc2 * 1024 + half * 512),
                                     start=(hc2 == 0), stop=False)
                            for half in range(2) for hc2 in range(8)][-1],
                          waits=[(s_poF, po7), (s_w2, nw2), psw(ybk[0]), psw(ybk[1])],
                          incs=[(s_peY, 1)])
                        E("pe", lambda e, t=tc, ii=i, pp=(PS[ybk[0]], PS[ybk[1]]): [
                            e.matmul(pp[half][0:128, 0:512],
                                     eo_all[0:128, hc2 * 512 + ii * 128: hc2 * 512 + (ii + 1) * 128],
                                     w2b.ap(0, 128, [[1, 512]], off=hc2 * 1024 + half * 512),
                                     start=False, stop=(hc2 == 15))
                            for half in range(2) for hc2 in range(8, 16)][-1],
                          waits=[(s_poF, po)], incs=[(s_peY, 1)])
                        py = s_peY.n
                        E("act", lambda e, pp=(PS[ybk[0]], PS[ybk[1]]), px=px_t[i % 2]: [
                            e.activation(px[0:128, 0:512], pp[0][0:128, 0:512], AF.Identity),
                            e.activation(px[0:128, 512:1024], pp[1][0:128, 0:512],
                                         AF.Identity)][-1],
                          waits=[(s_peY, py), G.w(("pxW", i % 2))], incs=[(s_acY, 1)])
                        pss(ybk[0], s_acY)
                        pss(ybk[1], s_acY)
                        ay = s_acY.n
                        accv = 16 * (nfl + npass * ei + pi_)
                        E("pool", lambda e, t=tc, px=px_t[i % 2], sq=P["sq"]: e.dma_start(
                            bass.AP(OUT, (sq * S + t * 128) * D, [[D, 128], [1, D]]),
                            px[0:128, 0:1024], accum_op=ALU.add),
                          waits=[(s_acY, ay), (s_acc[tc], accv)], incs=[(s_acc[tc], 16)], dma=True)
                        G.m[("pxW", i % 2)] = (s_acc[tc], s_acc[tc].n)
                    for hc in range(16):
                        G.m[("eoW", hc)] = (s_peY, s_peY.n)
                if routed:
                    G.s(("cwbW", cpar), s_dvF)
                ncwb = 4 if routed else 0
                assert s_peG.n == peG0 + ncwb + 64, (s_peG.n, peG0, ncwb)
                assert s_peU.n == peU0 + 64
                assert s_peY.n == peY0 + 32
            G.s("w2", s_peY)
            if ei + 1 < len(experts):
                emit_w2(ei + 1, (s_peY, peY0 + 32))
        E("pool", lambda e: e.nop(),
          waits=[(s_acc[t2], s_acc[t2].n) for t2 in range(NTC)], incs=[(s_fl, 1)])
        done = (s_fl, s_fl.n)
        for key in ("awx", "rw", "pw", ("xntW", id(xnt_b)),
                    ("kkW", 0), ("kkW", 1), ("qrW", 0), ("qrW", 1),
                    ("xtil", 0), ("xtil", 1)):
            G.m[key] = done
        G.m["ck"] = (s_peY, s_peY.n)

    emit_setup()
    if mode == "ffn1":
        s_dbg = em.sem("dbg")
        s_dbgp = em.sem("dbgp")
        if _DBG_PRE >= 1:
            E("sp", lambda e: e.dma_start(xn2[0:128, 0:16384], dbg["xn2_in"].ap()),
              incs=[(s_dbg, 16)], dma=True)
            E("pool", lambda e: e.dma_start(cw_fm[0:8, 0:2048], dbg["cw_in"].ap()),
              incs=[(s_dbgp, 16)], dma=True)
        for tc in range(NTC if _DBG_PRE >= 2 else 0):
            E("sp", lambda e, t=tc: e.dma_start(
                px_t[0][0:128, 0:1024], dbg["x2_in"].ap()[0:128, t * 1024:(t + 1) * 1024]),
              waits=[(s_acc[tc - 1], s_acc[tc - 1].n)] if tc > 0 else [],
              incs=[(s_dbg, 16)], dma=True)
            nd = s_dbg.n
            E("pool", lambda e, t=tc: e.dma_start(
                bass.AP(OUT, (t * 128) * D, [[D, 128], [1, D]]), px_t[0][0:128, 0:1024]),
              waits=[(s_dbg, nd)], incs=[(s_acc[tc], 16)], dma=True)
        if _DBG_PRE >= 2:
            E("pool", lambda e: e.nop(),
              waits=[(s_acc[t2], s_acc[t2].n) for t2 in range(NTC)], incs=[(s_fl, 1)])
        E("sp", lambda e: e.nop(), waits=[(s_fl, s_fl.n), (s_dbgp, s_dbgp.n)])
        em.barrier(s_bar)
        emit_experts([dict(pi=0, sq=0, xnt=xn2, cw=cw_fm,
                           cw_wait=(s_cwA, s_cwA.n), xw=[])], s_fl.n)
        em.barrier(s_bar)
        E("sp", lambda e: e.nop(), waits=[(s_acc[i], s_acc[i].n) for i in range(NTC)])
    elif mode == "attn1":
        emit_attn(0)
        dvo = s_dvO.n
        for tc in range(NTC):
            E("sp", lambda e, t=tc: e.dma_start(
                bass.AP(dbg["x2_out"], t * 1024, [[16384, 128], [1, 1024]]),
                x2ap(t, 0, 1024)),
              waits=[(s_dvO, dvo)], incs=[(s_fl, 16)], dma=True)
        allw = [(s, s.n) for s in em.sems if s.n > 0]
        for sc, v in allw:
            em.pending["sp"].append((sc, v))
        E("sp", lambda e: e.dma_start(dbg["arena_out"].ap(), a32h[:, :]),
          incs=[(s_fl, 16)], dma=True)
        E("sp", lambda e: e.nop(), waits=[(s_fl, s_fl.n)])
    else:
        for seq in range(BPC):
            emit_attn(seq)
            emit_router()
            emit_experts([dict(pi=0, sq=seq, xnt=xn2, cw=cw_fm,
                               cw_wait=(s_cwA, s_cwA.n), xw=[])], s_fl.n)
            em.barrier(s_bar)
        E("sp", lambda e: e.nop(), waits=[(s_acc[i], s_acc[i].n) for i in range(NTC)])

    em.finalize(es)
    es.close()
    return nc


def _rope_tables():
    inv = 1.0 / 10000.0 ** (np.arange(0, HD, 2, dtype=np.float32) / HD)
    f = np.arange(S, dtype=np.float32)[:, None] * inv[None, :]
    sn = (np.sin(f) * CSC).astype(np.float32)
    cs = (np.cos(f) * CSC).astype(np.float32)
    sstab = np.zeros((128, NTC * 64), np.float32)
    cctab = np.zeros((128, NTC * 64), np.float32)
    for tc in range(NTC):
        t0 = tc * 128
        sstab[:, tc * 64:tc * 64 + 32] = sn[t0:t0 + 128, :]
        sstab[:, tc * 64 + 32:tc * 64 + 64] = -sn[t0:t0 + 128, :]
        cctab[:, tc * 64:tc * 64 + 32] = cs[t0:t0 + 128, :]
        cctab[:, tc * 64 + 32:tc * 64 + 64] = cs[t0:t0 + 128, :]
    return (np.ascontiguousarray(sstab.astype(ml_dtypes.bfloat16)),
            np.ascontiguousarray(cctab.astype(ml_dtypes.bfloat16)))


def host_inputs(inp):
    f32c = lambda a: np.ascontiguousarray(np.asarray(a, dtype=np.float32))
    cb = f32c(np.asarray(inp["L0_attn_conv_b"]).reshape(3, 8, 128).transpose(2, 0, 1)
              .reshape(128, 24))
    rb = f32c(np.asarray(inp["L0_attn_rff_b"]).reshape(3, 3, 2, 128).transpose(3, 0, 1, 2)
              .reshape(128, 18))
    pb = f32c(np.asarray(inp["L0_attn_proj_b"]).reshape(1, 3072))
    ob = f32c(np.asarray(inp["L0_attn_out_b"]).reshape(1, 1024))
    lmT = f32c(np.asarray(inp["L0_attn_landmarks"]).T)
    rwT = f32c(np.asarray(inp["L0_ffn_router_w"]).T.reshape(8, 128, 8).transpose(1, 0, 2)
               .reshape(128, 64))
    eb = f32c(np.asarray(inp["L0_ffn_experts_bias"]).reshape(8, 1))
    sstab, cctab = _rope_tables()
    return {
        "conv_k": f32c(inp["L0_attn_conv_k"]), "conv_b": cb,
        "rff_w": f32c(inp["L0_attn_rff_w"]), "rff_b": rb,
        "proj_w": f32c(inp["L0_attn_proj_w"]), "proj_b": pb,
        "omega": f32c(inp["L0_attn_omega"]), "lmT": lmT,
        "out_w": f32c(inp["L0_attn_out_w"]), "out_b": ob,
        "rwT": rwT, "expb": eb,
        "w1": f32c(inp["L0_ffn_w1"]), "w3": f32c(inp["L0_ffn_w3"]),
        "w2": f32c(inp["L0_ffn_w2"]),
        "sw1": f32c(inp["L0_ffn_sw1"]), "sw3": f32c(inp["L0_ffn_sw3"]),
        "sw2": f32c(inp["L0_ffn_sw2"]),
        "sstab": sstab, "cctab": cctab,
    }


def kernel(**inputs):
    base = host_inputs(inputs)
    x = np.asarray(inputs["L0_attn_x"], dtype=np.float32)
    nc = build_program("full")
    in_maps = []
    for c in range(NCORES):
        m = dict(base)
        m["x"] = np.ascontiguousarray(x[c * BPC:(c + 1) * BPC])
        in_maps.append(m)
    res = run_bass_kernel_spmd(nc, in_maps, core_ids=list(range(NCORES)))
    return np.concatenate([r["out"] for r in res.results], axis=0)


# version 23 (final)
# speedup vs baseline: 1.2211x; 1.0298x over previous
import math
import numpy as np
import ml_dtypes
import concourse.bass as bass
import concourse.mybir as mybir
from concourse.bass_utils import run_bass_kernel_spmd

F32 = mybir.dt.float32
BF16 = mybir.dt.bfloat16
AF = mybir.ActivationFunctionType
ALU = mybir.AluOpType
AX = mybir.AxisListType

B, S, D = 16, 2048, 1024
NCORES = 8
BPC = B // NCORES
NH, HD = 16, 64
NTC = 16
GAMMAS = (0.5, 1.0, 2.0)
CSC = float(np.float32(math.sqrt(2.0 / 256.0)))
LN16 = float(np.float32(math.log(16.0)))
MAGIC = 12582912.0
TWO_PI = float(np.float32(2 * math.pi))
SINSHRINK = float(np.float32(1.0 - 4e-6))
AWID = 52000
_DBG_NE = None
_DBG_NTT = None
_DBG_PRE = 2
_DBG_NP = 3
_DBG_NBLK = 4
_DBG_SERIAL = 0


class SemC:
    __slots__ = ("name", "h", "n", "per")

    def __init__(self, name):
        self.name, self.h, self.n = name, None, 0


class Em:
    STREAMS = ("pe", "act", "dve", "pool", "sp")

    def __init__(self, nc):
        self.nc = nc
        self.ops = {s: [] for s in self.STREAMS}
        self.sems = []
        self.dirty = {s: False for s in self.STREAMS}
        self.pending = {s: [] for s in self.STREAMS}

    def sem(self, name):
        s = SemC(name)
        self.sems.append(s)
        return s

    def op(self, stream, body, waits=(), incs=(), dma=False):
        w = self.pending[stream] + [x for x in waits if x is not None]
        self.pending[stream] = []
        for s, n in incs:
            s.n += n
        self.ops[stream].append([w, body, list(incs), dma])
        self.dirty[stream] = True

    def barrier(self, bar):
        if not hasattr(bar, "per"):
            bar.per = {s: self.sem(f"{bar.name}_{s}") for s in self.STREAMS}
        edges = []
        for s in self.STREAMS:
            if self.ops[s]:
                bs = bar.per[s]
                self.ops[s].append([[], lambda e: e.nop(), [(bs, 1)], False])
                bs.n += 1
            edges.append((bar.per[s], bar.per[s].n))
        for s in self.STREAMS:
            for e_ in edges:
                if e_[0] is not bar.per[s]:
                    self.pending[s].append(e_)

    def n_ops(self):
        return sum(len(v) for v in self.ops.values())

    def finalize(self, es):
        for s in self.sems:
            s.h = es.enter_context(self.nc.semaphore(s.name))
        with self.nc.Block() as blk:
            def runner(stream):
                def run(eng):
                    for waits, body, incs, _dma in self.ops[stream]:
                        for sc, v in waits:
                            if v > 0:
                                eng.wait_ge(sc.h, v)
                        inst = body(eng)
                        for sc, n in incs:
                            inst.then_inc(sc.h, n)
                return run
            blk.tensor(runner("pe"))
            blk.scalar(runner("act"))
            blk.vector(runner("dve"))
            blk.gpsimd(runner("pool"))
            blk.sync(runner("sp"))


class Guard:
    def __init__(self):
        self.m = {}

    def w(self, key):
        return self.m.get(key)

    def s(self, key, sem):
        self.m[key] = (sem, sem.n)


class Tile:
    __slots__ = ("h", "wid", "c0", "cols")

    def __init__(self, h, wid, c0, cols):
        self.h, self.wid, self.c0, self.cols = h, wid, c0, cols

    def ap(self, p0=0, np_=128, fdims=None, off=0):
        base = p0 * self.wid + self.c0 + off
        if fdims is None:
            fdims = [[1, self.cols]]
        return bass.AP(self.h, base, [[self.wid, np_]] + [list(d) for d in fdims])

    def __getitem__(self, idx):
        ps, cs = idx
        p0 = ps.start or 0
        p1 = ps.stop if ps.stop is not None else 128
        c0 = cs.start or 0
        c1 = cs.stop if cs.stop is not None else self.cols
        return self.ap(p0, p1 - p0, [[1, c1 - c0]], off=c0)


def build_program(mode="full"):
    nc = bass.Bass("TRN2", target_bir_lowering=False)

    X = nc.dram_tensor("x", [BPC, S, D], F32, kind="ExternalInput")
    CK = nc.dram_tensor("conv_k", [3, 3, 256, 1024], F32, kind="ExternalInput")
    CB = nc.dram_tensor("conv_b", [128, 24], F32, kind="ExternalInput")
    RW = nc.dram_tensor("rff_w", [3, 3, 1024, 256], F32, kind="ExternalInput")
    RB = nc.dram_tensor("rff_b", [128, 18], F32, kind="ExternalInput")
    PW = nc.dram_tensor("proj_w", [3, 768, 1024], F32, kind="ExternalInput")
    PB = nc.dram_tensor("proj_b", [1, 3072], F32, kind="ExternalInput")
    OM = nc.dram_tensor("omega", [64, 256], F32, kind="ExternalInput")
    LMT = nc.dram_tensor("lmT", [64, 64], F32, kind="ExternalInput")
    OW = nc.dram_tensor("out_w", [1024, 1024], F32, kind="ExternalInput")
    OB = nc.dram_tensor("out_b", [1, 1024], F32, kind="ExternalInput")
    RT = nc.dram_tensor("rwT", [128, 64], F32, kind="ExternalInput")
    EBT = nc.dram_tensor("expb", [8, 1], F32, kind="ExternalInput")
    W1 = nc.dram_tensor("w1", [8, 1024, 2048], F32, kind="ExternalInput")
    W3 = nc.dram_tensor("w3", [8, 1024, 2048], F32, kind="ExternalInput")
    W2 = nc.dram_tensor("w2", [8, 2048, 1024], F32, kind="ExternalInput")
    SW1 = nc.dram_tensor("sw1", [1, 1024, 2048], F32, kind="ExternalInput")
    SW3 = nc.dram_tensor("sw3", [1, 1024, 2048], F32, kind="ExternalInput")
    SW2 = nc.dram_tensor("sw2", [1, 2048, 1024], F32, kind="ExternalInput")
    SST = nc.dram_tensor("sstab", [128, 1024], BF16, kind="ExternalInput")
    CCT = nc.dram_tensor("cctab", [128, 1024], BF16, kind="ExternalInput")
    OUT = nc.dram_tensor("out", [BPC, S, D], F32, kind="ExternalOutput")
    dbg = {}
    if mode == "attn1":
        dbg["x2_out"] = nc.dram_tensor("x2_out", [128, 16384], F32, kind="ExternalOutput")
        dbg["arena_out"] = nc.dram_tensor("arena_out", [128, AWID], F32, kind="ExternalOutput")
    if mode == "ffn1":
        dbg["xn2_in"] = nc.dram_tensor("xn2_in", [128, 16384], BF16, kind="ExternalInput")
        dbg["cw_in"] = nc.dram_tensor("cw_in", [8, 2048], F32, kind="ExternalInput")
        dbg["x2_in"] = nc.dram_tensor("x2_in", [128, 16384], F32, kind="ExternalInput")

    from contextlib import ExitStack
    es = ExitStack()
    a32h = es.enter_context(nc.sbuf_tensor("arena", [128, AWID], F32))
    abfh = a32h.bitcast(BF16)
    PS = [es.enter_context(nc.psum_tensor(f"ps{i}", [128, 512], F32)) for i in range(8)]
    PSB = [p.bitcast(BF16) for p in PS]

    def T32(c0, cols):
        return Tile(a32h, AWID, c0, cols)

    def TBF(c0, cols):
        return Tile(abfh, 2 * AWID, 2 * c0, cols)

    off_ = [0]
    def alloc(n):
        c = off_[0]
        off_[0] += n
        assert off_[0] <= AWID, off_[0]
        return c
    sst = TBF(alloc(512), 1024)
    cct = TBF(alloc(512), 1024)
    ident = TBF(alloc(64), 128)
    aug_om = TBF(alloc(128), 256)
    om_dup = TBF(alloc(128), 256)
    mhalf = TBF(alloc(64), 128)
    nylhs = TBF(alloc(32), 64)
    biasny = T32(alloc(1), 1)
    bias1 = T32(alloc(18), 18)
    bias2 = T32(alloc(18), 18)
    convb = T32(alloc(24), 24)
    rwt_b = TBF(alloc(32), 64)
    expb = T32(alloc(1), 1)
    pbrow = TBF(alloc(1536), 3072)
    outbr = TBF(alloc(512), 1024)
    ones1 = TBF(alloc(64), 128)
    onecol = TBF(alloc(1), 1)
    bln16 = T32(alloc(1), 1)
    epsln = T32(alloc(1), 1)
    eps6 = T32(alloc(1), 1)
    m64t = T32(alloc(1), 1)
    small = T32(alloc(16), 16)
    rsc = T32(alloc(64), 64)
    eps65 = Tile(rsc.h, rsc.wid, rsc.c0 + 48, 1)
    c_xn1 = alloc(8200)
    c_ck = alloc(3072)
    c_rw = alloc(3072)
    c_pw = alloc(3072)
    c_q = alloc(8192)
    c_v = alloc(8192)
    c_kv = alloc(2080)
    c_ow = alloc(4096)
    c_tz = alloc(6656)
    c_fz = alloc(1536)
    assert off_[0] <= AWID, off_[0]

    xn1 = TBF(c_xn1, 16400)
    ckt = TBF(c_ck, 6144)
    rwt = TBF(c_rw, 6144)
    pwt = TBF(c_pw, 6144)
    q_fm = TBF(c_q, 16384)
    v_sb = TBF(c_v, 16384)
    attnpre = TBF(c_v, 16384)
    kvacc = T32(c_kv, 2048)
    ksum = T32(c_kv + 2048, 32)
    outw_t = TBF(c_ow, 8192)
    kvks = TBF(c_tz + 5120, 2080)
    x_til = [T32(c_tz + 1024 * i, 1024) for i in range(2)]
    xsq_t = T32(c_tz + 2048, 1024)
    xn_tm = TBF(c_tz + 3072, 1024)
    h_fm = TBF(c_tz + 3584, 4096)
    qr_tm = TBF(c_tz, 2048)
    qr_tm2 = TBF(c_tz + 1024, 2048)
    kk_tm2 = TBF(c_ow, 2048)
    xnt_b = TBF(c_ow + 1024, 1024)
    pbst = T32(c_tz, 3072)
    lmt_f = T32(c_tz + 3072, 64)
    lmsq = T32(c_tz + 3136, 64)
    feats = TBF(c_fz, 3072)
    t1_t = [T32(c_tz + 5632 + 512 * i, 512) for i in range(2)]
    kk_tm = TBF(c_tz + 0, 2048)
    kkfm = TBF(c_tz + 1024, 2048)
    kny = TBF(c_tz + 2048, 2048)
    kp_all = TBF(c_tz + 3072, 4096)
    qp_t = [TBF(c_tz + 1024 * i, 2048) for i in range(2)]
    qsq_t = [TBF(c_tz + 2048 + 256 * i, 512) for i in range(2)]
    nd_t = [TBF(c_tz + 2560 + 256 * i, 512) for i in range(3)]
    xb2 = [T32(c_tz + 4096, 1024), T32(c_tz + 5632, 1024)]
    xsq2 = T32(c_fz, 1024)
    xn_tm2 = TBF(c_fz + 1024, 1024)
    x2lo = T32(c_xn1, 12288)
    x2hi = T32(c_tz, 4096)

    def x2ap(tc, a, b):
        if tc < 12:
            return x2lo[0:128, tc * 1024 + a: tc * 1024 + b]
        return x2hi[0:128, (tc - 12) * 1024 + a: (tc - 12) * 1024 + b]
    w1b = TBF(c_xn1, 16384)
    w3b = TBF(c_q, 16384)
    eo_all = TBF(c_xn1 + 8192, 8192)
    gs_t = [TBF(c_xn1 + 12288 + 256 * i, 512) for i in range(2)]
    us_t = [TBF(c_xn1 + 12800 + 256 * i, 512) for i in range(2)]
    px_t = [T32(c_xn1 + 13312 + 1024 * i, 1024) for i in range(2)]
    cwrow = TBF(c_xn1 + 15360, 4096)
    cwb = TBF(c_kv, 4096)
    w2b = TBF(c_ow, 16384)
    xn2 = TBF(c_v, 16384)
    logits = TBF(c_fz, 2048)
    rB1 = T32(c_fz + 1024, 128)
    rB2 = T32(c_fz + 1152, 128)
    rB3 = T32(c_fz + 1280, 128)
    cw_tm = TBF(c_tz + 4096, 256)
    cw_fm = TBF(c_tz + 5632, 2048)

    import kernel as _K
    _K.LAYOUT = dict(c_xn1=c_xn1, c_ck=c_ck, c_rw=c_rw, c_pw=c_pw, c_q=c_q, c_v=c_v,
                     c_kv=c_kv, c_ow=c_ow, c_tz=c_tz, c_fz=c_fz)
    em = Em(nc)
    G = Guard()

    def E(stream, body, waits=(), incs=(), dma=False):
        em.op(stream, body, waits, incs, dma)

    s_ld = em.sem("ld"); s_ldp = em.sem("ldp"); s_bar = em.sem("bar")
    s_setP = em.sem("setP"); s_setA = em.sem("setA"); s_setD = em.sem("setD")
    s_setPE = em.sem("setPE")
    s_x = [em.sem(f"x{i}") for i in range(2)]
    s_awc = em.sem("awc"); s_awr = em.sem("awr"); s_awp = em.sem("awp"); s_ow = em.sem("ow")
    s_lnA = em.sem("lnA"); s_lnD = em.sem("lnD")
    s_peC = em.sem("peC"); s_acC = em.sem("acC")
    s_peR = em.sem("peR"); s_acR = em.sem("acR"); s_dvR = em.sem("dvR"); s_acS = em.sem("acS")
    s_peP = em.sem("peP"); s_dvRo = em.sem("dvRo"); s_acV = em.sem("acV")
    s_peT = em.sem("peT"); s_dvT = em.sem("dvT"); s_acT = em.sem("acT")
    s_peN = em.sem("peN"); s_acN = em.sem("acN")
    s_peK = em.sem("peK"); s_acK = em.sem("acK")
    s_peV = em.sem("peV"); s_dvK = em.sem("dvK")
    s_peB = em.sem("peB"); s_acB = em.sem("acB"); s_dvB = em.sem("dvB"); s_poB = em.sem("poB")
    s_x2 = [em.sem(f"xb{i}") for i in range(2)]
    s_peO = em.sem("peO"); s_dvO = em.sem("dvO")
    s_ln2A = em.sem("ln2A")
    s_lg = em.sem("lg"); s_lga = em.sem("lga")
    s_cwD = em.sem("cwD"); s_cwA = em.sem("cwA"); s_cwr = em.sem("cwr")
    s_w1 = [em.sem(f"w1_{i}") for i in range(2)]
    s_w3 = [em.sem(f"w3_{i}") for i in range(2)]
    s_w2 = em.sem("w2s")
    s_peG = em.sem("peG"); s_peU = em.sem("peU")
    s_acF = em.sem("acF"); s_dvF = em.sem("dvF"); s_poF = em.sem("poF")
    s_peY = em.sem("peY"); s_acY = em.sem("acY")
    s_acc = [em.sem(f"acc{i}") for i in range(NTC)]
    s_fl = em.sem("fl")
    s_sv = em.sem("sv"); s_rl = em.sem("rl")

    def psw(b):
        return G.w(("ps", b))

    def pss(b, sem):
        G.s(("ps", b), sem)

    def emit_setup():
        for dst, src in [
            (convb[0:128, 0:24], CB.ap()),
            (bias1[0:128, 0:18], RB.ap()),
            (lmt_f[0:64, 0:64], LMT.ap()),
            (expb[0:8, 0:1], EBT.ap()),
            (pbst[0:1, 0:3072], PB.ap()),
            (sst[0:128, 0:1024], SST.ap()),
            (cct[0:128, 0:1024], CCT.ap()),
        ]:
            E("sp", lambda e, d=dst, s=src: e.dma_start(d, s), incs=[(s_ld, 16)], dma=True)
        nsp = s_ld.n
        for dst, src in [
            (aug_om.ap(0, 64, [[1, 256]]), OM.ap()),
            (om_dup.ap(0, 64, [[1, 256]]), OM.ap()),
            (om_dup.ap(64, 64, [[1, 256]]), OM.ap()),
            (rwt_b[0:128, 0:64], RT.ap()),
            (outbr[0:1, 0:1024], OB.ap()),
        ]:
            E("pool", lambda e, d=dst, s=src: e.dma_start(d, s), incs=[(s_ldp, 16)], dma=True)
        npl = s_ldp.n

        def pool_sets(e):
            e.memset(ones1[0:128, 0:128], 1.0)
            e.memset(onecol[0:128, 0:1], 1.0)
            e.memset(mhalf[0:128, 0:128], -0.5)
            e.memset(aug_om[64:128, 0:256], -0.5)
            e.memset(nylhs[64:128, 0:64], -1.0 / 64.0)
            e.memset(bln16[0:128, 0:1], -LN16)
            e.memset(epsln[0:128, 0:1], 1e-5)
            e.memset(eps6[0:128, 0:1], 1e-6)
            e.memset(eps65[0:64, 0:1], 0.0)
            e.memset(eps65[64:65, 0:1], 1e-6)
            e.memset(m64t[0:128, 0:1], -1.0 / 64.0)
            e.memset(ident[0:128, 0:128], 0.0)
            return e.memset(xn1[0:128, 0:16400], 0.0)
        E("pool", pool_sets, incs=[(s_setP, 1)])
        E("pool", lambda e: e.affine_select(out=ident[0:128, 0:128],
                                            in_=ident[0:128, 0:128],
                                            compare_op=ALU.not_equal, fill=1.0, base=0,
                                            pattern=[[-1, 128]], channel_multiplier=1),
          waits=[(s_setP, s_setP.n)], incs=[(s_setP, 1)])
        E("act", lambda e: e.activation(nylhs[0:64, 0:64], lmt_f[0:64, 0:64], AF.Identity,
                                        scale=2.0 / 64.0),
          waits=[(s_ld, nsp)], incs=[(s_setA, 1)])
        E("dve", lambda e: e.tensor_tensor(lmsq[0:64, 0:64], lmt_f[0:64, 0:64],
                                           lmt_f[0:64, 0:64], ALU.mult),
          waits=[(s_ld, nsp)], incs=[(s_setD, 1)])
        E("pe", lambda e: e.matmul(PS[7][0:64, 0:1], lmsq[0:64, 0:64], m64t[0:64, 0:1],
                                   start=True, stop=True),
          waits=[(s_setP, 1), (s_setD, 1), (s_ldp, npl)], incs=[(s_setPE, 1)])
        E("act", lambda e: e.activation(biasny[0:64, 0:1], PS[7][0:64, 0:1], AF.Identity),
          waits=[(s_setPE, 1)], incs=[(s_setA, 1)])
        pss(7, s_setA)
        E("dve", lambda e: e.tensor_single_scalar(bias2[0:128, 0:18], bias1[0:128, 0:18],
                                                  math.pi / 2, ALU.add),
          waits=[(s_ld, nsp)], incs=[(s_setD, 1)])
        E("dve", lambda e: e.tensor_single_scalar(bias2[0:128, 0:18], bias2[0:128, 0:18],
                                                  SINSHRINK, ALU.mult),
          waits=[(s_setD, s_setD.n)], incs=[(s_setD, 1)])
        E("dve", lambda e: e.tensor_scalar(bias1[0:128, 0:18], bias1[0:128, 0:18],
                                           1.0 / TWO_PI, 0.25, ALU.mult, ALU.add),
          incs=[(s_setD, 1)])
        E("dve", lambda e: e.tensor_single_scalar(pbrow[0:1, 0:3072], pbst[0:1, 0:3072],
                                                  1.0 / CSC, ALU.mult),
          waits=[(s_ldp, npl), (s_ld, nsp)], incs=[(s_setD, 1)])
        em.barrier(s_bar)
        setw = [(s_setP, s_setP.n), (s_setA, s_setA.n), (s_setD, s_setD.n),
                (s_setPE, s_setPE.n), (s_ld, s_ld.n), (s_ldp, s_ldp.n)]
        for st_ in ("pe", "act", "dve", "pool", "sp"):
            E(st_, lambda e: e.nop(), waits=setw)
        G.m[("xtil", 0)] = (s_setD, s_setD.n)
        G.m[("xtil", 1)] = (s_setD, s_setD.n)

    def emit_ln_tc(src_fn, dst_fm, dst_stride, dst_pad, tc, sem_done, xsq=None, xnt=None):
        xnt = xnt if xnt is not None else xn_tm
        dv1 = emit_ln_stats(src_fn, xsq, xnt)
        emit_ln_tp(dst_fm, dst_stride, dst_pad, tc, sem_done, xnt, dv1)

    def emit_ln_stats(src_fn, xsq=None, xnt=None):
        st = small
        xsq = xsq if xsq is not None else xsq_t
        xnt = xnt if xnt is not None else xn_tm
        E("dve", lambda e: e.reduce_sum(st[0:128, 0:1], src_fn(0, 1024), AX.X),
          incs=[(s_lnD, 1)])
        E("dve", lambda e: e.tensor_tensor(xsq[0:128, 0:1024], src_fn(0, 1024),
                                           src_fn(0, 1024), ALU.mult), incs=[(s_lnD, 1)])
        E("dve", lambda e: e.reduce_sum(st[0:128, 1:2], xsq[0:128, 0:1024], AX.X),
          waits=[(s_lnD, s_lnD.n)], incs=[(s_lnD, 1)])
        E("dve", lambda e: e.tensor_scalar(st[0:128, 2:3], st[0:128, 0:1], 1.0 / D, None,
                                           ALU.mult),
          waits=[(s_lnD, s_lnD.n)], incs=[(s_lnD, 1)])
        E("dve", lambda e: e.tensor_tensor(st[0:128, 3:4], st[0:128, 2:3], st[0:128, 2:3],
                                           ALU.mult),
          waits=[(s_lnD, s_lnD.n)], incs=[(s_lnD, 1)])
        E("dve", lambda e: e.scalar_tensor_tensor(st[0:128, 4:5], st[0:128, 1:2], 1.0 / D,
                                                  st[0:128, 3:4], ALU.mult, ALU.subtract),
          waits=[(s_lnD, s_lnD.n)], incs=[(s_lnD, 1)])
        dv0 = s_lnD.n
        E("act", lambda e: e.activation(st[0:128, 5:6], st[0:128, 4:5], AF.Ln,
                                        bias=epsln[0:128, 0:1]),
          waits=[(s_lnD, dv0)], incs=[(s_lnA, 1)])
        E("act", lambda e: e.activation(st[0:128, 6:7], st[0:128, 5:6], AF.Exp, scale=-0.5),
          waits=[(s_lnA, s_lnA.n)], incs=[(s_lnA, 1)])
        ac0 = s_lnA.n
        E("dve", lambda e: e.tensor_scalar(xnt[0:128, 0:1024], src_fn(0, 1024),
                                           st[0:128, 2:3], st[0:128, 6:7], ALU.subtract, ALU.mult),
          waits=[(s_lnA, ac0), G.w(("xntW", id(xnt)))], incs=[(s_lnD, 1)])
        return s_lnD.n

    def emit_ln_tp(dst_fm, dst_stride, dst_pad, tc, sem_done, xnt, dv1):
        for half in range(2):
            bk = 6 + half
            E("pe", lambda e, h=half, p=PSB[bk]: [
                e.transpose(p[0:128, 128 * i:128 * (i + 1)],
                            xnt[0:128, (h * 4 + i) * 128:(h * 4 + i + 1) * 128],
                            ident[0:128, 0:128]) for i in range(4)][-1],
              waits=[(s_lnD, dv1), psw(bk)], incs=[(s_peT, 1)])
            if half == 1:
                G.s(("xntW", id(xnt)), s_peT)
            pt = s_peT.n
            E("act", lambda e, h=half, p=PSB[bk]: e.activation(
                dst_fm.ap(0, 128, [[dst_stride, 4], [1, 128]],
                          off=(h * 4) * dst_stride + dst_pad + tc * 128),
                p[0:128, 0:512], AF.Identity),
              waits=[(s_peT, pt)], incs=[(sem_done, 1)])
            pss(bk, sem_done)

    ln_dv1 = {}

    def emit_ln1_stats(seq, tc):
        par = tc % 2
        E("sp", lambda e, t=tc, p=par: e.dma_start(
            x_til[p][0:128, 0:1024],
            bass.AP(X, (seq * S + t * 128) * D, [[D, 128], [1, D]])),
          waits=[G.w(("xtil", par))], incs=[(s_x[par], 16)], dma=True)
        nx = s_x[par].n
        em.pending["dve"].append((s_x[par], nx))
        ln_dv1[tc] = emit_ln_stats(lambda a, b, p=par: x_til[p][0:128, a:b],
                                   xnt=[xn_tm, xnt_b][par])
        G.s(("xtil", par), s_lnD)

    def emit_ln1_tp(tc):
        emit_ln_tp(xn1, 2050, 1, tc, s_lnA, [xn_tm, xnt_b][tc % 2], ln_dv1[tc])

    def emit_ln1_tc(seq, tc):
        emit_ln1_stats(seq, tc)
        emit_ln1_tp(tc)

    def emit_aw(p):
        E("pool", lambda e, pp=p: e.dma_start(
            ckt.ap(0, 128, [[2048, 3], [1024, 2], [1, 1024]]),
            bass.AP(CK, pp * 3 * 256 * 1024,
                    [[1024, 128], [256 * 1024, 3], [128 * 1024, 2], [1, 1024]])),
          waits=[G.w("ck")], incs=[(s_awc, 16)], dma=True)
        E("pool", lambda e, pp=p: e.dma_start(
            rwt.ap(0, 128, [[2048, 3], [256, 8], [1, 256]]),
            bass.AP(RW, pp * 3 * 1024 * 256,
                    [[256, 128], [1024 * 256, 3], [128 * 256, 8], [1, 256]])),
          waits=[G.w("rw"), G.w("awx")], incs=[(s_awr, 16)], dma=True)
        E("pool", lambda e, pp=p: e.dma_start(
            pwt.ap(0, 128, [[1024, 6], [1, 1024]]),
            bass.AP(PW, pp * 768 * 1024, [[1024, 128], [128 * 1024, 6], [1, 1024]])),
          waits=[G.w("pw"), G.w("awx")], incs=[(s_awp, 16)], dma=True)
        return s_awc.n, s_awr.n, s_awp.n

    def emit_conv_group(p, blk, oc, lnA_need, awc_n):
        bk = oc % 2
        gco = oc // 2
        E("pe", lambda e, oc=oc, g=gco, ps=PS[bk], b=blk: [
            e.matmul(ps[0:128, 0:512],
                     ckt.ap(0, 128, [[1, 128]], off=w * 2048 + cic * 1024 + oc * 128),
                     xn1.ap(0, 128, [[1, 512]], off=(g * 2 + cic) * 2050 + w + b * 512),
                     start=(w == 0 and cic == 0), stop=(w == 2 and cic == 1))
            for w in range(3) for cic in range(2)][-1],
          waits=[(s_awc, awc_n), (s_lnA, lnA_need), G.w("pads"), psw(bk)],
          incs=[(s_peC, 1)])
        pc = s_peC.n
        E("act", lambda e, oc=oc, ps=PS[bk], pp=p: e.activation(
            h_fm[0:128, oc * 512:(oc + 1) * 512], ps[0:128, 0:512], AF.Identity,
            bias=convb[0:128, pp * 8 + oc:pp * 8 + oc + 1]),
          waits=[(s_peC, pc), G.w("hW")], incs=[(s_acC, 1)])
        pss(bk, s_acC)

    def emit_convrff(p, blk, lnA_need, awc_n, awr_n):
        for oc in range(8):
            emit_conv_group(p, blk, oc, lnA_need, awc_n)
        acC_blk = s_acC.n
        emit_rff(p, acC_blk, awr_n)

    def emit_rff(p, acC_blk, awr_n):
        for gi in range(3):
            spc = float(np.float32(math.sqrt(2 * GAMMAS[gi]) / TWO_PI))
            for mc in range(2):
                idx = gi * 2 + mc
                bk = 2 + idx % 2
                t1 = t1_t[idx % 2]
                dd = t1
                E("pe", lambda e, gi=gi, mc=mc, ps=PS[bk]: [
                    e.matmul(ps[0:128, 0:512],
                             rwt.ap(0, 128, [[1, 128]], off=gi * 2048 + dc * 256 + mc * 128),
                             h_fm[0:128, dc * 512:(dc + 1) * 512],
                             start=(dc == 0), stop=(dc == 7)) for dc in range(8)][-1],
                  waits=[(s_awr, awr_n), (s_acC, acC_blk), psw(bk)], incs=[(s_peR, 1)])
                pr = s_peR.n
                bcol = p * 6 + idx
                E("act", lambda e, ps=PS[bk], t1=t1, sc=spc, bc=bcol: e.activation(
                    t1[0:128, 0:512], ps[0:128, 0:512], AF.Identity,
                    bias=bias1[0:128, bc:bc + 1], scale=sc),
                  waits=[(s_peR, pr), G.w(("t1", idx % 2))], incs=[(s_acR, 1)])
                ar = s_acR.n
                E("dve", lambda e, t1=t1: e.tensor_single_scalar(
                    t1[0:128, 0:512], t1[0:128, 0:512], MAGIC, ALU.add),
                  waits=[(s_acR, ar)], incs=[(s_dvR, 1)])
                E("dve", lambda e, t1=t1: e.tensor_single_scalar(
                    t1[0:128, 0:512], t1[0:128, 0:512], MAGIC, ALU.subtract),
                  waits=[(s_dvR, s_dvR.n)], incs=[(s_dvR, 1)])
                E("dve", lambda e, ps=PS[bk], t1=t1, dd=dd, sc=spc: e.scalar_tensor_tensor(
                    dd[0:128, 0:512], ps[0:128, 0:512], sc, t1[0:128, 0:512],
                    ALU.mult, ALU.subtract),
                  waits=[(s_dvR, s_dvR.n)], incs=[(s_dvR, 1)])
                pss(bk, s_dvR)
                G.s(("t1", idx % 2), s_dvR)
                dr = s_dvR.n
                E("act", lambda e, dd=dd, idx=idx, bc=bcol: e.activation(
                    feats[0:128, idx * 512:(idx + 1) * 512], dd[0:128, 0:512], AF.Sin,
                    bias=bias2[0:128, bc:bc + 1], scale=float(TWO_PI * SINSHRINK)),
                  waits=[(s_dvR, dr), G.w("featsW")], incs=[(s_acS, 1)])
                G.s(("t1", idx % 2), s_acS)
        G.s("hW", s_peR)

    def emit_projmm(p, i, awp_n, acs_n, bks=(4, 5)):
        E("pe", lambda e, i=i, p=p, bb=bks: [
            y for half in range(2) for y in (
                [e.matmul(PS[bb[half]][0:128, 0:512],
                          feats[0:128, fc * 512 + i * 128: fc * 512 + (i + 1) * 128],
                          pwt.ap(0, 128, [[1, 512]], off=fc * 1024 + half * 512),
                          start=(fc == 0), stop=False) for fc in range(6)] +
                [e.matmul(PS[bb[half]][0:128, 0:512], ones1[0:1, 0:128],
                          pbrow[0:1, p * 1024 + half * 512: p * 1024 + (half + 1) * 512],
                          start=False, stop=True)])][-1],
          waits=[(s_acS, acs_n), (s_awp, awp_n), psw(bks[0]), psw(bks[1])], incs=[(s_peP, 1)])
        return s_peP.n

    def emit_v_tc(tc, ppn):
        for half in range(2):
            w = [(s_peP, ppn)]
            if tc == 0 and half == 0:
                w.append(G.w("xn2sv"))
            E("act", lambda e, h=half, t=tc: e.activation(
                v_sb[0:128, t * 1024 + h * 512: t * 1024 + (h + 1) * 512],
                PS[4 + h][0:128, 0:512], AF.Identity, scale=CSC),
              waits=w, incs=[(s_acV, 1)])
            pss(4 + half, s_acV)

    def emit_rope(tc, dest, kind, ppn, destW, bks=(4, 5)):
        for half in range(2):
            ps = PS[bks[half]]
            tmp = t1_t[half]
            sstap = sst.ap(0, 128, [[0, 8], [1, 64]], off=tc * 64)
            cctap = cct.ap(0, 128, [[0, 8], [1, 64]], off=tc * 64)
            E("dve", lambda e, ps=ps, tmp=tmp, ap=sstap: e.tensor_tensor(
                tmp[0:128, 0:512], ps[0:128, 0:512], ap, ALU.mult),
              waits=[(s_peP, ppn), G.w(("t1", half))], incs=[(s_dvRo, 1)])
            if kind == "q":
                outap = dest.ap(0, 128, [[1, 512]], off=half * 512)
                outsw = dest.ap(0, 128, [[64, 8], [-32, 2], [1, 32]], off=half * 512 + 32)
            else:
                outap = dest.ap(0, 128, [[128, 8], [1, 64]], off=half * 8 * 128)
                outsw = dest.ap(0, 128, [[128, 8], [-32, 2], [1, 32]], off=half * 8 * 128 + 32)
            E("dve", lambda e, ps=ps, oa=outap, ap=cctap: e.tensor_tensor(
                oa, ps[0:128, 0:512], ap, ALU.mult),
              waits=[G.w(destW) if half == 0 else None], incs=[(s_dvRo, 1)])
            tmpsw = tmp.ap(0, 128, [[64, 8], [-32, 2], [1, 32]], off=32)
            E("dve", lambda e, osw=outsw, tsw=tmpsw: e.tensor_tensor(osw, osw, tsw, ALU.add),
              waits=[(s_dvRo, s_dvRo.n)], incs=[(s_dvRo, 1)])
            G.s(("t1", half), s_dvRo)
            pss(bks[half], s_dvRo)

    def emit_q_tc(tc, ppn):
        emit_rope(tc, qr_tm, "q", ppn, "qrW")
        dro = s_dvRo.n
        E("pe", lambda e: [
            e.transpose(PSB[6][0:128, dc * 128:(dc + 1) * 128],
                        qr_tm[0:128, dc * 128:(dc + 1) * 128], ident[0:128, 0:128])
            for dc in range(8)][-1],
          waits=[(s_dvRo, dro), psw(6)], incs=[(s_peT, 1)])
        G.s("qrW", s_peT)
        pt = s_peT.n
        E("dve", lambda e, t=tc: e.tensor_copy(
            q_fm.ap(0, 128, [[2048, 8], [1, 128]], off=t * 128), PSB[6][0:128, 0:1024]),
          waits=[(s_peT, pt)], incs=[(s_dvT, 1)])
        pss(6, s_dvT)

    rope_done = {}

    def emit_q_rope(tc, ppn):
        par = tc % 2
        emit_rope(tc, [qr_tm, qr_tm2][par], "q", ppn, ("qrW", par))
        rope_done[tc] = s_dvRo.n

    def emit_q_tp(tc):
        par = tc % 2
        bk = 6 + par
        qrb = [qr_tm, qr_tm2][par]
        E("pe", lambda e, b=bk, q=qrb: [
            e.transpose(PSB[b][0:128, dc * 128:(dc + 1) * 128],
                        q[0:128, dc * 128:(dc + 1) * 128], ident[0:128, 0:128])
            for dc in range(8)][-1],
          waits=[(s_dvRo, rope_done[tc]), psw(bk)], incs=[(s_peT, 1)])
        G.s(("qrW", par), s_peT)
        pt = s_peT.n
        E("act", lambda e, t=tc, b=bk: e.activation(
            q_fm.ap(0, 128, [[2048, 8], [1, 128]], off=t * 128),
            PSB[b][0:128, 0:1024], AF.Identity),
          waits=[(s_peT, pt)], incs=[(s_acT, 1)])
        pss(bk, s_acT)

    def emit_k_rope(tc, ppn, bks=(4, 5)):
        par = tc % 2
        kkb = [kk_tm, kk_tm2][par]
        emit_rope(tc, kkb, "k", ppn, ("kkW", par), bks)
        kin = kkb.ap(0, 128, [[128, 16], [1, 64]], off=0)
        kout = kkb.ap(0, 128, [[128, 16], [1, 64]], off=64)
        E("dve", lambda e, ki=kin, ko=kout: e.tensor_tensor(ko, ki, ki, ALU.mult),
          waits=[(s_dvRo, s_dvRo.n)], incs=[(s_dvRo, 1)])
        rope_done[tc] = s_dvRo.n

    def emit_k_rest(tc):
        par = tc % 2
        kkb = [kk_tm, kk_tm2][par]
        tbks = (4, 5) if par == 0 else (2, 3)
        dro = rope_done[tc]
        dvt_h = []
        for half in range(2):
            bk = tbks[half]
            E("pe", lambda e, h=half, p=PSB[bk], kb=kkb: [
                e.transpose(p[0:128, j * 128:(j + 1) * 128],
                            kb[0:128, (h * 8 + j) * 128:(h * 8 + j + 1) * 128],
                            ident[0:128, 0:128]) for j in range(8)][-1],
              waits=[(s_dvRo, dro), psw(bk)], incs=[(s_peT, 1)])
            pt = s_peT.n
            E("dve", lambda e, h=half, p=PSB[bk]: e.tensor_copy(
                kkfm[0:128, h * 1024:(h + 1) * 1024], p[0:128, 0:1024]),
              waits=[(s_peT, pt), G.w("kkfmW") if half == 0 else None],
              incs=[(s_dvT, 1)])
            dvt_h.append(s_dvT.n)
            pss(bk, s_dvT)
        G.s(("kkW", par), s_peT)
        dvt1 = s_dvT.n
        for grp in range(4):
            bk = 6 + grp % 2
            E("pe", lambda e, g=grp, ps=PS[bk]: [
                e.matmul(ps[0:64, j * 128:(j + 1) * 128], nylhs[0:128, 0:64],
                         kkfm[0:128, (g * 4 + j) * 128:(g * 4 + j + 1) * 128],
                         start=True, stop=True) for j in range(4)][-1],
              waits=[(s_dvT, dvt_h[0] if grp < 2 else dvt1), psw(bk)], incs=[(s_peN, 1)])
            pn = s_peN.n
            E("act", lambda e, g=grp, ps=PS[bk]: e.activation(
                kny.ap(0, 64, [[1, 512]], off=g * 512), ps[0:64, 0:512], AF.Exp,
                bias=biasny[0:64, 0:1]),
              waits=[(s_peN, pn), G.w("knyW")], incs=[(s_acN, 1)])
            pss(bk, s_acN)
        G.s("kkfmW", s_peN)
        acn1 = s_acN.n
        E("dve", lambda e: e.tensor_tensor(kny.ap(64, 64, [[1, 2048]]),
                                           kny.ap(0, 64, [[1, 2048]]),
                                           kny.ap(0, 64, [[1, 2048]]), ALU.mult),
          waits=[(s_acN, acn1)], incs=[(s_dvT, 1)])
        dvt2 = s_dvT.n
        for pr in range(8):
            bk = tbks[pr % 2]
            E("pe", lambda e, pr=pr, ps=PS[bk]: [
                e.matmul(ps[0:128, j * 256:(j + 1) * 256],
                         kny[0:128, (pr * 2 + j) * 128:(pr * 2 + j + 1) * 128],
                         aug_om[0:128, 0:256], start=True, stop=True) for j in range(2)][-1],
              waits=[(s_dvT, dvt2), psw(bk)], incs=[(s_peK, 1)])
            pk = s_peK.n
            E("act", lambda e, pr=pr, ps=PS[bk]: e.activation(
                kp_all[0:128, pr * 512:(pr + 1) * 512], ps[0:128, 0:512], AF.Exp,
                bias=bln16[0:128, 0:1]),
              waits=[(s_peK, pk), G.w("kpW")], incs=[(s_acK, 1)])
            pss(bk, s_acK)
        G.s("knyW", s_peK)
        ack = s_acK.n
        acv = s_acV.n
        E("pe", lambda e, t=tc: [
            e.matmul(PS[(h * 2 + ec) // 8][0:128,
                     ((h * 2 + ec) % 8) * 64:((h * 2 + ec) % 8) * 64 + 64],
                     kp_all[0:128, h * 256 + ec * 128: h * 256 + (ec + 1) * 128],
                     v_sb[0:128, t * 1024 + h * 64: t * 1024 + (h + 1) * 64],
                     start=True, stop=True)
            for h in range(16) for ec in range(2)][-1],
          waits=[(s_acK, ack), (s_acV, acv), psw(0), psw(1), psw(2), psw(3)],
          incs=[(s_peV, 1)])
        E("pe", lambda e: [
            e.matmul(PS[6][0:128, sl:sl + 1],
                     kp_all[0:128, (sl // 2) * 256 + (sl % 2) * 128:
                            (sl // 2) * 256 + (sl % 2 + 1) * 128],
                     onecol[0:128, 0:1], start=True, stop=True) for sl in range(32)][-1],
          waits=[psw(6)], incs=[(s_peV, 1)])
        G.s("kpW", s_peV)
        pv = s_peV.n
        first = (tc == 0)
        for b_ in range(4):
            if first:
                E("dve", lambda e, b=b_: e.tensor_copy(kvacc[0:128, b * 512:(b + 1) * 512],
                                                       PS[b][0:128, 0:512]),
                  waits=[(s_peV, pv)])
            else:
                E("dve", lambda e, b=b_: e.tensor_tensor(kvacc[0:128, b * 512:(b + 1) * 512],
                                                         kvacc[0:128, b * 512:(b + 1) * 512],
                                                         PS[b][0:128, 0:512], ALU.add),
                  waits=[(s_peV, pv)])
        if first:
            E("dve", lambda e: e.tensor_copy(ksum[0:128, 0:32], PS[6][0:128, 0:32]),
              incs=[(s_dvK, 1)])
        else:
            E("dve", lambda e: e.tensor_tensor(ksum[0:128, 0:32], ksum[0:128, 0:32],
                                               PS[6][0:128, 0:32], ALU.add), incs=[(s_dvK, 1)])
        for b_ in range(4):
            pss(b_, s_dvK)
        pss(6, s_dvK)

    def emit_k_tc(tc, ppn):
        emit_k_rope(tc, ppn)
        emit_k_rest(tc)

    def emit_attn(seq):
        ln_done = {}
        dvo_done = {}
        if seq > 0:
            E("pool", lambda e: [
                e.memset(xn1.ap(0, 128, [[2050, 8], [1, 1]], off=0), 0.0),
                e.memset(xn1.ap(0, 128, [[2050, 8], [1, 1]], off=2049), 0.0)][-1],
              incs=[(s_setP, 1)])
        G.m["pads"] = (s_setP, s_setP.n)
        awc_n, awr_n, awp_n = emit_aw(2)
        for tc in range(4):
            emit_ln1_tc(seq, tc)
            ln_done[tc] = s_lnA.n
        for p_i, (p, kind) in enumerate([(2, "v"), (0, "q"), (1, "k")][:_DBG_NP]):
            if p_i > 0:
                awc_n, awr_n, awp_n = emit_aw(p)
            if kind in ("v", "q") and _DBG_NBLK == 4 and _DBG_SERIAL == 0:
                if p_i == 0:
                    emit_ln1_stats(seq, 4)
                    emit_ln1_stats(seq, 5)
                    emit_ln1_tp(4)
                    ln_done[4] = s_lnA.n
                    pre_sq = [6, 7]
                    pre_tp = [5, 6, 7]
                else:
                    pre_sq = []
                    pre_tp = []
                for oc in range(8):
                    emit_conv_group(p, 0, oc, ln_done[4], awc_n)
                    if oc % 2 == 1:
                        if pre_sq:
                            emit_ln1_stats(seq, pre_sq.pop(0))
                        if pre_tp:
                            t_ = pre_tp.pop(0)
                            emit_ln1_tp(t_)
                            ln_done[t_] = s_lnA.n
                conv_done = s_acC.n
                ln_q = []
                ln_sq = []
                for blk in range(4):
                    emit_rff(p, conv_done, awr_n)
                    if p_i == 0 and blk < 2:
                        t0 = 4 * blk + 8
                        emit_ln1_stats(seq, t0)
                        emit_ln1_stats(seq, t0 + 1)
                        emit_ln1_tp(t0)
                        ln_done[t0] = s_lnA.n
                        ln_sq = [t0 + 2, t0 + 3]
                        ln_q = [t0 + 1, t0 + 2, t0 + 3]
                    if blk + 1 < 4:
                        lnn = ln_done[min(4 * (blk + 1) + 4, NTC - 1)]
                        emit_conv_group(p, blk + 1, 0, lnn, awc_n)
                        emit_conv_group(p, blk + 1, 1, lnn, awc_n)
                    if kind == "q" and blk > 0:
                        emit_q_tp(blk * 4 - 1)
                    acs = s_acS.n
                    for i in range(4):
                        tc = blk * 4 + i
                        ppn = emit_projmm(p, i, awp_n, acs)
                        if kind == "v":
                            emit_v_tc(tc, ppn)
                        else:
                            emit_q_rope(tc, ppn)
                        if ln_sq:
                            emit_ln1_stats(seq, ln_sq.pop(0))
                        if blk + 1 < 4 and i < 3:
                            lnn = ln_done[min(4 * (blk + 1) + 4, NTC - 1)]
                            emit_conv_group(p, blk + 1, 2 + 2 * i, lnn, awc_n)
                            emit_conv_group(p, blk + 1, 3 + 2 * i, lnn, awc_n)
                        if ln_q:
                            t_ = ln_q.pop(0)
                            emit_ln1_tp(t_)
                            ln_done[t_] = s_lnA.n
                        if kind == "q" and i > 0:
                            emit_q_tp(tc - 1)
                    if blk + 1 < 4:
                        conv_done = s_acC.n
                    G.s("featsW", s_peP)
                if kind == "q":
                    emit_q_tp(NTC - 1)
            elif kind == "k" and _DBG_NBLK == 4 and _DBG_SERIAL == 0:
                for blk in range(4):
                    emit_convrff(p, blk, ln_done[min(4 * blk + 4, NTC - 1)],
                                 awc_n, awr_n)
                    if blk > 0:
                        emit_k_rest(blk * 4 - 1)
                    acs = s_acS.n
                    for i in range(4):
                        tc = blk * 4 + i
                        bks = (4, 5) if i % 2 == 0 else (2, 3)
                        ppn = emit_projmm(p, i, awp_n, acs, bks)
                        emit_k_rope(tc, ppn, bks)
                        if i > 0:
                            emit_k_rest(tc - 1)
                    G.s("featsW", s_peP)
                emit_k_rest(NTC - 1)
            else:
                for blk in range(_DBG_NBLK):
                    if p_i == 0:
                        for tc in range(4 * blk + 4, min(4 * blk + 8, NTC)):
                            emit_ln1_tc(seq, tc)
                            ln_done[tc] = s_lnA.n
                    lnA_need = ln_done[min(4 * blk + 4, NTC - 1)]
                    emit_convrff(p, blk, lnA_need, awc_n, awr_n)
                    if _DBG_SERIAL in (1, 2):
                        em.barrier(s_bar)
                    acs = s_acS.n
                    for i in range(4):
                        tc = blk * 4 + i
                        ppn = emit_projmm(p, i, awp_n, acs)
                        if kind == "v":
                            emit_v_tc(tc, ppn)
                        elif kind == "q":
                            emit_q_tc(tc, ppn)
                        else:
                            emit_k_tc(tc, ppn)
                        if _DBG_SERIAL in (1, 3):
                            em.barrier(s_bar)
                    G.s("featsW", s_peP)
            if kind == "v":
                G.s("ck", s_peC)
                G.s("rw", s_peR)
                G.s("pw", s_peP)
            elif kind == "q":
                G.s("ck", s_peC)
                G.s("rw", s_peR)
                G.s("pw", s_peP)
        E("pool", lambda e: e.dma_start(
            outw_t.ap(0, 128, [[1024, 8], [1, 1024]]),
            bass.AP(OW, 0, [[1024, 128], [128 * 1024, 8], [1, 1024]])),
          waits=[(s_peT, s_peT.n)], incs=[(s_ow, 16)], dma=True)
        ow_n = s_ow.n
        if _DBG_NP < 3:
            return
        E("dve", lambda e: e.tensor_copy(
            kvks.ap(0, 128, [[65, 32], [1, 64]], off=0),
            kvacc.ap(0, 128, [[64, 32], [1, 64]], off=0)),
          waits=[(s_dvK, s_dvK.n), G.w("cwsv")], incs=[(s_dvB, 1)])
        E("dve", lambda e: e.tensor_copy(
            kvks.ap(0, 128, [[65, 32], [1, 1]], off=64),
            ksum.ap(0, 128, [[1, 32]], off=0)), incs=[(s_dvB, 1)])
        kvready = s_dvB.n
        pev_all = s_peV.n
        attnpre_done = {}
        iters = [(h, tt) for h in range(NH) for tt in range(4)]
        NIT = len(iters)
        st_fav = {}
        st_nd = {}
        st_rbc = {}
        st_exp = {}
        st_evac = {}
        st_recip = {}
        st_mult = {}
        st_qsq = {}

        def q_slice(i):
            h, tt = iters[i]
            return q_fm.ap((h % 2) * 64, 64, [[1, 512]], off=(h // 2) * 2048 + tt * 512)

        qf_ready = s_acT.n

        def emit_qsq(i):
            par = i % 2
            bb = (iters[i][0] % 2) * 64
            w = ([(s_peB, st_fav[i - 2])] if i - 2 in st_fav
                 else [(s_peV, pev_all), (s_acT, qf_ready)])
            E("pool", lambda e, qs=q_slice(i), p=par, b=bb: e.tensor_tensor(
                qsq_t[p].ap(b, 64, [[1, 512]]), qs, qs, ALU.mult), waits=w, incs=[(s_poB, 1)])
            st_qsq[i] = s_poB.n

        def emit_fav(i):
            par = i % 2
            h, tt = iters[i]
            bb = (h % 2) * 64
            w = [(s_poB, st_qsq[i])]
            if i - 2 in st_exp:
                w.append((s_acB, st_exp[i - 2]))
            else:
                w += [psw(2 * par), psw(2 * par + 1)]
            E("pe", lambda e, qs=q_slice(i), b=bb, p=par: [
                y for ec in range(2) for y in (
                    e.matmul(PS[2 * p + ec][0:128, 0:512],
                             om_dup.ap(b, 64, [[1, 128]], off=ec * 128), qs,
                             start=True, stop=False),
                    e.matmul(PS[2 * p + ec][0:128, 0:512], mhalf.ap(b, 64, [[1, 128]]),
                             qsq_t[p].ap(b, 64, [[1, 512]]), start=False, stop=True))][-1],
              waits=w, incs=[(s_peB, 1)])
            st_fav[i] = s_peB.n

        def emit_exp(i):
            par = i % 2
            w = [(s_peB, st_fav[i])]
            if i - 2 in st_nd:
                w.append((s_peB, st_nd[i - 2]))
            for ec in range(2):
                E("act", lambda e, ec=ec, p=par: e.activation(
                    qp_t[p][0:128, ec * 512:(ec + 1) * 512], PS[2 * p + ec][0:128, 0:512],
                    AF.Exp, bias=bln16[0:128, 0:1]),
                  waits=w if ec == 0 else [], incs=[(s_acB, 1)])
            st_exp[i] = s_acB.n

        def emit_nd(i):
            par = i % 2
            h, tt = iters[i]
            w = [(s_acB, st_exp[i]), (s_dvB, kvready)]
            if i - 2 in st_evac:
                w.append((s_acB, st_evac[i - 2]))
            else:
                w.append(psw(4 + par))
            E("pe", lambda e, hh=h, p=par: [
                e.matmul(PS[4 + p][0:65, 0:512], kvks[0:128, (2 * hh) * 65:(2 * hh + 1) * 65],
                         qp_t[p][0:128, 0:512], start=True, stop=False),
                e.matmul(PS[4 + p][0:65, 0:512], kvks[0:128, (2 * hh + 1) * 65:(2 * hh + 2) * 65],
                         qp_t[p][0:128, 512:1024], start=False, stop=True)][-1],
              waits=w, incs=[(s_peB, 1)])
            st_nd[i] = s_peB.n

        def emit_evac(i):
            p3 = i % 3
            par = i % 2
            w = [(s_peB, st_nd[i])]
            if i - 3 in st_mult:
                w.append((s_dvB, st_mult[i - 3]))
            if i - 3 in st_recip:
                w.append((s_dvB, st_recip[i - 3]))
            E("act", lambda e, p=p3, pp=par: e.activation(
                nd_t[p][0:65, 0:512], PS[4 + pp][0:65, 0:512], AF.Identity,
                bias=eps65[0:65, 0:1]), waits=w, incs=[(s_acB, 1)])
            st_evac[i] = s_acB.n

        def emit_recip(i):
            p3 = i % 3
            w = [(s_acB, st_evac[i])]
            def _recip(e, p=p3):
                with nc.allow_low_precision(reason="bf16 denom reciprocal (operand-level rounding)"):
                    return e.reciprocal(nd_t[p].ap(64, 1, [[1, 512]]),
                                        nd_t[p].ap(64, 1, [[1, 512]]))
            E("dve", _recip, waits=w, incs=[(s_dvB, 1)])
            st_recip[i] = s_dvB.n

        def emit_rbc(i):
            p3 = i % 3
            par = i % 2
            w = [(s_dvB, st_recip[i])]
            if i - 2 in st_mult:
                w.append((s_dvB, st_mult[i - 2]))
            else:
                w.append(psw(6 + par))
            E("pe", lambda e, p=p3, pp=par: e.matmul(
                PS[6 + pp][0:64, 0:512], ones1.ap(64, 1, [[1, 64]]),
                nd_t[p].ap(64, 1, [[1, 512]]), start=True, stop=True),
              waits=w, incs=[(s_peB, 1)])
            st_rbc[i] = s_peB.n

        def emit_mult(i):
            p3 = i % 3
            par = i % 2
            h, tt = iters[i]
            bb = (h % 2) * 64
            dc = h // 2
            E("dve", lambda e, p=p3, pp=par, b=bb, d=dc, t=tt: e.tensor_tensor(
                attnpre.ap(b, 64, [[1, 512]], off=d * 2048 + t * 512),
                nd_t[p].ap(0, 64, [[1, 512]]), PS[6 + pp][0:64, 0:512], ALU.mult),
              waits=[(s_peB, st_rbc[i])], incs=[(s_dvB, 1)])
            st_mult[i] = s_dvB.n
            h2, tt2 = iters[i]
            if h2 == NH - 1:
                attnpre_done[tt2] = s_dvB.n

        emit_qsq(0)
        for i in range(NIT):
            if i + 1 < NIT:
                emit_qsq(i + 1)
            emit_fav(i)
            emit_exp(i)
            if i - 3 >= 0:
                emit_rbc(i - 3)
                emit_mult(i - 3)
            if i - 1 >= 0:
                emit_nd(i - 1)
                emit_evac(i - 1)
                emit_recip(i - 1)
        emit_nd(NIT - 1)
        emit_evac(NIT - 1)
        emit_recip(NIT - 1)
        for i_ in (NIT - 3, NIT - 2, NIT - 1):
            emit_rbc(i_)
            emit_mult(i_)
        for b_ in range(6):
            G.m[("ps", b_)] = (s_acB, s_acB.n)
        for b_ in (6, 7):
            G.m[("ps", b_)] = (s_dvB, s_dvB.n)
        G.s(("xb2", 0), s_peB)
        G.s(("xb2", 1), s_peB)
        for tc in range(NTC):
            par = tc % 2
            if tc >= 2:
                emit_ln2_tc(seq, tc - 2, dvo_done[tc - 2])
            E("sp", lambda e, t=tc, pq=par: e.dma_start(
                xb2[pq][0:128, 0:1024],
                bass.AP(X, (seq * S + t * 128) * D, [[D, 128], [1, D]])),
              waits=[G.w(("xb2", par)), G.w("cwsv") if tc < 2 else None],
              incs=[(s_x2[par], 16)], dma=True)
            nx = s_x2[par].n
            bks = (0, 1) if tc % 2 == 0 else (2, 3)
            E("pe", lambda e, t=tc, pp=(PS[bks[0]], PS[bks[1]]): [
                [e.matmul(pp[half][0:128, 0:512],
                          attnpre[0:128, dc2 * 2048 + t * 128: dc2 * 2048 + (t + 1) * 128],
                          outw_t.ap(0, 128, [[1, 512]], off=dc2 * 1024 + half * 512),
                          start=(dc2 == 0), stop=False)
                 for half in range(2) for dc2 in range(8)],
                e.matmul(pp[0][0:128, 0:512], ones1[0:1, 0:128], outbr[0:1, 0:512],
                         start=False, stop=True),
                e.matmul(pp[1][0:128, 0:512], ones1[0:1, 0:128], outbr[0:1, 512:1024],
                         start=False, stop=True)][-1],
              waits=[(s_dvB, attnpre_done[tc // 4]), (s_ow, ow_n), psw(bks[0]), psw(bks[1])],
              incs=[(s_peO, 1)])
            po = s_peO.n
            E("dve", lambda e, t=tc, pq=par, pp=(PS[bks[0]], PS[bks[1]]): [
                e.tensor_tensor(x2ap(t, 0, 512),
                                xb2[pq][0:128, 0:512], pp[0][0:128, 0:512], ALU.add),
                e.tensor_tensor(x2ap(t, 512, 1024),
                                xb2[pq][0:128, 512:1024], pp[1][0:128, 0:512], ALU.add)][-1],
              waits=[(s_peO, po), (s_x2[par], nx)], incs=[(s_dvO, 1)])
            pss(bks[0], s_dvO)
            pss(bks[1], s_dvO)
            G.s(("xb2", par), s_dvO)
            dvo_done[tc] = s_dvO.n
        emit_ln2_tc(seq, NTC - 2, dvo_done[NTC - 2])
        emit_ln2_tc(seq, NTC - 1, dvo_done[NTC - 1])
        E("pool", lambda e: e.nop(),
          waits=[(s_acc[t2], s_acc[t2].n) for t2 in range(NTC)], incs=[(s_fl, 1)])

    def emit_ln2_tc(seq, tc, dvo_wait):
        em.pending["dve"].append((s_dvO, dvo_wait))
        emit_ln_tc(lambda a, b, t=tc: x2ap(t, a, b), xn2, 2048, 0, tc, s_ln2A,
                   xsq=xsq2, xnt=xn_tm2)
        dvl = s_lnD.n
        E("pool", lambda e, t=tc: e.dma_start(
            bass.AP(OUT, (seq * S + t * 128) * D, [[D, 128], [1, D]]),
            x2ap(t, 0, 1024)),
          waits=[(s_lnD, dvl)], incs=[(s_acc[tc], 16)], dma=True)

    def emit_router():
        for tt in range(4):
            bk = tt % 2
            E("pe", lambda e, t=tt, ps=PS[bk]: [
                e.matmul(ps[0:8, 0:512], rwt_b[0:128, dc * 8:(dc + 1) * 8],
                         xn2.ap(0, 128, [[1, 512]], off=dc * 2048 + t * 512),
                         start=(dc == 0), stop=(dc == 7)) for dc in range(8)][-1],
              waits=[(s_ln2A, s_ln2A.n), psw(bk)], incs=[(s_lg, 1)])
            pl = s_lg.n
            E("act", lambda e, t=tt, ps=PS[bk]: e.activation(
                logits.ap(0, 8, [[1, 512]], off=t * 512), ps[0:8, 0:512], AF.Identity,
                bias=expb[0:8, 0:1]),
              waits=[(s_lg, pl)], incs=[(s_lga, 1)])
            pss(bk, s_lga)
        lgall = s_lga.n
        E("pe", lambda e: [
            e.transpose(PSB[2][0:128, t * 8:(t + 1) * 8],
                        logits.ap(0, 8, [[1, 128]], off=t * 128), ident[0:8, 0:8])
            for t in range(16)][-1],
          waits=[(s_lga, lgall), psw(2)], incs=[(s_lg, 1)])
        pl = s_lg.n
        E("act", lambda e: e.activation(rB1[0:128, 0:128], PSB[2][0:128, 0:128],
                                        AF.Identity),
          waits=[(s_lg, pl)], incs=[(s_cwA, 1)])
        pss(2, s_cwA)
        v8 = lambda t_: t_.ap(0, 128, [[8, 16], [1, 8]])
        bc = lambda off: rsc.ap(0, 128, [[1, 16], [0, 8]], off=off)
        E("dve", lambda e: e.tensor_reduce(rsc[0:128, 0:16], v8(rB1), AX.X, ALU.max),
          waits=[(s_cwA, s_cwA.n)], incs=[(s_cwD, 1)])
        E("dve", lambda e: e.tensor_tensor(rB2[0:128, 0:128], rB1[0:128, 0:128],
                                           bc(0), ALU.subtract),
          waits=[(s_cwD, s_cwD.n)], incs=[(s_cwD, 1)])
        E("act", lambda e: e.activation(rB2[0:128, 0:128], rB2[0:128, 0:128], AF.Exp),
          waits=[(s_cwD, s_cwD.n)], incs=[(s_cwA, 1)])
        E("dve", lambda e: e.tensor_single_scalar(rB3[0:128, 0:128], rB2[0:128, 0:128],
                                                  1.0, ALU.is_ge),
          waits=[(s_cwA, s_cwA.n)], incs=[(s_cwD, 1)])
        E("dve", lambda e: e.tensor_tensor(rB1[0:128, 0:128], rB2[0:128, 0:128],
                                           rB3[0:128, 0:128], ALU.subtract),
          waits=[(s_cwD, s_cwD.n)], incs=[(s_cwD, 1)])
        E("dve", lambda e: e.tensor_reduce(rsc[0:128, 16:32], v8(rB1), AX.X, ALU.max),
          waits=[(s_cwD, s_cwD.n)], incs=[(s_cwD, 1)])
        E("dve", lambda e: e.tensor_tensor(rB1[0:128, 0:128], rB2[0:128, 0:128],
                                           bc(16), ALU.is_ge),
          waits=[(s_cwD, s_cwD.n)], incs=[(s_cwD, 1)])
        E("dve", lambda e: e.tensor_tensor(rB3[0:128, 0:128], rB2[0:128, 0:128],
                                           rB1[0:128, 0:128], ALU.mult),
          waits=[(s_cwD, s_cwD.n)], incs=[(s_cwD, 1)])
        E("dve", lambda e: e.tensor_single_scalar(rsc[0:128, 32:48], rsc[0:128, 16:32],
                                                  1.0, ALU.add),
          waits=[(s_cwD, s_cwD.n)], incs=[(s_cwD, 1)])
        E("dve", lambda e: e.reciprocal(rsc[0:128, 0:16], rsc[0:128, 32:48]),
          waits=[(s_cwD, s_cwD.n)], incs=[(s_cwD, 1)])
        E("dve", lambda e: e.tensor_tensor(cw_tm[0:128, 0:128], rB3[0:128, 0:128],
                                           bc(0), ALU.mult),
          waits=[(s_cwD, s_cwD.n)], incs=[(s_cwD, 1)])
        dcw = s_cwD.n
        for tc in range(NTC):
            bk2 = 4 + tc % 2
            E("pe", lambda e, t=tc, ps=PSB[bk2]: e.transpose(
                ps[0:8, 0:128], cw_tm.ap(0, 128, [[1, 8]], off=t * 8), ident[0:128, 0:128]),
              waits=[(s_cwD, dcw) if tc < 2 else None, psw(bk2)], incs=[(s_lg, 1)])
            pt2 = s_lg.n
            E("act", lambda e, t=tc, ps=PSB[bk2]: e.activation(
                cw_fm.ap(0, 8, [[1, 128]], off=t * 128), ps[0:8, 0:128], AF.Identity),
              waits=[(s_lg, pt2), G.w("cwsv") if tc == 0 else None], incs=[(s_cwA, 1)])
            pss(bk2, s_cwA)

    def emit_experts(passes, fl_all):
        experts = [(SW1, SW3, SW2, 0, False)] + [(W1, W3, W2, e, True) for e in range(8)]
        if _DBG_NE is not None:
            experts = experts[:_DBG_NE]
        sems_w = {}
        npass = len(passes)
        nfl = fl_all // 1

        def emit_w13_chunk(ei, ch, wait_w1, wait_w3):
            TW1, TW3, _, eidx, _ = experts[ei]
            E("pool", lambda e, tw=TW1, ex=eidx, c=ch: e.dma_start(
                w1b.ap(0, 128, [[2048, 8], [1, 1024]], off=c * 1024),
                bass.AP(tw, ex * 1024 * 2048 + c * 1024,
                        [[2048, 128], [128 * 2048, 8], [1, 1024]])),
              waits=[wait_w1, (s_fl, fl_all)], incs=[(s_w1[ch], 16)], dma=True)
            E("pool", lambda e, tw=TW3, ex=eidx, c=ch: e.dma_start(
                w3b.ap(0, 128, [[2048, 8], [1, 1024]], off=c * 1024),
                bass.AP(tw, ex * 1024 * 2048 + c * 1024,
                        [[2048, 128], [128 * 2048, 8], [1, 1024]])),
              waits=[wait_w3, (s_fl, fl_all)], incs=[(s_w3[ch], 16)], dma=True)
            a, b, c2, d2 = sems_w.setdefault(ei, ([0, 0], [0, 0], 0, 0))
            a[ch] = s_w1[ch].n
            b[ch] = s_w3[ch].n

        def emit_w2(ei, wait_w2):
            _, _, TW2, eidx, routed = experts[ei]
            E("pool", lambda e, t2w=TW2, ex=eidx: e.dma_start(
                w2b.ap(0, 128, [[1024, 16], [1, 1024]]),
                bass.AP(t2w, ex * 2048 * 1024, [[1024, 128], [128 * 1024, 16], [1, 1024]])),
              waits=[wait_w2, (s_ln2A, s_ln2A.n), (s_cwA, s_cwA.n), (s_fl, fl_all)],
              incs=[(s_w2, 16)], dma=True)
            a, b, c2, d2 = sems_w.setdefault(ei, ([0, 0], [0, 0], 0, 0))
            sems_w[ei] = (a, b, s_w2.n, d2)

        cwr_n = {}

        def emit_cwrow(ei, P):
            _, _, _, eidx, routed = experts[ei]
            if routed:
                cpar = P["pi"]
                E("sp", lambda e, ex=eidx, cp=cpar, src=P["cw"]: e.dma_start(
                    cwrow.ap(0, 1, [[1, 2048]], off=cp * 2048),
                    src.ap(ex, 1, [[1, 2048]])),
                  waits=[P["cw_wait"], G.w(("cwrowW", cpar))],
                  incs=[(s_cwr, 16)], dma=True)
            cwr_n[(ei, P["pi"])] = s_cwr.n

        def emit_cwb(ei, P):
            _, _, _, eidx, routed = experts[ei]
            if not routed:
                return
            cpar = P["pi"]
            ncr = cwr_n[(ei, cpar)]
            for t2 in range(4):
                E("pe", lambda e, t=t2, cp=cpar: e.matmul(
                    PS[t % 2][0:128, 0:512], ones1[0:1, 0:128],
                    cwrow.ap(0, 1, [[1, 512]], off=cp * 2048 + t * 512),
                    start=True, stop=True),
                  waits=[(s_cwr, ncr), psw(t2 % 2)], incs=[(s_peG, 1)])
                if t2 == 3:
                    G.s(("cwrowW", cpar), s_peG)
                pgc = s_peG.n
                E("act", lambda e, t=t2, cp=cpar: e.activation(
                    cwb.ap(0, 128, [[1, 512]], off=cp * 2048 + t * 512),
                    PS[t % 2][0:128, 0:512], AF.Identity),
                  waits=[(s_peG, pgc), G.w(("cwbW", cpar)) if t2 == 0 else None],
                  incs=[(s_acF, 1)])
                pss(t2 % 2, s_acF)

        emit_w13_chunk(0, 0, G.w(("w1c", 0)), G.w(("w3c", 0)))
        emit_w13_chunk(0, 1, G.w(("w1c", 1)), G.w(("w3c", 1)))
        emit_w2(0, G.w("w2"))
        for P in passes:
            emit_cwrow(0, P)
        for ei, (TW1, TW3, TW2, eidx, routed) in enumerate(experts):
            nw1c, nw3c, nw2, _ = sems_w[ei]
            for pi_, P in enumerate(passes):
                last = pi_ == npass - 1
                cpar = P["pi"]
                peG0, peU0, peY0 = s_peG.n, s_peU.n, s_peY.n
                emit_cwb(ei, P)
                if ei + 1 < len(experts):
                    emit_cwrow(ei + 1, P)
                cwb_ready = s_acF.n
                for tt in range(4 if _DBG_NTT is None else _DBG_NTT):
                    for hc in range(16):
                        if last and tt == 3 and hc == 8 and ei + 1 < len(experts):
                            nc4 = 4 if routed else 0
                            emit_w13_chunk(ei + 1, 0, (s_peG, peG0 + nc4 + 56),
                                           (s_peU, peU0 + 56))
                        gbk = hc % 2
                        ubk = 2 + hc % 2
                        xw = P["xw"] if (tt == 0 and hc == 0) else []
                        E("pe", lambda e, hcv=hc, t=tt, ps=PS[gbk], xnt=P["xnt"]: [
                            e.matmul(ps[0:128, 0:512],
                                     w1b.ap(0, 128, [[1, 128]], off=dc * 2048 + hcv * 128),
                                     xnt.ap(0, 128, [[1, 512]], off=dc * 2048 + t * 512),
                                     start=(dc == 0), stop=(dc == 7)) for dc in range(8)][-1],
                          waits=[(s_w1[hc // 8], nw1c[hc // 8]), psw(gbk)] + xw,
                          incs=[(s_peG, 1)])
                        pg = s_peG.n
                        if last and tt == 3 and hc == 7:
                            G.s(("w1c", 0), s_peG)
                        if last and tt == 3 and hc == 15:
                            G.s(("w1c", 1), s_peG)
                        E("pe", lambda e, hcv=hc, t=tt, ps=PS[ubk], xnt=P["xnt"]: [
                            e.matmul(ps[0:128, 0:512],
                                     w3b.ap(0, 128, [[1, 128]], off=dc * 2048 + hcv * 128),
                                     xnt.ap(0, 128, [[1, 512]], off=dc * 2048 + t * 512),
                                     start=(dc == 0), stop=(dc == 7)) for dc in range(8)][-1],
                          waits=[(s_w3[hc // 8], nw3c[hc // 8]), psw(ubk)] + xw,
                          incs=[(s_peU, 1)])
                        pu = s_peU.n
                        if last and tt == 3 and hc == 7:
                            G.s(("w3c", 0), s_peU)
                        if last and tt == 3 and hc == 15:
                            G.s(("w3c", 1), s_peU)
                        E("act", lambda e, hcv=hc, ps=PS[gbk]: e.activation(
                            gs_t[hcv % 2][0:128, 0:512], ps[0:128, 0:512], AF.Silu),
                          waits=[(s_peG, pg), G.w(("gsW", hc % 2))], incs=[(s_acF, 1)])
                        pss(gbk, s_acF)
                        if routed:
                            E("dve", lambda e, hcv=hc, ps=PS[ubk], t=tt, cp=cpar: e.scalar_tensor_tensor(
                                us_t[hcv % 2][0:128, 0:512], ps[0:128, 0:512], 1.0,
                                cwb.ap(0, 128, [[1, 512]], off=cp * 2048 + t * 512),
                                ALU.mult, ALU.mult),
                              waits=[(s_peU, pu), (s_acF, cwb_ready), G.w(("usW", hc % 2))],
                              incs=[(s_dvF, 1)])
                        else:
                            E("dve", lambda e, hcv=hc, ps=PS[ubk]: e.tensor_copy(
                                us_t[hcv % 2][0:128, 0:512], ps[0:128, 0:512]),
                              waits=[(s_peU, pu), G.w(("usW", hc % 2))], incs=[(s_dvF, 1)])
                        pss(ubk, s_dvF)
                        af, df = s_acF.n, s_dvF.n
                        E("pool", lambda e, hcv=hc: e.tensor_tensor(
                            eo_all[0:128, hcv * 512:(hcv + 1) * 512],
                            gs_t[hcv % 2][0:128, 0:512], us_t[hcv % 2][0:128, 0:512], ALU.mult),
                          waits=[(s_acF, af), (s_dvF, df), G.w(("eoW", hc))], incs=[(s_poF, 1)])
                        G.s(("gsW", hc % 2), s_poF)
                        G.s(("usW", hc % 2), s_poF)
                        if hc == 7:
                            po7 = s_poF.n
                    if last and tt == 3 and ei + 1 < len(experts):
                        nc4 = 4 if routed else 0
                        emit_w13_chunk(ei + 1, 1, (s_peG, peG0 + nc4 + 64),
                                       (s_peU, peU0 + 64))
                    po = s_poF.n
                    for i in range(4):
                        tc = tt * 4 + i
                        ybk = (4, 5) if i % 2 == 0 else (6, 7)
                        E("pe", lambda e, t=tc, ii=i, pp=(PS[ybk[0]], PS[ybk[1]]): [
                            e.matmul(pp[half][0:128, 0:512],
                                     eo_all[0:128, hc2 * 512 + ii * 128: hc2 * 512 + (ii + 1) * 128],
                                     w2b.ap(0, 128, [[1, 512]], off=hc2 * 1024 + half * 512),
                                     start=(hc2 == 0), stop=False)
                            for half in range(2) for hc2 in range(8)][-1],
                          waits=[(s_poF, po7), (s_w2, nw2), psw(ybk[0]), psw(ybk[1])],
                          incs=[(s_peY, 1)])
                        E("pe", lambda e, t=tc, ii=i, pp=(PS[ybk[0]], PS[ybk[1]]): [
                            e.matmul(pp[half][0:128, 0:512],
                                     eo_all[0:128, hc2 * 512 + ii * 128: hc2 * 512 + (ii + 1) * 128],
                                     w2b.ap(0, 128, [[1, 512]], off=hc2 * 1024 + half * 512),
                                     start=False, stop=(hc2 == 15))
                            for half in range(2) for hc2 in range(8, 16)][-1],
                          waits=[(s_poF, po)], incs=[(s_peY, 1)])
                        py = s_peY.n
                        E("act", lambda e, pp=(PS[ybk[0]], PS[ybk[1]]), px=px_t[i % 2]: [
                            e.activation(px[0:128, 0:512], pp[0][0:128, 0:512], AF.Identity),
                            e.activation(px[0:128, 512:1024], pp[1][0:128, 0:512],
                                         AF.Identity)][-1],
                          waits=[(s_peY, py), G.w(("pxW", i % 2))], incs=[(s_acY, 1)])
                        pss(ybk[0], s_acY)
                        pss(ybk[1], s_acY)
                        ay = s_acY.n
                        accv = 16 * (nfl + npass * ei + pi_)
                        E("pool", lambda e, t=tc, px=px_t[i % 2], sq=P["sq"]: e.dma_start(
                            bass.AP(OUT, (sq * S + t * 128) * D, [[D, 128], [1, D]]),
                            px[0:128, 0:1024], accum_op=ALU.add),
                          waits=[(s_acY, ay), (s_acc[tc], accv)], incs=[(s_acc[tc], 16)], dma=True)
                        G.m[("pxW", i % 2)] = (s_acc[tc], s_acc[tc].n)
                    for hc in range(16):
                        G.m[("eoW", hc)] = (s_peY, s_peY.n)
                if routed:
                    G.s(("cwbW", cpar), s_dvF)
                ncwb = 4 if routed else 0
                assert s_peG.n == peG0 + ncwb + 64, (s_peG.n, peG0, ncwb)
                assert s_peU.n == peU0 + 64
                assert s_peY.n == peY0 + 32
            G.s("w2", s_peY)
            if ei + 1 < len(experts):
                emit_w2(ei + 1, (s_peY, peY0 + 32))
        E("pool", lambda e: e.nop(),
          waits=[(s_acc[t2], s_acc[t2].n) for t2 in range(NTC)], incs=[(s_fl, 1)])
        done = (s_fl, s_fl.n)
        for key in ("awx", "rw", "pw", ("xntW", id(xnt_b)),
                    ("kkW", 0), ("kkW", 1), ("qrW", 0), ("qrW", 1),
                    ("xtil", 0), ("xtil", 1)):
            G.m[key] = done
        G.m["ck"] = (s_peY, s_peY.n)

    emit_setup()
    if mode == "ffn1":
        s_dbg = em.sem("dbg")
        s_dbgp = em.sem("dbgp")
        if _DBG_PRE >= 1:
            E("sp", lambda e: e.dma_start(xn2[0:128, 0:16384], dbg["xn2_in"].ap()),
              incs=[(s_dbg, 16)], dma=True)
            E("pool", lambda e: e.dma_start(cw_fm[0:8, 0:2048], dbg["cw_in"].ap()),
              incs=[(s_dbgp, 16)], dma=True)
        for tc in range(NTC if _DBG_PRE >= 2 else 0):
            E("sp", lambda e, t=tc: e.dma_start(
                px_t[0][0:128, 0:1024], dbg["x2_in"].ap()[0:128, t * 1024:(t + 1) * 1024]),
              waits=[(s_acc[tc - 1], s_acc[tc - 1].n)] if tc > 0 else [],
              incs=[(s_dbg, 16)], dma=True)
            nd = s_dbg.n
            E("pool", lambda e, t=tc: e.dma_start(
                bass.AP(OUT, (t * 128) * D, [[D, 128], [1, D]]), px_t[0][0:128, 0:1024]),
              waits=[(s_dbg, nd)], incs=[(s_acc[tc], 16)], dma=True)
        if _DBG_PRE >= 2:
            E("pool", lambda e: e.nop(),
              waits=[(s_acc[t2], s_acc[t2].n) for t2 in range(NTC)], incs=[(s_fl, 1)])
        E("sp", lambda e: e.nop(), waits=[(s_fl, s_fl.n), (s_dbgp, s_dbgp.n)])
        em.barrier(s_bar)
        emit_experts([dict(pi=0, sq=0, xnt=xn2, cw=cw_fm,
                           cw_wait=(s_cwA, s_cwA.n), xw=[])], s_fl.n)
        em.barrier(s_bar)
        E("sp", lambda e: e.nop(), waits=[(s_acc[i], s_acc[i].n) for i in range(NTC)])
    elif mode == "attn1":
        emit_attn(0)
        dvo = s_dvO.n
        for tc in range(NTC):
            E("sp", lambda e, t=tc: e.dma_start(
                bass.AP(dbg["x2_out"], t * 1024, [[16384, 128], [1, 1024]]),
                x2ap(t, 0, 1024)),
              waits=[(s_dvO, dvo)], incs=[(s_fl, 16)], dma=True)
        allw = [(s, s.n) for s in em.sems if s.n > 0]
        for sc, v in allw:
            em.pending["sp"].append((sc, v))
        E("sp", lambda e: e.dma_start(dbg["arena_out"].ap(), a32h[:, :]),
          incs=[(s_fl, 16)], dma=True)
        E("sp", lambda e: e.nop(), waits=[(s_fl, s_fl.n)])
    else:
        for seq in range(BPC):
            emit_attn(seq)
            emit_router()
            emit_experts([dict(pi=0, sq=seq, xnt=xn2, cw=cw_fm,
                               cw_wait=(s_cwA, s_cwA.n), xw=[])], s_fl.n)
            em.barrier(s_bar)
        E("sp", lambda e: e.nop(), waits=[(s_acc[i], s_acc[i].n) for i in range(NTC)])

    em.finalize(es)
    es.close()
    return nc


def _rope_tables():
    inv = 1.0 / 10000.0 ** (np.arange(0, HD, 2, dtype=np.float32) / HD)
    f = np.arange(S, dtype=np.float32)[:, None] * inv[None, :]
    sn = (np.sin(f) * CSC).astype(np.float32)
    cs = (np.cos(f) * CSC).astype(np.float32)
    sstab = np.zeros((128, NTC * 64), np.float32)
    cctab = np.zeros((128, NTC * 64), np.float32)
    for tc in range(NTC):
        t0 = tc * 128
        sstab[:, tc * 64:tc * 64 + 32] = sn[t0:t0 + 128, :]
        sstab[:, tc * 64 + 32:tc * 64 + 64] = -sn[t0:t0 + 128, :]
        cctab[:, tc * 64:tc * 64 + 32] = cs[t0:t0 + 128, :]
        cctab[:, tc * 64 + 32:tc * 64 + 64] = cs[t0:t0 + 128, :]
    return (np.ascontiguousarray(sstab.astype(ml_dtypes.bfloat16)),
            np.ascontiguousarray(cctab.astype(ml_dtypes.bfloat16)))


def host_inputs(inp):
    f32c = lambda a: np.ascontiguousarray(np.asarray(a, dtype=np.float32))
    cb = f32c(np.asarray(inp["L0_attn_conv_b"]).reshape(3, 8, 128).transpose(2, 0, 1)
              .reshape(128, 24))
    rb = f32c(np.asarray(inp["L0_attn_rff_b"]).reshape(3, 3, 2, 128).transpose(3, 0, 1, 2)
              .reshape(128, 18))
    pb = f32c(np.asarray(inp["L0_attn_proj_b"]).reshape(1, 3072))
    ob = f32c(np.asarray(inp["L0_attn_out_b"]).reshape(1, 1024))
    lmT = f32c(np.asarray(inp["L0_attn_landmarks"]).T)
    rwT = f32c(np.asarray(inp["L0_ffn_router_w"]).T.reshape(8, 128, 8).transpose(1, 0, 2)
               .reshape(128, 64))
    eb = f32c(np.asarray(inp["L0_ffn_experts_bias"]).reshape(8, 1))
    sstab, cctab = _rope_tables()
    return {
        "conv_k": f32c(inp["L0_attn_conv_k"]), "conv_b": cb,
        "rff_w": f32c(inp["L0_attn_rff_w"]), "rff_b": rb,
        "proj_w": f32c(inp["L0_attn_proj_w"]), "proj_b": pb,
        "omega": f32c(inp["L0_attn_omega"]), "lmT": lmT,
        "out_w": f32c(inp["L0_attn_out_w"]), "out_b": ob,
        "rwT": rwT, "expb": eb,
        "w1": f32c(inp["L0_ffn_w1"]), "w3": f32c(inp["L0_ffn_w3"]),
        "w2": f32c(inp["L0_ffn_w2"]),
        "sw1": f32c(inp["L0_ffn_sw1"]), "sw3": f32c(inp["L0_ffn_sw3"]),
        "sw2": f32c(inp["L0_ffn_sw2"]),
        "sstab": sstab, "cctab": cctab,
    }


def kernel(**inputs):
    base = host_inputs(inputs)
    x = np.asarray(inputs["L0_attn_x"], dtype=np.float32)
    nc = build_program("full")
    in_maps = []
    for c in range(NCORES):
        m = dict(base)
        m["x"] = np.ascontiguousarray(x[c * BPC:(c + 1) * BPC])
        in_maps.append(m)
    res = run_bass_kernel_spmd(nc, in_maps, core_ids=list(range(NCORES)))
    return np.concatenate([r["out"] for r in res.results], axis=0)
